# version 09 (final)
# speedup vs baseline: 1.2726x; 1.0017x over previous
import numpy as np
from contextlib import ExitStack

import concourse.bass as bass
import concourse.mybir as mybir

F32 = mybir.dt.float32
BF16 = mybir.dt.bfloat16
AF = mybir.ActivationFunctionType
ALU = mybir.AluOpType
AX = mybir.AxisListType

LN_EPS = 1e-5


def cdiv(a, b):
    return (a + b - 1) // b


class Sem:

    def __init__(self, nc, ctx, name):
        self.h = ctx.enter_context(nc.semaphore(name))
        self.n = 0

    def bump(self, k=1):
        self.n += k
        return self.n


class Emit:

    def __init__(self, nc, ctx):
        self.nc = nc
        self.ctx = ctx
        self.q = {"g": [], "sy": [], "t": [], "v": [], "sc": []}
        self.sems = {}

    def sem(self, name):
        if name not in self.sems:
            self.sems[name] = Sem(self.nc, self.ctx, name)
        return self.sems[name]

    def __call__(self, eng, method, *args, waits=(), incs=(), **kw):
        def f(e, method=method, args=args, kw=kw, waits=tuple(waits), incs=tuple(incs)):
            for h, v in waits:
                if v > 0:
                    e.wait_ge(h, v)
            ins = getattr(e, method)(*args, **kw)
            for h, k in incs:
                ins.then_inc(h, k)
        self.q[eng].append(f)

    def raw(self, eng, fn):
        self.q[eng].append(fn)

    def wait(self, eng, h, v):
        if v > 0:
            self.raw(eng, lambda e, h=h, v=v: e.wait_ge(h, v))

    def finalize(self):
        block = self.ctx.enter_context(self.nc.Block())

        @block.gpsimd
        def _(g):
            for f in self.q["g"]:
                f(g)

        @block.sync
        def _(sy):
            for f in self.q["sy"]:
                f(sy)

        @block.tensor
        def _(t):
            for f in self.q["t"]:
                f(t)

        @block.vector
        def _(v):
            for f in self.q["v"]:
                f(v)

        @block.scalar
        def _(sc):
            for f in self.q["sc"]:
                f(sc)


class WStream:

    def __init__(self, E, name, free, ring=16, queue="sy"):
        self.E = E
        self.buf = E.ctx.enter_context(E.nc.sbuf_tensor(name, [128, ring * free], BF16))
        self.free = free
        self.ring = ring
        self.queues = [queue] if isinstance(queue, str) else list(queue)
        self.loaded = [E.sem(f"{name}_ld{j}") for j in range(ring)]
        self.i_load = 0
        self.last_cert = None

    def load(self, src_ap, transpose=True, waits=()):
        i = self.i_load
        self.i_load += 1
        slot = i % self.ring
        lv = self.loaded[slot].bump(16)
        if transpose:
            prows, wdt = src_ap.shape[-1], src_ap.shape[0]
        else:
            prows, wdt = src_ap.shape[0], src_ap.shape[-1]
        ap = self.buf[0:prows, slot * self.free:slot * self.free + wdt]
        self.E(self.queues[slot % len(self.queues)], "dma_start", out=ap,
               in_=src_ap, transpose=transpose,
               waits=list(waits), incs=[(self.loaded[slot].h, 16)])
        return ap, (self.loaded[slot].h, lv)


def make_identity(E, dtype=F32):
    ident = E.ctx.enter_context(E.nc.sbuf_tensor("ident", [128, 128], dtype))
    ini = E.sem("ident_ini")
    E("g", "memset", ident[:, :], 1.0, incs=[(ini.h, 1)])
    ini.bump(1)
    E("g", "affine_select", ident[:, :], ident[:, :], pattern=[[-1, 128]],
      compare_op=ALU.is_equal, fill=0.0, base=0, channel_multiplier=1,
      waits=[(ini.h, 1)], incs=[(ini.h, 1)])
    ini.bump(1)
    return ident, ini


def mm_stage(E, *, ws, nw, wsrc_fn, n_ogg, OGG, kc, rhs_fn, width,
             ps_pool, bank_base, mm_sem, ev_sem, evac_fn, pre_waits=(),
             first_load_waits=(), ev_per_ogg=None, kprog=None):
    m2 = OGG // 128
    nlive = nw * m2
    assert bank_base + 2 * nlive <= 8
    if ev_per_ogg is None:
        ev_per_ogg = nlive
    ev_base = ev_sem.n
    mm_base = mm_sem.n
    KG = 8
    ring = ws.ring
    ngroups = (kc - KG) // KG if kc > KG else 0
    if kc * nw > ring // 2:
        assert kprog is not None, "need kprog for ring reuse within an ogg"
    kp_base = kprog.n if kprog is not None else 0

    def cert(ogg_i, k_i):
        if ogg_i < 0:
            return None
        grp = k_i // KG + 1
        if grp <= ngroups:
            return (kprog.h, kp_base + ogg_i * ngroups + grp)
        return (mm_sem.h, mm_base + ogg_i + 1)

    for ogg in range(n_ogg):
        par = ogg & 1
        banks = [ps_pool[bank_base + par * nlive + j] for j in range(nlive)]
        tiles = []
        for k in range(kc):
            row = []
            for w in range(nw):
                gi = (ogg * kc + k) * nw + w
                lw = list(first_load_waits[w]) if ogg == 0 else []
                pi = gi - ring
                if pi >= 0:
                    kk = pi // nw
                    c = cert(kk // kc, kk % kc)
                    if c is not None:
                        lw.append(c)
                elif ws.last_cert is not None:
                    lw.append(ws.last_cert)
                row.append(ws.load(wsrc_fn(w, ogg, k), waits=lw))
            tiles.append(row)
        for k in range(kc):
            for w in range(nw):
                ap, ldw = tiles[k][w]
                for m in range(m2):
                    waits = []
                    if k == 0 and w == 0 and m == 0:
                        waits += list(pre_waits)
                        bank_wait = ev_base + ev_per_ogg * (ogg - 1)
                        if ogg >= 2 and bank_wait > 0:
                            waits.append((ev_sem.h, bank_wait))
                    if m == 0:
                        waits.append(ldw)
                    incs = []
                    if (kprog is not None and w == 0 and m == 0 and k >= KG
                            and k % KG == 0 and k // KG <= ngroups):
                        kprog.bump(1)
                        incs.append((kprog.h, 1))
                    if k == kc - 1 and w == nw - 1 and m == m2 - 1:
                        mm_sem.bump(1)
                        incs.append((mm_sem.h, 1))
                    E("t", "matmul", banks[w * m2 + m][:, 0:width],
                      ap[:, m * 128:(m + 1) * 128], rhs_fn(k),
                      start=(k == 0), stop=(k == kc - 1), waits=waits, incs=incs)
        evac_fn(ogg, m2, banks, mm_sem.n)
    ws.last_cert = (mm_sem.h, mm_sem.n)


def ln_fm(E, name, seg_ap, segs, width, ps_pool, bank_ids, scr, D):
    zb, sq = scr["zb"], scr["sq"]
    ones_col, ones_row, row = scr["ones_col"], scr["ones_row"], scr["row"]
    ini_sem, ini_val = scr["ini"]
    guard = list(scr.get("guard", ()))
    lnz, lnsq = E.sem(name + "_z"), E.sem(name + "_sq")
    lnst, lnv1 = E.sem(name + "_st"), E.sem(name + "_v1")
    lnsd, lnv2 = E.sem(name + "_v2a"), E.sem(name + "_v2")
    lnbc, lnfin = E.sem(name + "_bc"), E.sem(name + "_fin")
    bsum, bsq, bA, bB = [ps_pool[b] for b in bank_ids]
    prev_st = lnst.n

    for s in range(segs):
        w = guard if s == 0 else []
        if s == 0 and prev_st > 0:
            w = w + [(lnst.h, prev_st)]
        E("g", "tensor_copy", zb[:, s * width:(s + 1) * width], seg_ap(s),
          waits=w, incs=[(lnz.h, 1)])
        lnz.bump(1)
        E("sc", "activation", sq[:, s * width:(s + 1) * width],
          zb[:, s * width:(s + 1) * width], AF.Square,
          waits=[(lnz.h, lnz.n)], incs=[(lnsq.h, 1)])
        lnsq.bump(1)
    for s in range(segs):
        E("t", "matmul", bsum[0:1, 0:width], ones_col[:, :],
          zb[:, s * width:(s + 1) * width], start=(s == 0), stop=(s == segs - 1),
          waits=([(ini_sem.h, ini_val), (lnz.h, lnz.n)] if s == 0 else []))
        E("t", "matmul", bsq[0:1, 0:width], ones_col[:, :],
          sq[:, s * width:(s + 1) * width], start=(s == 0), stop=(s == segs - 1),
          waits=([(lnsq.h, lnsq.n)] if s == 0 else []),
          incs=([(lnst.h, 1)] if s == segs - 1 else []))
    lnst.bump(1)
    mean = row[:, 0:width]
    ex2 = row[:, width:2 * width]
    var = row[:, 2 * width:3 * width]
    rstd = row[:, 3 * width:4 * width]
    vc = E.sem(name + "_vc")
    b = vc.n
    E("v", "tensor_scalar_mul", mean, bsum[0:1, 0:width], 1.0 / D,
      waits=[(lnst.h, lnst.n)], incs=[(vc.h, 1)])
    vc.bump(1)
    E("v", "tensor_scalar_mul", ex2, bsq[0:1, 0:width], 1.0 / D,
      waits=[(vc.h, b + 1)], incs=[(vc.h, 1)])
    vc.bump(1)
    E("v", "tensor_tensor", var, mean, mean, ALU.mult,
      waits=[(vc.h, b + 2)], incs=[(vc.h, 1)])
    vc.bump(1)
    E("v", "tensor_sub", var, ex2, var,
      waits=[(vc.h, b + 3)], incs=[(vc.h, 1)])
    vc.bump(1)
    E("sc", "activation", rstd, var, AF.Sqrt, bias=scr["eps"],
      waits=[(vc.h, b + 4), (ini_sem.h, ini_val)], incs=[(lnsd.h, 1)])
    lnsd.bump(1)
    E("v", "reciprocal", rstd, rstd, waits=[(lnsd.h, lnsd.n)], incs=[(vc.h, 1)])
    vc.bump(1)
    E("v", "tensor_tensor", mean, mean, rstd, ALU.mult,
      waits=[(vc.h, b + 5)], incs=[(vc.h, 1)])
    vc.bump(1)
    E("v", "tensor_scalar_mul", mean, mean, -1.0,
      waits=[(vc.h, b + 6)], incs=[(vc.h, 1)])
    vc.bump(1)
    E("t", "matmul", bA[:, 0:width], ones_row[:, :], rstd, start=True, stop=True,
      waits=[(vc.h, vc.n)])
    E("t", "matmul", bB[:, 0:width], ones_row[:, :], mean, start=True, stop=True,
      incs=[(lnbc.h, 1)])
    lnbc.bump(1)
    ac = E.sem(name + "_ac")
    for s in range(segs):
        E("v", "tensor_tensor", seg_ap(s), seg_ap(s), bA[:, 0:width], ALU.mult,
          waits=([(lnbc.h, lnbc.n)] if s == 0 else []), incs=[(ac.h, 1)])
        E("v", "tensor_tensor", seg_ap(s), seg_ap(s), bB[:, 0:width], ALU.add,
          waits=[(ac.h, ac.bump(1))], incs=[(lnfin.h, 1)])
        lnfin.bump(1)
    return lnfin, lnfin.n


def emit_pe(nc, ctx, *, D, HID, T, TC, x3, out,
            w1, w1s, w2, w2s, h1, h1s, h2, h2s, w3, w3s):
    assert TC <= 512 and T % TC == 0 and D % 256 == 0 and HID % 256 == 0
    KD, KH = D // 128, HID // 128
    NTC = T // TC
    OGG = 256

    E = Emit(nc, ctx)
    sbt = lambda name, shape, dt: ctx.enter_context(nc.sbuf_tensor("pe_" + name, shape, dt))
    ps_pool = [ctx.enter_context(nc.psum_tensor(f"pe_ps{i}", [128, 512], F32))
               for i in range(8)]
    dram = lambda name, shape: nc.dram_tensor("pe_" + name, shape, BF16).ap()

    w1b, w2b = dram("w1b", [HID, D]), dram("w2b", [HID, D])
    h1b, h2b = dram("h1b", [HID, HID]), dram("h2b", [HID, HID])
    w3b = dram("w3b", [D, HID])
    x3b = dram("x3b", [D, T])

    cast_mark = {}

    def cast(name, src, dst, rpc=512):
        cs = E.sem("cast_" + name)
        for i in range(cdiv(src.shape[0], rpc)):
            r0, r1 = i * rpc, min((i + 1) * rpc, src.shape[0])
            cs.bump(16)
            E("g", "dma_start", out=dst[r0:r1], in_=src[r0:r1],
              incs=[(cs.h, 16)])
        cast_mark[name] = (cs.h, cs.n)

    cast("x3", x3, x3b, rpc=1024)
    cast("w1", w1, w1b)
    cast("w2", w2, w2b)
    cast("w3", w3, w3b)
    cast("h1", h1, h1b)
    cast("h2", h2, h2b)

    ident, ident_ini = make_identity(E)
    sld, smm, sev = E.sem("sld"), E.sem("smm"), E.sem("sev")
    scale_sb = {}
    for nm, src, n in [("s1", w1s, HID), ("s2", w2s, HID), ("hs1", h1s, HID),
                       ("hs2", h2s, HID), ("s3", w3s, D)]:
        rows = n // 128
        raw = sbt("sraw_" + nm, [rows, 128], F32)
        scale_sb[nm] = sbt("scol_" + nm, [128, rows], F32)
        sld.bump(16)
        E("g", "dma_start", out=raw[:, :],
          in_=src.rearrange("(r p) -> r p", p=128), incs=[(sld.h, 16)])
        prev_ev = sev.n
        mmv = smm.bump(1)
        E("t", "transpose", ps_pool[0][:, 0:rows], raw[:, :], ident[0:rows, 0:rows],
          waits=[(sld.h, 16 * 5), (ident_ini.h, 2), (sev.h, prev_ev)],
          incs=[(smm.h, 1)])
        sev.bump(1)
        E("v", "tensor_copy", scale_sb[nm][:, :], ps_pool[0][:, 0:rows],
          waits=[(smm.h, mmv)], incs=[(sev.h, 1)])
    scales_done = sev.n

    zbf = sbt("zbf", [128, KD * T], BF16)
    hsl = sbt("h", [128, KH * TC], BF16)
    gsl = sbt("g1", [128, KH * TC], BF16)
    z3 = sbt("z3", [128, KD * TC], F32)
    x3r = sbt("x3r", [128, 8 * TC], F32)
    tmp = sbt("tmp", [128, 4 * 512], F32)
    scr = dict(
        zb=sbt("lnzb", [128, KD * 512], BF16),
        sq=sbt("lnsq", [128, KD * 512], BF16),
        ones_col=sbt("ones_col", [128, 1], BF16),
        ones_row=sbt("ones_row", [1, 128], F32),
        row=sbt("lnrow", [1, 4 * 512], F32),
        epsc=sbt("epsc", [1, 1], F32),
    )
    dum = sbt("dum", [1, 1], F32)
    ini = E.sem("ini")
    E("g", "memset", scr["ones_col"][:, :], 1.0)
    E("g", "memset", scr["epsc"][:, :], LN_EPS)
    E("g", "memset", scr["ones_row"][:, :], 1.0, incs=[(ini.h, 1)])
    ini.bump(1)
    scr["ini"] = (ini, 1)
    scr["eps"] = scr["epsc"][0:1, 0:1]

    zld = E.sem("zld")
    for k in range(KD):
        zld.bump(16)
        E("g", "dma_start", out=zbf[:, k * T:(k + 1) * T],
          in_=x3b[k * 128:(k + 1) * 128, :],
          waits=([cast_mark["x3"]] if k == 0 else []),
          incs=[(zld.h, 16)])
    zbf_ready = zld.n

    x3ld = E.sem("x3ld")
    x3_lv = {}

    def sched_x3_loads(c):
        for k in range(KD):
            gseg = c * KD + k
            x3ld.bump(16)
            E("g", "dma_start", out=x3r[:, (gseg % 8) * TC:(gseg % 8 + 1) * TC],
              in_=x3[k * 128:(k + 1) * 128, c * TC:(c + 1) * TC],
              waits=([(ev4.h, KD * c)] if (c > 0 and k == 0) else []),
              incs=[(x3ld.h, 16)])
        for k in range(KD):
            x3_lv[c * KD + k] = x3ld.n

    wring = WStream(E, "pe_wring", OGG, 16)

    mm1, mm2, mm3, mm4 = [E.sem(f"mm{i}") for i in range(1, 5)]
    kp1, kp2, kp3, kp4 = [E.sem(f"kp{i}") for i in range(1, 5)]
    ev1, ev2, ev3, ev4 = [E.sem(f"ev{i}") for i in range(1, 5)]
    si1 = E.sem("si1")
    stq = E.sem("stq")
    lnfin_val = 0

    for c in range(NTC):
        sched_x3_loads(c)
        def evac1(ogg, m2, banks, mm_val, c=c):
            G = c * (HID // OGG) + ogg
            for m in range(m2):
                idx = ogg * m2 + m
                tslot = tmp[:, ((G % 2) * 2 + m) * 512:((G % 2) * 2 + m) * 512 + TC]
                w = [(mm1.h, mm_val)] if m == 0 else []
                if m == 0:
                    w.append((sev.h, scales_done))
                if G >= 2 and m == 0:
                    w.append((ev1.h, 2 * (G - 1)))
                E("sc", "activation", tslot, banks[m][:, 0:TC], AF.Silu,
                  scale=scale_sb["s1"][:, idx:idx + 1],
                  waits=w, incs=[(si1.h, 1)])
                si1.bump(1)
                E("v", "scalar_tensor_tensor",
                  hsl[:, idx * TC:(idx + 1) * TC], banks[m2 + m][:, 0:TC],
                  scale_sb["s2"][:, idx:idx + 1], tslot, ALU.mult, ALU.mult,
                  waits=[(si1.h, si1.n)], incs=[(ev1.h, 1)])
                ev1.bump(1)

        pre = [(zld.h, zbf_ready), (sev.h, scales_done)]
        if lnfin_val > 0:
            pre.append((E.sem("ln_fin").h, lnfin_val))
        mm_stage(E, ws=wring, nw=2,
                 wsrc_fn=lambda w, ogg, k: (w1b if w == 0 else w2b)[
                     ogg * OGG:(ogg + 1) * OGG, k * 128:(k + 1) * 128],
                 n_ogg=HID // OGG, OGG=OGG, kc=KD,
                 rhs_fn=lambda k, c=c: zbf[:, k * T + c * TC:k * T + (c + 1) * TC],
                 width=TC, ps_pool=ps_pool, bank_base=0, mm_sem=mm1, ev_sem=ev1,
                 evac_fn=evac1, pre_waits=pre, ev_per_ogg=2, kprog=kp1,
                 first_load_waits=[[cast_mark["w1"]],
                                   [cast_mark["w2"]]])

        def evac2(ogg, m2, banks, mm_val):
            for m in range(m2):
                idx = ogg * m2 + m
                E("sc", "activation", gsl[:, idx * TC:(idx + 1) * TC],
                  banks[m][:, 0:TC], AF.Gelu,
                  scale=scale_sb["hs1"][:, idx:idx + 1],
                  waits=([(mm2.h, mm_val), (sev.h, scales_done)] if m == 0 else []),
                  incs=[(ev2.h, 1)])
                ev2.bump(1)

        mm_stage(E, ws=wring, nw=1,
                 wsrc_fn=lambda w, ogg, k: h1b[ogg * OGG:(ogg + 1) * OGG,
                                               k * 128:(k + 1) * 128],
                 n_ogg=HID // OGG, OGG=OGG, kc=KH,
                 rhs_fn=lambda k: hsl[:, k * TC:(k + 1) * TC],
                 width=TC, ps_pool=ps_pool, bank_base=0, mm_sem=mm2, ev_sem=ev2,
                 evac_fn=evac2, kprog=kp2, pre_waits=[(ev1.h, ev1.n)],
                 first_load_waits=[[cast_mark["h1"]]])

        def evac3(ogg, m2, banks, mm_val):
            for m in range(m2):
                idx = ogg * m2 + m
                E("sc", "activation", hsl[:, idx * TC:(idx + 1) * TC],
                  banks[m][:, 0:TC], AF.Gelu,
                  scale=scale_sb["hs2"][:, idx:idx + 1],
                  waits=([(mm3.h, mm_val), (sev.h, scales_done)] if m == 0 else []),
                  incs=[(ev3.h, 1)])
                ev3.bump(1)

        mm_stage(E, ws=wring, nw=1,
                 wsrc_fn=lambda w, ogg, k: h2b[ogg * OGG:(ogg + 1) * OGG,
                                               k * 128:(k + 1) * 128],
                 n_ogg=HID // OGG, OGG=OGG, kc=KH,
                 rhs_fn=lambda k: gsl[:, k * TC:(k + 1) * TC],
                 width=TC, ps_pool=ps_pool, bank_base=4, mm_sem=mm3, ev_sem=ev3,
                 evac_fn=evac3, kprog=kp3, pre_waits=[(ev2.h, ev2.n)],
                 first_load_waits=[[cast_mark["h2"]]])

        def evac4(ogg, m2, banks, mm_val, c=c):
            for m in range(m2):
                idx = ogg * m2 + m
                gseg = c * KD + idx
                w = [(x3ld.h, x3_lv[gseg])]
                if m == 0:
                    w.append((mm4.h, mm_val))
                    w.append((sev.h, scales_done))
                if c > 0:
                    w.append((stq.h, 16 * KD * c))
                E("v", "scalar_tensor_tensor",
                  z3[:, idx * TC:(idx + 1) * TC], banks[m][:, 0:TC],
                  scale_sb["s3"][:, idx:idx + 1],
                  x3r[:, (gseg % 8) * TC:(gseg % 8) * TC + TC],
                  ALU.mult, ALU.add,
                  waits=w, incs=[(ev4.h, 1)])
                ev4.bump(1)

        mm_stage(E, ws=wring, nw=1,
                 wsrc_fn=lambda w, ogg, k: w3b[ogg * OGG:(ogg + 1) * OGG,
                                               k * 128:(k + 1) * 128],
                 n_ogg=D // OGG, OGG=OGG, kc=KH,
                 rhs_fn=lambda k: hsl[:, k * TC:(k + 1) * TC],
                 width=TC, ps_pool=ps_pool, bank_base=0, mm_sem=mm4, ev_sem=ev4,
                 evac_fn=evac4, kprog=kp4, pre_waits=[(ev3.h, ev3.n)],
                 first_load_waits=[[cast_mark["w3"]]])

        scr["guard"] = [(ev4.h, ev4.n)]
        lnfin, lnfin_val = ln_fm(
            E, "ln", lambda s: z3[:, s * TC:(s + 1) * TC], KD, TC,
            ps_pool, [4, 5, 6, 7], scr, D)
        for s in range(KD):
            stq.bump(16)
            E("sy", "dma_start", out=out[s * 128:(s + 1) * 128, c * TC:(c + 1) * TC],
              in_=z3[:, s * TC:(s + 1) * TC],
              waits=[(lnfin.h, lnfin_val - KD + s + 1)], incs=[(stq.h, 16)])

    E.wait("sy", stq.h, stq.n)
    E.finalize()
    return E


def emit_pd(nc, ctx, *, D, S, par, x2, out, wq, wk, wv, wo,
            bq=None, bk=None, bv=None, bo=None, wc=None, sw=None, gate=None):
    l0 = gate is not None
    HD = 64
    H = D // HD
    KD = D // 128
    NB = S // 128
    NOWN = NB // 2
    TOWN = NOWN * 128
    NQG = TOWN // 512
    TGW = 256
    NTG = S // TGW

    E = Emit(nc, ctx)
    sbt = lambda name, shape, dt: ctx.enter_context(nc.sbuf_tensor("pd_" + name, shape, dt))
    ps = [ctx.enter_context(nc.psum_tensor(f"pd_ps{i}", [128, 512], F32)) for i in range(8)]
    dram = lambda name, shape: nc.dram_tensor("pd_" + name, shape, BF16).ap()

    x2b = dram("x2b", [D, S])
    wqb, wkb = dram("wqb", [D, D]), dram("wkb", [D, D])
    wvb, wob = dram("wvb", [D, D]), dram("wob", [D, D])

    cast_mark = {}

    def cast(name, src, dst, rpc=512):
        cs = E.sem("cast_" + name)
        for i in range(cdiv(src.shape[0], rpc)):
            r0, r1 = i * rpc, min((i + 1) * rpc, src.shape[0])
            cs.bump(16)
            E("g", "dma_start", out=dst[r0:r1], in_=src[r0:r1], incs=[(cs.h, 16)])
        cast_mark[name] = (cs.h, cs.n)

    cast("x2", x2, x2b, rpc=1024)
    cast("wq", wq, wqb)
    cast("wk", wk, wkb)
    cast("wv", wv, wvb)
    cast("wo", wo, wob)

    ident, ident_ini = make_identity(E)

    Ksl = sbt("K", [128, KD * S], BF16)
    Vsl = sbt("V", [128, (S // 128) * (H * 65)], BF16)
    Qq = sbt("Qq", [128, KD * 512], BF16)
    Qblk = sbt("Qblk", [128, 2 * KD * 128], BF16)
    Qblk2 = sbt("Qblk2", [128, 2 * KD * 128], BF16)
    Osl = sbt("O", [128, KD * 512], BF16)
    z3 = sbt("z3", [128, KD * 512], F32)
    x2g = sbt("x2g", [128, max(KD * TGW * 2, 2 * KD * 256)], BF16)
    att = sbt("att", [128, 2 * 512], BF16)
    rbs = sbt("rbs", [128, 2 * 512], F32)
    tri = sbt("tri", [128, 128], F32)
    cb = sbt("cb", [128, H], F32)
    cmx = sbt("cmx", [128, 4 * H], F32)
    srow = sbt("srow", [1, 512 + 16], F32)
    scr = dict(
        zb=x2g[:, 0:KD * 256],
        sq=x2g[:, KD * 256:2 * KD * 256],
        ones_col=sbt("ones_col", [128, 1], BF16),
        ones_row=sbt("ones_row", [1, 128], F32),
        row=sbt("lnrow", [1, 4 * 256], F32),
        epsc=sbt("epsc", [1, 1], F32),
    )
    dum = sbt("dum", [1, 1], F32)
    Qdram = nc.dram_tensor("pd_Qdram", [D, TOWN], BF16).ap()
    ini = E.sem("ini")
    E("g", "memset", scr["ones_col"][:, :], 1.0)
    E("g", "memset", scr["epsc"][:, :], LN_EPS)
    E("g", "memset", Vsl[:, :], 1.0, incs=[(ini.h, 1)])
    ini.bump(1)
    E("g", "memset", scr["ones_row"][:, :], 1.0, incs=[(ini.h, 1)])
    ini.bump(1)
    E("g", "memset", tri[:, :], 0.0, incs=[(ini.h, 1)])
    ini.bump(1)
    E("g", "affine_select", tri[:, :], tri[:, :], pattern=[[1, 128]],
      compare_op=ALU.is_ge, fill=-1e30, base=0, channel_multiplier=-1,
      waits=[(ini.h, 3)], incs=[(ini.h, 1)])
    ini.bump(1)
    scr["ini"] = (ini, ini.n)
    scr["eps"] = scr["epsc"][0:1, 0:1]
    INI = ini.n

    bcol = {}
    if l0:
        sldA, smmA, sevA = E.sem("sldA"), E.sem("smmA"), E.sem("sevA")
        brawls = {}
        for nm, bsrc in [("bq", bq), ("bk", bk), ("bv", bv), ("bo", bo)]:
            rows = D // 128
            braw = ctx.enter_context(nc.sbuf_tensor("pa_braw_" + nm, [rows, 128], F32))
            bcol[nm] = ctx.enter_context(nc.sbuf_tensor("pa_bcol_" + nm, [128, rows], F32))
            sldA.bump(16)
            E("g", "dma_start", out=braw[:, :],
              in_=bsrc.rearrange("(r p) -> r p", p=128), incs=[(sldA.h, 16)])
            brawls[nm] = braw
        bn = 0
        for nm in ("bq", "bk", "bv", "bo"):
            rows = D // 128
            prev = sevA.n
            smmA.bump(1)
            E("t", "transpose", ps[0][:, 0:rows], brawls[nm][:, :],
              ident[0:rows, 0:rows],
              waits=[(sldA.h, 16 * 4), (ident_ini.h, 2), (sevA.h, prev)],
              incs=[(smmA.h, 1)])
            sevA.bump(1)
            E("v", "tensor_copy", bcol[nm][:, :], ps[0][:, 0:rows],
              waits=[(smmA.h, smmA.n)], incs=[(sevA.h, 1)])
        BIAS_DONE = sevA.n
        NHP = H // 2
        Asl = ctx.enter_context(nc.sbuf_tensor("pa_A", [128, NHP * 64], BF16))
        Msl = ctx.enter_context(nc.sbuf_tensor("pa_M", [128, NHP * 64], F32))
        Wcl = ctx.enter_context(nc.sbuf_tensor("pa_Wc", [128, NHP * 64], F32))
        gts = ctx.enter_context(nc.sbuf_tensor("pa_gt", [1, H], F32))
        gtc = ctx.enter_context(nc.sbuf_tensor("pa_gtc", [128, NHP], F32))
        Atmp = ctx.enter_context(nc.sbuf_tensor("pa_At", [128, 64], F32))
        aini = E.sem("aini")
        adma = E.sem("adma")
        agm = E.sem("agm")
        E("g", "memset", Msl[:, :], 0.0, incs=[(aini.h, 1)])
        aini.bump(1)
        for e in range(2):
            adma.bump(16)
            E("g", "dma_start",
              out=Wcl[e * 64:(e + 1) * 64, :].rearrange("f (j g) -> f j g", g=64),
              in_=wc.rearrange("(j e) f g -> e f j g", e=2)[e],
              incs=[(adma.h, 16)])
        adma.bump(16)
        E("g", "dma_start", out=gts[:, :], in_=gate[None, :], incs=[(adma.h, 16)])
        E("sc", "activation", gts[:, :], gts[:, :], AF.Sigmoid,
          waits=[(adma.h, adma.n)], incs=[(aini.h, 1)])
        aini.bump(1)
        msw = E.sem("msw")
        MS = sw.shape[1]
        for f in range(64):
            L = min(MS, 64 - f)
            for e in range(2):
                msw.bump(16)
                dst = Msl[e * 64 + f:e * 64 + f + 1, :].rearrange(
                    "p (j g) -> p j g", g=64)[:, :, f:f + L]
                E("sc", "dma_start", out=dst,
                  in_=sw.rearrange("(j e) t -> e j t", e=2)[e][None, :, 0:L],
                  allow_slow_non_contiguous=True,
                  waits=([(aini.h, 1)] if (f == 0 and e == 0) else []),
                  incs=[(msw.h, 16)])
        for j in range(NHP):
            for e in range(2):
                agm.bump(1)
                E("t", "matmul", ps[1][e * 64:e * 64 + 64, j:j + 1],
                  scr["ones_row"][0:1, 0:64], gts[0:1, 2 * j + e:2 * j + e + 1],
                  start=True, stop=True,
                  waits=([(aini.h, aini.n), (ini.h, INI)] if (j == 0 and e == 0) else []),
                  incs=[(agm.h, 1)])
        E("v", "tensor_copy", gtc[:, :], ps[1][:, 0:NHP],
          waits=[(agm.h, agm.n)], incs=[(agm.h, 1)])
        agm.bump(1)
        AGM_DONE = agm.n
        acb = E.sem("acb")
        for j in range(NHP):
            w = [(agm.h, agm.n)]
            if j == 0:
                w += [(msw.h, msw.n), (aini.h, aini.n)]
            else:
                w.append((acb.h, acb.n))
            E("v", "tensor_sub", Atmp[:, :], Wcl[:, j * 64:(j + 1) * 64],
              Msl[:, j * 64:(j + 1) * 64], waits=w, incs=[(acb.h, 1)])
            acb.bump(1)
            E("v", "tensor_scalar", Atmp[:, :], Atmp[:, :], gtc[:, j:j + 1], 0.125,
              ALU.mult, ALU.mult, waits=[(acb.h, acb.n)], incs=[(acb.h, 1)])
            acb.bump(1)
            E("v", "scalar_tensor_tensor", Asl[:, j * 64:(j + 1) * 64],
              Msl[:, j * 64:(j + 1) * 64], 0.125, Atmp[:, :],
              ALU.mult, ALU.add, waits=[(acb.h, acb.n)], incs=[(acb.h, 1)])
            acb.bump(1)
        A_DONE = acb.n



    wring = WStream(E, "pd_wring", 256, 16)

    xld = [E.sem("xld0"), E.sem("xld1")]
    xrel = E.sem("xrel")
    xg_lv = {}

    def sched_x2_group(tg):
        s = xld[tg % 2]
        for k in range(KD):
            s.bump(16)
            E("g", "dma_start",
              out=x2g[:, ((tg % 2) * KD + k) * TGW:((tg % 2) * KD + k) * TGW + TGW],
              in_=x2b[k * 128:(k + 1) * 128, tg * TGW:(tg + 1) * TGW],
              waits=([cast_mark["x2"]] if tg == 0 and k == 0 else
                     ([(xrel.h, tg - 1)] if (tg >= 2 and k == 0) else [])),
              incs=[(s.h, 16)])
        xg_lv[tg] = (s.h, s.n)

    def xg_ap(tg, k):
        return x2g[:, ((tg % 2) * KD + k) * TGW:((tg % 2) * KD + k) * TGW + TGW]

    qev, kev, vev = E.sem("qev"), E.sem("kev"), E.sem("vev")
    qst = [E.sem("qst0"), E.sem("qst1")]
    pmm = E.sem("pmm")
    OGG = 256

    for tg in range(NTG):
        sched_x2_group(tg)
        for ogg in range(D // OGG):
            for k in range(KD):
                lw = [cast_mark["wk"]] if (tg == 0 and ogg == 0 and k == 0) else []
                if wring.last_cert is not None:
                    lw.append(wring.last_cert)
                ap, ldw = wring.load(
                    wkb[ogg * OGG:(ogg + 1) * OGG, k * 128:(k + 1) * 128], waits=lw)
                for m in range(2):
                    w = [ldw] if m == 0 else []
                    if k == 0 and m == 0:
                        if ogg < 2:
                            w.append((qev.h, qev.n))
                        else:
                            w.append((kev.h, kev.n - 2))
                        w.append(xg_lv[tg])
                        if l0 and tg == 0 and ogg == 0:
                            w += [(sevA.h, BIAS_DONE), (agm.h, AGM_DONE)]
                    incs = []
                    if k == KD - 1 and m == 1:
                        pmm.bump(1)
                        incs = [(pmm.h, 1)]
                    E("t", "matmul", ps[(ogg % 2) * 2 + m][:, 0:TGW],
                      ap[:, m * 128:(m + 1) * 128], xg_ap(tg, k),
                      start=(k == 0), stop=(k == KD - 1), waits=w, incs=incs)
            wring.last_cert = (pmm.h, pmm.n)
            for m in range(2):
                r = ogg * 2 + m
                kw = [(pmm.h, pmm.n)] if m == 0 else []
                if l0:
                    if m == 0:
                        kw.append((sevA.h, BIAS_DONE))
                    E("v", "tensor_scalar_add",
                      Ksl[:, r * S + tg * TGW:r * S + tg * TGW + TGW],
                      ps[(ogg % 2) * 2 + m][:, 0:TGW], bcol["bk"][:, r:r + 1],
                      waits=kw, incs=[(kev.h, 1)])
                else:
                    E("v", "tensor_copy",
                      Ksl[:, r * S + tg * TGW:r * S + tg * TGW + TGW],
                      ps[(ogg % 2) * 2 + m][:, 0:TGW],
                      waits=kw, incs=[(kev.h, 1)])
                kev.bump(1)
        NV = cdiv(D, 256)
        VW = min(256, D)
        HPT = VW // 64
        for tt in range(TGW // 128):
            gt = tg * (TGW // 128) + tt
            for nv in range(NV):
                vbank = ps[4 + (vev.n % 2)]
                for k in range(KD):
                    lw = [cast_mark["wv"]] if (tg == 0 and tt == 0 and nv == 0 and k == 0) else []
                    lw.append(wring.last_cert)
                    ap, ldw = wring.load(
                        wvb[nv * VW:(nv + 1) * VW, k * 128:(k + 1) * 128], waits=lw)
                    w = [ldw]
                    if k == 0:
                        w.append((vev.h, max(0, vev.n - 1)))
                        if l0 and tg == 0 and tt == 0 and nv == 0:
                            w.append((agm.h, AGM_DONE))
                    incs = []
                    if k == KD - 1:
                        pmm.bump(1)
                        incs = [(pmm.h, 1)]
                    E("t", "matmul", vbank[:, 0:VW],
                      xg_ap(tg, k)[:, tt * 128:(tt + 1) * 128], ap,
                      start=(k == 0), stop=(k == KD - 1), waits=w, incs=incs)
                wring.last_cert = (pmm.h, pmm.n)
                hbase = nv * HPT
                dst = Vsl[:, gt * (H * 65) + hbase * 65:gt * (H * 65) + hbase * 65 + HPT * 65]
                E("v", "tensor_copy",
                  dst.rearrange("p (h c) -> p h c", c=65)[:, :, 0:64],
                  vbank[:, 0:VW].rearrange("p (h c) -> p h c", c=64),
                  waits=[(pmm.h, pmm.n), (ini.h, INI)], incs=[(vev.h, 1)])
                vev.bump(1)
        for tt in range(TGW // 128):
            gb = tg * (TGW // 128) + tt
            if gb % 2 != par:
                continue
            ob = gb // 2
            for ogg in range(D // OGG):
                for k in range(KD):
                    lw = [cast_mark["wq"]] if (ogg == 0 and k == 0 and ob == 0) else []
                    lw.append(wring.last_cert)
                    ap, ldw = wring.load(
                        wqb[ogg * OGG:(ogg + 1) * OGG, k * 128:(k + 1) * 128], waits=lw)
                    for m in range(2):
                        w = [ldw] if m == 0 else []
                        if k == 0 and m == 0:
                            if ogg < 2:
                                w.append((kev.h, kev.n))
                            delta = 2 if (D // OGG) >= 2 else 0
                            w.append((qev.h, max(0, qev.n - delta)))
                        incs = []
                        if k == KD - 1 and m == 1:
                            pmm.bump(1)
                            incs = [(pmm.h, 1)]
                        E("t", "matmul", ps[(ogg % 2) * 2 + m][:, 0:128],
                          ap[:, m * 128:(m + 1) * 128],
                          xg_ap(tg, k)[:, tt * 128:(tt + 1) * 128],
                          start=(k == 0), stop=(k == KD - 1), waits=w, incs=incs)
                wring.last_cert = (pmm.h, pmm.n)
                pb = ob % 2
                for m in range(2):
                    r = ogg * 2 + m
                    w = []
                    if m == 0:
                        w.append((pmm.h, pmm.n))
                        if ogg == 0:
                            w.append((qst[pb].h, qst[pb].n))
                    if l0:
                        if ogg == 0 and m == 0:
                            w.append((sevA.h, BIAS_DONE))
                        E("v", "tensor_scalar_add",
                          Qblk[:, (pb * KD + r) * 128:(pb * KD + r) * 128 + 128],
                          ps[(ogg % 2) * 2 + m][:, 0:128],
                          bcol["bq"][:, r:r + 1], waits=w, incs=[(qev.h, 1)])
                    else:
                        E("sc", "activation",
                          Qblk[:, (pb * KD + r) * 128:(pb * KD + r) * 128 + 128],
                          ps[(ogg % 2) * 2 + m][:, 0:128], AF.Copy, scale=0.125,
                          waits=w, incs=[(qev.h, 1)])
                    qev.bump(1)
            pb = ob % 2
            if l0:
                qpm = E.sem("qpm")
                for hd in range(H):
                    ft, fo = hd // 2, (hd % 2) * 64
                    j = hd // 2
                    if hd >= 2:
                        whd = qev.n - 1
                    else:
                        whd = max(0, qev.n - 9 + hd)
                    w = [(qev.h, qev.n if hd == 0 else whd)]
                    if ob < 2 and hd == 0:
                        w.append((acb.h, A_DONE))
                    qpm.bump(1)
                    E("t", "matmul", ps[6 + (hd % 2)][0:64, 0:128],
                      Asl[:, j * 64:(j + 1) * 64][fo:fo + 64, :],
                      Qblk[:, (pb * KD + ft) * 128:(pb * KD + ft) * 128 + 128][fo:fo + 64, :],
                      start=True, stop=True, waits=w, incs=[(qpm.h, 1)])
                    cw = [(qpm.h, qpm.n)]
                    if hd == 0:
                        cw.append((qst[pb].h, qst[pb].n))
                    E("v", "tensor_copy",
                      Qblk2[:, (pb * KD + ft) * 128:(pb * KD + ft) * 128 + 128][fo:fo + 64, :],
                      ps[6 + (hd % 2)][0:64, 0:128],
                      waits=cw, incs=[(qev.h, 1)])
                    qev.bump(1)
                qsrc = Qblk2
            else:
                qsrc = Qblk
            qst[pb].bump(16)
            E("g", "dma_start",
              out=Qdram.rearrange("(r p) t -> p r t", p=128)[:, :, ob * 128:(ob + 1) * 128],
              in_=qsrc[:, pb * KD * 128:(pb + 1) * KD * 128].rearrange("p (r c) -> p r c", c=128),
              waits=[(qev.h, qev.n)], incs=[(qst[pb].h, 16)])
        xrel.bump(1)
        E("v", "tensor_copy", dum[0:1, 0:1], scr["epsc"][0:1, 0:1],
          waits=[(pmm.h, pmm.n)], incs=[(xrel.h, 1)])
    KEV, VEV, QEV = kev.n, vev.n, qev.n

    p0r = E.sem("p0r")
    p0t = E.sem("p0t")
    p0c = E.sem("p0c")
    smm = E.sem("smm")
    sxp = E.sem("sxp")
    msk = E.sem("msk")
    avm = E.sem("avm")
    aev = E.sem("aev")
    rr = E.sem("rr")
    qql = E.sem("qql")
    womm = E.sem("womm")
    woev = E.sem("woev")
    x2ld = E.sem("x2ld")
    stq = E.sem("stq")
    kpw = E.sem("kpw")
    lnstS = E.sem("ln_st")
    lnfinS = E.sem("ln_fin")

    def qg_qap(hd, qg):
        ft, fo = hd // 2, (hd % 2) * 64
        return Qq[:, ft * 512:ft * 512 + 512][fo:fo + 64, :]

    lnfin_val = 0
    for qg in range(NQG):
        w = [(qst[0].h, qst[0].n), (qst[1].h, qst[1].n)]
        if qg > 0:
            w += [(smm.h, smm.n), (p0t.h, p0t.n)]
        for k in range(KD):
            qql.bump(16)
            E("g", "dma_start", out=Qq[:, k * 512:(k + 1) * 512],
              in_=Qdram[k * 128:(k + 1) * 128, qg * 512:(qg + 1) * 512],
              waits=(w if k == 0 else []), incs=[(qql.h, 16)])
        qqlv = qql.n
        for hd in range(H):
            ft, fo = hd // 2, (hd % 2) * 64
            for tt in range(4):
                gb = 8 * qg + 2 * tt + par
                kap = Ksl[:, ft * S + gb * 128:ft * S + gb * 128 + 128][fo:fo + 64, :]
                qap = Qq[:, ft * 512 + tt * 128:ft * 512 + tt * 128 + 128][fo:fo + 64, :]
                w = [(p0r.h, p0r.n)] if tt == 0 else []
                if hd == 0 and tt == 0:
                    w.append((qql.h, qqlv))
                    if qg == 0:
                        w += [(kev.h, KEV), (vev.h, VEV)]
                    else:
                        w.append((lnfinS.h, lnfin_val))
                E("t", "matmul", ps[6][:, tt * 128:(tt + 1) * 128], qap, kap,
                  start=True, stop=True, waits=w,
                  incs=([(p0t.h, 1)] if tt == 3 else []))
            p0t.bump(1)
            E("v", "tensor_reduce", cmx[:, hd * 4:hd * 4 + 4],
              ps[6][:, 0:512].rearrange("p (t c) -> p t c", c=128),
              AX.X, ALU.max, waits=[(p0t.h, p0t.n)], incs=[(p0r.h, 1)])
            p0r.bump(1)
            E("v", "reduce_max", cmx[:, hd * 4:hd * 4 + 1],
              cmx[:, hd * 4:hd * 4 + 4], AX.X,
              waits=[(p0r.h, p0r.n)], incs=[(p0r.h, 1)])
            p0r.bump(1)
            E("t", "transpose", ps[7][0:1, 0:128], cmx[:, hd * 4:hd * 4 + 1],
              ident[:, :], waits=[(p0r.h, p0r.n), (ident_ini.h, 2)],
              incs=[(p0t.h, 1)])
            p0t.bump(1)
            E("v", "reduce_max", srow[0:1, 512 + hd:513 + hd],
              ps[7][0:1, 0:128], AX.X, negate=True,
              waits=[(p0t.h, p0t.n)], incs=[(p0r.h, 1)])
            p0r.bump(1)
            E("t", "matmul", ps[7][:, 128:129], scr["ones_row"][:, :],
              srow[0:1, 512 + hd:513 + hd], start=True, stop=True,
              waits=[(p0r.h, p0r.n), (ini.h, INI)], incs=[(p0t.h, 1)])
            p0t.bump(1)
            E("v", "tensor_copy", cb[:, hd:hd + 1], ps[7][:, 128:129],
              waits=[(p0t.h, p0t.n), (sxp.h, sxp.n)], incs=[(p0c.h, 1)])
            p0c.bump(1)
        womm_prev = womm.n
        for hd in range(H):
            ft, fo = hd // 2, (hd % 2) * 64
            nchunk = 8 * qg + 7 + par
            avbank = ps[2 + (hd % 2)]
            for kb in range(nchunk):
                kap = Ksl[:, ft * S + kb * 128:ft * S + kb * 128 + 128][fo:fo + 64, :]
                sbank = ps[kb % 2]
                w = [(sxp.h, max(0, sxp.n - 1))]
                if kb == 0:
                    w.append((p0c.h, p0c.n))
                    w.append((aev.h, max(0, aev.n - 1)))
                E("t", "matmul", sbank[:, 0:512], kap, qg_qap(hd, qg),
                  start=True, stop=True, waits=w, incs=[(smm.h, 1)])
                smm.bump(1)
                r = kb - (8 * qg + par)
                if 0 <= r <= 6 and r % 2 == 0:
                    t0 = r // 2
                    E("v", "tensor_tensor",
                      sbank[:, t0 * 128:(t0 + 1) * 128],
                      sbank[:, t0 * 128:(t0 + 1) * 128], tri[:, :], ALU.add,
                      waits=[(smm.h, smm.n), (ini.h, INI)], incs=[(msk.h, 1)])
                    msk.bump(1)
                    ew = [(msk.h, msk.n)]
                else:
                    ew = [(smm.h, smm.n)]
                aslot = att[:, (kb % 2) * 512:(kb % 2) * 512 + 512]
                E("sc", "activation", aslot, sbank[:, 0:512], AF.Exp,
                  bias=cb[:, hd:hd + 1], waits=ew, incs=[(sxp.h, 1)])
                sxp.bump(1)
                nzero = min(4, (r + 1) // 2) if r >= 1 else 0
                for t0 in range(nzero):
                    E("v", "memset", aslot[:, t0 * 128:(t0 + 1) * 128], 0.0,
                      waits=[(sxp.h, sxp.n)], incs=[(msk.h, 1)])
                    msk.bump(1)
                avw = [(sxp.h, sxp.n)]
                if nzero:
                    avw.append((msk.h, msk.n))
                vap = Vsl[:, kb * (H * 65) + hd * 65:kb * (H * 65) + (hd + 1) * 65]
                E("t", "matmul", avbank[0:65, 0:512], vap, aslot,
                  start=(kb == 0), stop=(kb == nchunk - 1),
                  waits=avw, incs=[(avm.h, 1)])
                avm.bump(1)
            sslot = srow[0:1, 0:512]
            E("v", "tensor_copy", sslot, avbank[64:65, 0:512],
              waits=[(avm.h, avm.n), (p0t.h, p0t.n)], incs=[(rr.h, 1)])
            rr.bump(1)
            E("sc", "activation", sslot, sslot, AF.Ln,
              waits=[(rr.h, rr.n)], incs=[(rr.h, 1)])
            rr.bump(1)
            E("sc", "activation", sslot, sslot, AF.Exp, scale=-1.0,
              waits=[(rr.h, rr.n)], incs=[(rr.h, 1)])
            rr.bump(1)
            wb = [(rr.h, rr.n), (aev.h, max(0, aev.n - 1)), (lnstS.h, lnstS.n)]
            E("t", "matmul", ps[4 + (hd % 2)][0:64, 0:512],
              scr["ones_row"][0:1, 0:64], sslot, start=True, stop=True,
              waits=wb, incs=[(p0t.h, 1)])
            p0t.bump(1)
            E("v", "tensor_copy", rbs[0:64, (hd % 2) * 512:(hd % 2) * 512 + 512],
              ps[4 + (hd % 2)][0:64, 0:512],
              waits=[(p0t.h, p0t.n)], incs=[(rr.h, 1)])
            rr.bump(1)
            ow = [(rr.h, rr.n)]
            if qg > 0 and hd == 0:
                ow.append((womm.h, womm_prev))
            if l0:
                E("v", "tensor_tensor",
                  Osl[:, ft * 512:ft * 512 + 512][fo:fo + 64, :],
                  avbank[0:64, 0:512], rbs[0:64, (hd % 2) * 512:(hd % 2) * 512 + 512],
                  ALU.mult, waits=ow, incs=[(rr.h, 1)])
                rr.bump(1)
                E("v", "tensor_scalar_add",
                  Osl[:, ft * 512:ft * 512 + 512][fo:fo + 64, :],
                  Osl[:, ft * 512:ft * 512 + 512][fo:fo + 64, :],
                  bcol["bv"][fo:fo + 64, ft:ft + 1],
                  waits=[(rr.h, rr.n)], incs=[(aev.h, 1)])
            else:
                E("v", "tensor_tensor",
                  Osl[:, ft * 512:ft * 512 + 512][fo:fo + 64, :],
                  avbank[0:64, 0:512], rbs[0:64, (hd % 2) * 512:(hd % 2) * 512 + 512],
                  ALU.mult, waits=ow, incs=[(aev.h, 1)])
            aev.bump(1)
        for k in range(KD):
            x2ld.bump(16)
            lw = []
            if k == 0:
                lw.append((aev.h, aev.n))
                lw.append((stq.h, stq.n))
            srcap = x2[k * 128:(k + 1) * 128, :].rearrange(
                "p (b c) -> p b c", c=128)[:, par::2, :][:, 4 * qg:4 * qg + 4, :]
            E("g", "dma_start", out=z3[:, k * 512:(k + 1) * 512],
              in_=srcap, waits=lw, incs=[(x2ld.h, 16)])
        x2lv = x2ld.n

        def evacwo(ogg, m2, banks, mm_val, qg=qg, x2lv=x2lv):
            for m in range(m2):
                idx = ogg * m2 + m
                w = [(x2ld.h, x2lv)]
                if m == 0:
                    w.append((womm.h, mm_val))
                if l0:
                    E("v", "scalar_tensor_tensor",
                      z3[:, idx * 512:(idx + 1) * 512], banks[m][:, 0:512],
                      bcol["bo"][:, idx:idx + 1],
                      z3[:, idx * 512:(idx + 1) * 512],
                      ALU.add, ALU.add, waits=w, incs=[(woev.h, 1)])
                else:
                    E("v", "tensor_add",
                      z3[:, idx * 512:(idx + 1) * 512], banks[m][:, 0:512],
                      z3[:, idx * 512:(idx + 1) * 512],
                      waits=w, incs=[(woev.h, 1)])
                woev.bump(1)

        mm_stage(E, ws=wring, nw=1,
                 wsrc_fn=lambda w, ogg, k: wob[ogg * OGG:(ogg + 1) * OGG,
                                               k * 128:(k + 1) * 128],
                 n_ogg=D // OGG, OGG=OGG, kc=KD,
                 rhs_fn=lambda k: Osl[:, k * 512:(k + 1) * 512],
                 width=512, ps_pool=ps, bank_base=0, mm_sem=womm, ev_sem=woev,
                 evac_fn=evacwo, ev_per_ogg=2, kprog=kpw,
                 pre_waits=[(aev.h, aev.n)],
                 first_load_waits=[[cast_mark["wo"]]])

        for half in range(2):
            scr["guard"] = [(woev.h, woev.n)]
            lnfin, lnfin_val = ln_fm(
                E, "ln", lambda s, half=half: z3[:, s * 512 + half * 256:s * 512 + half * 256 + 256],
                KD, 256, ps, [4, 5, 6, 7], scr, D)
            for s in range(KD):
                stq.bump(16)
                E("sy", "dma_start",
                  out=out[s * 128:(s + 1) * 128, qg * 512 + half * 256:qg * 512 + half * 256 + 256],
                  in_=z3[:, s * 512 + half * 256:s * 512 + half * 256 + 256],
                  waits=[(lnfin.h, lnfin_val - KD + s + 1)], incs=[(stq.h, 16)])

    E.wait("sy", stq.h, stq.n)
    E.finalize()
    return E


def emit_pb(nc, ctx, *, D, T, DI, x_in, out, wqkv, wo, w1, b1, w2, b2,
            ln_g, ln_b, qs, ks, posb, lmask, SHALF=2048):
    HD = 64
    H = D // HD
    KD = D // 128
    HB = 384
    TT = HB + T
    NTT = TT // 128
    XTS = 2433 if T == 2048 else (TT + 1)
    W = 256
    DI2 = 2 * DI
    NAT = cdiv(DI, 128)
    LK = DI - (NAT - 1) * 128
    TC = 512
    NCH = T // TC
    QK_SCALE = 8.0

    E = Emit(nc, ctx)
    sbt = lambda name, shape, dt: ctx.enter_context(nc.sbuf_tensor("pb_" + name, shape, dt))
    ps = [ctx.enter_context(nc.psum_tensor(f"pb_ps{i}", [128, 512], F32)) for i in range(8)]
    dram = lambda name, shape: nc.dram_tensor("pb_" + name, shape, BF16).ap()

    wqkvb = dram("wqkvb", [D, 3 * D])
    wob = dram("wob", [D, D])
    w1b = dram("w1b", [D, DI2])
    w2b = dram("w2b", [DI, D])

    cast_mark = {}

    def cast(name, src, dst, rpc=512):
        cs = E.sem("cast_" + name)
        for i in range(cdiv(src.shape[0], rpc)):
            r0, r1 = i * rpc, min((i + 1) * rpc, src.shape[0])
            cs.bump(16)
            E("g", "dma_start", out=dst[r0:r1], in_=src[r0:r1], incs=[(cs.h, 16)])
        cast_mark[name] = (cs.h, cs.n)

    cast("wqkv", wqkv, wqkvb)
    cast("wo", wo, wob)
    cast("w1", w1, w1b)
    cast("w2", w2, w2b)

    ident, ident_ini = make_identity(E)
    identb = sbt("identb", [128, 128], BF16)
    ib_ini = E.sem("ib_ini")
    E("g", "memset", identb[:, :], 1.0, incs=[(ib_ini.h, 1)])
    ib_ini.bump(1)
    E("g", "affine_select", identb[:, :], identb[:, :], pattern=[[-1, 128]],
      compare_op=ALU.is_equal, fill=0.0, base=0, channel_multiplier=1,
      waits=[(ib_ini.h, 1)], incs=[(ib_ini.h, 1)])
    ib_ini.bump(1)

    xsl = sbt("x", [128, KD * XTS], F32)
    xld = E.sem("xld")
    ini = E.sem("ini")
    for k in range(KD):
        E("g", "memset", xsl[:, k * XTS:k * XTS + 1], 0.0,
          incs=([(ini.h, 1)] if k == KD - 1 else []))
    ini.bump(1)
    for k in range(KD):
        xld.bump(16)
        E("g", "dma_start", out=xsl[:, k * XTS + 1:k * XTS + 1 + TT],
          in_=x_in[k * 128:(k + 1) * 128, :],
          waits=([(ini.h, 1)] if k == 0 else []), incs=[(xld.h, 16)])
    XLD = xld.n

    def xap(k, c0, w, shifted):
        off = k * XTS + 1 + c0 - (1 if shifted else 0)
        return xsl[:, off:off + w]

    sld, smm, sev = E.sem("sld"), E.sem("smm"), E.sem("sev")
    vraw = {}
    vcol = {}
    specs = [("qs", qs, 64), ("ks", ks, 64), ("g", ln_g, D), ("b", ln_b, D),
             ("b2", b2, D), ("lm", lmask, TT)]
    for nm, src, n in specs:
        rows = cdiv(n, 128)
        pr = 128 if n >= 128 else n
        vraw[nm] = sbt("vraw_" + nm, [rows, pr], F32)
        vcol[nm] = sbt("vcol_" + nm, [128, rows], F32)
        sld.bump(16)
        E("g", "dma_start", out=vraw[nm][:, :],
          in_=src.rearrange("(r p) -> r p", p=pr), incs=[(sld.h, 16)])
    SLD = sld.n
    for nm, src, n in specs:
        rows = cdiv(n, 128)
        pr = 128 if n >= 128 else n
        prev = sev.n
        smm.bump(1)
        E("t", "transpose", ps[0][0:pr, 0:rows], vraw[nm][:, :],
          ident[0:rows, 0:rows],
          waits=[(sld.h, SLD), (ident_ini.h, 1), (sev.h, prev)],
          incs=[(smm.h, 1)])
        sev.bump(1)
        E("v", "tensor_copy", vcol[nm][0:pr, :], ps[0][0:pr, 0:rows],
          waits=[(smm.h, smm.n)], incs=[(sev.h, 1)])
    for nm in ("qs", "ks"):
        prev = sev.n
        sev.bump(1)
        E("v", "tensor_copy", vcol[nm][64:128, :], vcol[nm][0:64, :],
          waits=[(sev.h, prev)], incs=[(sev.h, 1)])
    b1a = sbt("b1a", [128, NAT], F32)
    b1g = sbt("b1g", [128, NAT], F32)
    b1raw = sbt("b1raw", [NAT - 1, 128], F32)
    b1ld = E.sem("b1ld")
    b1mm = E.sem("b1mm")
    nfull = NAT - 1
    for dst, off in ((b1a, 0), (b1g, DI)):
        lv = b1ld.bump(16)
        E("g", "dma_start", out=b1raw[:, :],
          in_=b1[off:off + nfull * 128].rearrange("(r p) -> r p", p=128),
          waits=[(b1mm.h, b1mm.n)], incs=[(b1ld.h, 16)])
        b1mm.bump(1)
        E("t", "transpose", ps[2][:, 0:nfull], b1raw[:, :],
          ident[0:nfull, 0:nfull],
          waits=[(b1ld.h, lv), (ident_ini.h, 1), (b1mm.h, b1mm.n - 1)],
          incs=[(b1mm.h, 1)])
        b1mm.bump(1)
        E("v", "tensor_copy", dst[:, 0:nfull], ps[2][:, 0:nfull],
          waits=[(b1mm.h, b1mm.n - 1)], incs=[(b1mm.h, 1)])
        lv = b1ld.bump(16)
        E("g", "dma_start", out=b1raw[0:1, 0:LK],
          in_=b1[off + nfull * 128:off + DI][None, :],
          waits=[(b1mm.h, b1mm.n)], incs=[(b1ld.h, 16)])
        b1mm.bump(1)
        E("t", "transpose", ps[2][0:LK, 0:1], b1raw[0:1, 0:LK],
          ident[0:1, 0:1],
          waits=[(b1ld.h, lv), (b1mm.h, b1mm.n - 1)], incs=[(b1mm.h, 1)])
        b1mm.bump(1)
        E("v", "tensor_copy", dst[0:LK, nfull:NAT], ps[2][0:LK, 0:1],
          waits=[(b1mm.h, b1mm.n - 1)], incs=[(b1mm.h, 1)])
    B1LD = b1mm.n
    SEV = sev.n

    xo2 = sbt("fxo", [128, KD * 512], F32)
    msl2 = sbt("fm", [128, NAT * 512], BF16)
    csl = sbt("cs", [128, TT], BF16)
    ssl = sbt("ss", [128, TT], BF16)
    cil = sbt("ci", [128, TT], BF16)
    sil = sbt("si", [128, TT], BF16)
    rota = E.sem("rota")
    rcol = sbt("rcol", [128, 4], F32)
    if KD * 512 >= 3584:
        rrow = xo2[:, 0:3072]
        rib = xo2[:, 3072:3584].bitcast(mybir.dt.int32)
    else:
        rrow = sbt("rrow", [128, 512 * 6], F32)
        rib = sbt("rib", [128, 512], mybir.dt.int32)
    posr = sbt("posr", [1, 1], F32)
    rota16 = E.sem("rota16")
    rota16.bump(16)
    E("g", "dma_start", out=posr[:, :], in_=posb[None, :], incs=[(rota16.h, 16)])
    E("g", "iota", rcol[:, 0:1], pattern=[[0, 1]], base=0, channel_multiplier=1,
      allow_small_or_imprecise_dtypes=True, incs=[(rota.h, 1)])
    rota.bump(1)
    vchain = E.sem("rvch")

    def vop(method, *a, w=(), **kw):
        pv = vchain.n
        vchain.bump(1)
        E("v", method, *a, waits=list(w) + [(vchain.h, pv)], incs=[(vchain.h, 1)], **kw)

    def scop(method, *a, w=(), **kw):
        pv = vchain.n
        vchain.bump(1)
        E("sc", method, *a, waits=list(w) + [(vchain.h, pv)], incs=[(vchain.h, 1)], **kw)

    ri = sbt("ri", [128, 1], mybir.dt.int32)
    vop("tensor_scalar_mul", rcol[:, 1:2], rcol[:, 0:1], 1.0 / 32,
        w=[(rota.h, 1)])
    vop("tensor_copy", ri[:, :], rcol[:, 1:2])
    vop("tensor_copy", rcol[:, 1:2], ri[:, :])
    vop("scalar_tensor_tensor", rcol[:, 1:2], rcol[:, 1:2], -32.0, rcol[:, 0:1],
        ALU.mult, ALU.add)
    vop("tensor_scalar", rcol[:, 2:3], rcol[:, 1:2], 1.0 / 32, -0.499,
        ALU.mult, ALU.add)
    vop("tensor_copy", ri[:, :], rcol[:, 2:3])
    vop("tensor_copy", rcol[:, 2:3], ri[:, :])
    vop("scalar_tensor_tensor", rcol[:, 1:2], rcol[:, 2:3], -32.0, rcol[:, 1:2],
        ALU.mult, ALU.add)
    scop("activation", rcol[:, 2:3], rcol[:, 1:2], AF.Exp,
         scale=-float(np.log(10000.0) / 32.0))
    vop("tensor_scalar", rcol[:, 3:4], rcol[:, 1:2], 2.0 / 89.6, 25.6 / 89.6,
        ALU.mult, ALU.add)
    scop("activation", rcol[:, 3:4], rcol[:, 3:4], AF.Ln)
    RCOL = vchain.n

    TWO_PI = float(2 * np.pi)
    posc = sbt("posc", [128, 1], F32)
    onesr = sbt("onesr", [1, 128], F32)
    E("g", "memset", onesr[:, :], 1.0, incs=[(rota.h, 1)])
    rota.bump(1)
    E("t", "matmul", ps[1][:, 0:1], onesr[:, :], posr[0:1, 0:1],
      start=True, stop=True,
      waits=[(rota16.h, 16), (rota.h, rota.n)], incs=[(rota.h, 1)])
    rota.bump(1)
    vop("tensor_copy", posc[:, :], ps[1][:, 0:1], w=[(rota.h, rota.n)])
    nrt = cdiv(TT, 512)
    for rc in range(nrt):
        c0 = rc * 512
        wdt = min(512, TT - c0)
        A = rrow[:, 0:wdt]
        B = rrow[:, 512:512 + wdt]
        C = rrow[:, 1024:1024 + wdt]
        Dw = rrow[:, 1536:1536 + wdt]
        Ew = rrow[:, 2048:2048 + wdt]
        Fw = rrow[:, 2560:2560 + wdt]
        E("g", "iota", A, pattern=[[1, wdt]], base=0, channel_multiplier=0,
          allow_small_or_imprecise_dtypes=True,
          waits=[(vchain.h, vchain.n)], incs=[(rota.h, 1)])
        rota.bump(1)
        vop("tensor_scalar_add", A, A, float(c0), w=[(rota.h, rota.n)])
        vop("tensor_scalar", A, A, posc[:, 0:1], None, ALU.add)
        vop("tensor_scalar", B, A, 1.0 / 128.0, -float(SHALF) / 128.0,
            ALU.mult, ALU.add)
        vop("tensor_scalar", B, B, rcol[:, 3:4], None, ALU.mult,
            w=[(vchain.h, RCOL)])
        scop("activation", Ew, B, AF.Exp)
        scop("activation", Fw, B, AF.Exp, scale=-1.0)
        vop("tensor_scalar", A, A, rcol[:, 2:3], None, ALU.mult)
        vop("tensor_scalar_mul", B, A, 1.0 / TWO_PI)
        vop("tensor_copy", rib[:, 0:wdt], B)
        vop("tensor_copy", B, rib[:, 0:wdt])
        vop("scalar_tensor_tensor", A, B, -TWO_PI, A, ALU.mult, ALU.add)
        scop("activation", B, A, AF.Sin, scale=0.5)
        scop("activation", C, A, AF.Sin, scale=0.25)
        scop("activation", C, C, AF.Square)
        vop("tensor_scalar", C, C, -2.0, 1.0, ALU.mult, ALU.add)
        vop("scalar_tensor_tensor", Dw, B, 2.0, C, ALU.mult, ALU.mult)
        scop("activation", B, B, AF.Square)
        vop("tensor_scalar", B, B, -2.0, 1.0, ALU.mult, ALU.add)
        vop("tensor_tensor", csl[:, c0:c0 + wdt], B, Ew, ALU.mult)
        vop("tensor_tensor", ssl[:, c0:c0 + wdt], Dw, Ew, ALU.mult)
        vop("tensor_tensor", cil[:, c0:c0 + wdt], B, Fw, ALU.mult)
        vop("tensor_tensor", sil[:, c0:c0 + wdt], Dw, Fw, ALU.mult)
    ROTA_DONE = vchain.n

    Rt = sbt("R", [128, 128], BF16)
    rsel = E.sem("rsel")
    E("g", "memset", Rt[:, :], 0.0, incs=[(rsel.h, 1)])
    rsel.bump(1)
    neg1 = sbt("neg1", [64, 64], BF16)
    E("g", "memset", neg1[0:32, :], 1.0, incs=[(rsel.h, 1)])
    rsel.bump(1)
    E("g", "memset", neg1[32:64, :], -1.0, incs=[(rsel.h, 1)])
    rsel.bump(1)
    E("g", "affine_select", Rt[0:32, 0:64], neg1[0:32, 0:64],
      pattern=[[1, 64]], compare_op=ALU.is_equal, fill=0.0, base=-32,
      channel_multiplier=-1, waits=[(rsel.h, 3)], incs=[(rsel.h, 1)])
    rsel.bump(1)
    E("g", "affine_select", Rt[32:64, 0:64], neg1[32:64, 0:64],
      pattern=[[1, 64]], compare_op=ALU.is_equal, fill=0.0, base=0,
      channel_multiplier=-1, waits=[(rsel.h, rsel.n)], incs=[(rsel.h, 1)])
    rsel.bump(1)
    E("v", "tensor_copy", Rt[64:128, 64:128], Rt[0:64, 0:64],
      waits=[(rsel.h, rsel.n)], incs=[(rsel.h, 1)])
    rsel.bump(1)
    RSEL = rsel.n

    NPAIR = 2
    FTB2 = max(1, KD // 4)
    NBQ = KD // FTB2
    LN8 = float(np.log(8.0))
    kpsl = sbt("kp", [128, NPAIR * KD * 256], BF16)
    vsl = sbt("vh", [128, NPAIR * 2 * H * 64], BF16)
    qpsl = sbt("qp", [128, 2 * KD * 256], BF16)
    xbt = sbt("xbt", [128, KD * 257], BF16)
    sqa = sbt("sqa", [128, KD * 256], BF16)
    ibsa = sqa
    invq = sbt("invq", [128, 4 * H], F32)
    nrowq = sbt("nrowq", [H, 256], F32)
    invkb = sbt("invkb", [H, 256], BF16)
    qsel = sbt("qsel", [128, KD * H], BF16)
    ksel = sbt("ksel", [H, KD * 128], BF16)
    ln8c = sbt("ln8c", [128, 1], F32)
    ubf = sbt("ubf", [128, 4 * 768], BF16)
    attq = sbt("attq", [128, 2 * 384], BF16)
    attT = sbt("attT", [128, 2 * 384], BF16)
    negx = sbt("negx", [128, 2], F32)
    negb = sbt("negb", [128, 2], F32)
    rsv = sbt("rsv", [128, 2], F32)
    Osl = sbt("O", [128, KD * 512], BF16)
    mask3 = sbt("mask3", [128, 384], F32)
    lmq = sbt("lmq", [128, 3 * 384], BF16)
    dum2 = sbt("dum2", [1, 1], F32)
    epsc2 = sbt("epsc2", [1, 1], F32)

    mini = E.sem("mini")
    E("g", "memset", mask3[:, 0:384], 0.0, incs=[(mini.h, 1)])
    mini.bump(1)
    E("g", "memset", ln8c[:, :], LN8, incs=[(mini.h, 1)])
    mini.bump(1)
    E("g", "memset", dum2[:, :], 0.0, incs=[(mini.h, 1)])
    mini.bump(1)
    E("g", "affine_select", mask3[:, 0:128], mask3[:, 0:128], pattern=[[1, 128]],
      compare_op=ALU.is_ge, fill=-1e20, base=0, channel_multiplier=-1,
      waits=[(mini.h, mini.n)], incs=[(mini.h, 1)])
    mini.bump(1)
    E("g", "affine_select", mask3[:, 256:384], mask3[:, 256:384], pattern=[[-1, 128]],
      compare_op=ALU.is_ge, fill=-1e20, base=0, channel_multiplier=1,
      waits=[(mini.h, mini.n)], incs=[(mini.h, 1)])
    mini.bump(1)
    E("g", "memset", qsel[:, :], 1.0, incs=[(mini.h, 1)])
    mini.bump(1)
    E("g", "memset", ksel[:, :], 1.0, incs=[(mini.h, 1)])
    mini.bump(1)
    for ft in range(KD):
        qs_t = qsel[:, ft * H:(ft + 1) * H]
        E("g", "affine_select", qs_t[0:64, :], qs_t[0:64, :], pattern=[[1, H]],
          compare_op=ALU.is_equal, fill=0.0, base=-2 * ft, channel_multiplier=0,
          waits=[(mini.h, mini.n)], incs=[(mini.h, 1)])
        mini.bump(1)
        E("g", "affine_select", qs_t[64:128, :], qs_t[64:128, :], pattern=[[1, H]],
          compare_op=ALU.is_equal, fill=0.0, base=-(2 * ft + 1),
          channel_multiplier=0, waits=[(mini.h, mini.n)], incs=[(mini.h, 1)])
        mini.bump(1)
        ks_t = ksel[:, ft * 128:(ft + 1) * 128]
        E("g", "affine_select", ks_t[:, 0:64], ks_t[:, 0:64], pattern=[[0, 64]],
          compare_op=ALU.is_equal, fill=0.0, base=-2 * ft, channel_multiplier=1,
          waits=[(mini.h, mini.n)], incs=[(mini.h, 1)])
        mini.bump(1)
        E("g", "affine_select", ks_t[:, 64:128], ks_t[:, 64:128], pattern=[[0, 64]],
          compare_op=ALU.is_equal, fill=0.0, base=-(2 * ft + 1),
          channel_multiplier=1, waits=[(mini.h, mini.n)], incs=[(mini.h, 1)])
        mini.bump(1)
    MINI = mini.n
    lmm = E.sem("lmm")
    lmr = msl2.bitcast(F32)[0:1, 0:TT]
    lmld = E.sem("lmld")
    lmld.bump(16)
    E("g", "dma_start", out=lmr[:, :], in_=lmask[None, :], incs=[(lmld.h, 16)])
    for i, t in enumerate((2, 3, 4)):
        prev = lmm.n
        lmm.bump(1)
        E("t", "matmul", ps[5][:, 0:384], onesr[:, :],
          lmr[0:1, (t - 2) * 128:(t + 1) * 128],
          start=True, stop=True,
          waits=[(lmld.h, 16), (rota.h, 2), (lmm.h, prev)], incs=[(lmm.h, 1)])
        E("v", "tensor_copy", lmq[:, i * 384:(i + 1) * 384], ps[5][:, 0:384],
          waits=[(lmm.h, lmm.n)], incs=[(lmm.h, 1)])
        lmm.bump(1)
    LMM = lmm.n
    bankc5_init = (lmm.h, LMM)

    wring = WStream(E, "pb_wring", 256, 11)
    wring2 = WStream(E, "pb_wring2", 256, 11, queue="g")

    bankc = [None] * 8
    ubc = [None, None, None, None]

    def bwait(b):
        return [bankc[b]] if bankc[b] is not None else []

    pq = E.sem("pq")
    xbc = E.sem("xbc")
    pev = E.sem("pev")
    pevS = E.sem("pevS")
    apg = E.sem("apg")
    smx = E.sem("smx")
    sxp = E.sem("sxp")
    att_s = E.sem("att_s")
    avm = E.sem("avm")
    aev = E.sem("aev")
    womm = E.sem("womm")
    woev = E.sem("woev")
    ffst = E.sem("ffst")
    stq = E.sem("stq")
    pgv = E.sem("pgv")

    def proj_pair(p, ntl):
        W = 128 * ntl
        c0 = p * 256
        pslot = p % NPAIR
        ppar = p % 2
        for k in range(KD):
            w = [(pq.h, pq.n)] if k == 0 else []
            if p == 0 and k == 0:
                w.append((xld.h, XLD))
            E("g", "tensor_copy",
              xbt[:, k * 257:k * 257 + W + 1],
              xap(k, c0, W + 1, shifted=True), waits=w, incs=[(xbc.h, 1)])
            xbc.bump(1)
        XBC = xbc.n

        def xbap(k, shifted, tl=None):
            base = k * 257
            b = base if shifted else base + 1
            if tl is None:
                return xbt[:, b:b + W]
            return xbt[:, b + tl * 128:b + tl * 128 + 128]

        def stream_qk(coloff, bank, colb):
            for k in range(KD):
                ring = wring if k % 2 == 0 else wring2
                lw = []
                if p == 0 and coloff == 0 and colb == 0 and k <= 1:
                    lw.append(cast_mark["wqkv"])
                if ring.last_cert:
                    lw.append(ring.last_cert)
                ap, ldw = ring.load(
                    wqkvb[k * 128:(k + 1) * 128, coloff:coloff + 128],
                    transpose=False, waits=lw)
                w = [ldw]
                if k == 0:
                    w += bwait(ps.index(bank))
                    w.append((xbc.h, XBC))
                incs = []
                if k == KD - 1:
                    pq.bump(1)
                    incs = [(pq.h, 1)]
                E("t", "matmul", bank[:, colb:colb + W], ap,
                  xbap(k, shifted=(k >= KD // 2)),
                  start=(k == 0), stop=(k == KD - 1), waits=w, incs=incs)
            wring.last_cert = (pq.h, pq.n)
            wring2.last_cert = (pq.h, pq.n)

        for ft in range(KD):
            stream_qk(ft * 128, ps[ft // FTB2], (ft % FTB2) * 256)
        QMM = pq.n
        for bi in range(NBQ):
            if ntl == 2:
                E("sc", "activation",
                  sqa[:, bi * FTB2 * 256:(bi * FTB2 + FTB2) * 256],
                  ps[bi][:, 0:FTB2 * 256], AF.Square,
                  waits=[(pq.h, QMM)], incs=[(pevS.h, 1)])
                pevS.bump(1)
            else:
                for fi in range(FTB2):
                    ft = bi * FTB2 + fi
                    E("sc", "activation", sqa[:, ft * 256:ft * 256 + W],
                      ps[bi][:, fi * 256:fi * 256 + W], AF.Square,
                      waits=[(pq.h, QMM)], incs=[(pevS.h, 1)])
                    pevS.bump(1)
        SQQ = pevS.n
        for ft in range(KD):
            w = []
            if ft == 0:
                w = [(pevS.h, SQQ), (mini.h, MINI)] + bwait(4)
            E("t", "matmul", ps[4][0:H, 0:W], qsel[:, ft * H:(ft + 1) * H],
              sqa[:, ft * 256:ft * 256 + W], start=(ft == 0),
              stop=(ft == KD - 1), waits=w,
              incs=([(pq.h, 1)] if ft == KD - 1 else []))
        pq.bump(1)
        QSUM = pq.n
        E("v", "tensor_copy", nrowq[:, 0:W], ps[4][0:H, 0:W],
          waits=[(pq.h, QSUM)], incs=[(pev.h, 1)])
        pev.bump(1)
        bankc[4] = (pev.h, pev.n)
        NRQ = pev.n
        for tl in range(ntl):
            w = [(pev.h, NRQ), (ident_ini.h, 2)]
            if tl == 0:
                w += bwait(5)
            E("t", "transpose", ps[5][:, tl * H:(tl + 1) * H],
              nrowq[:, tl * 128:(tl + 1) * 128], ident[0:H, 0:H],
              waits=w, incs=[(pq.h, 1)])
            pq.bump(1)
        iq = invq[:, (ppar * 2) * H:(ppar * 2 + ntl) * H]
        E("v", "tensor_scalar_max", iq, ps[5][:, 0:ntl * H], 1e-24,
          waits=[(pq.h, pq.n), (sxp.h, sxp.n)], incs=[(pev.h, 1)])
        pev.bump(1)
        bankc[5] = (pev.h, pev.n)
        E("sc", "activation", iq, iq, AF.Ln,
          waits=[(pev.h, pev.n)], incs=[(pevS.h, 1)])
        pevS.bump(1)
        E("sc", "activation", iq, iq, AF.Exp, scale=-0.5, bias=ln8c[:, 0:1],
          waits=[(pevS.h, pevS.n), (mini.h, MINI)], incs=[(pevS.h, 1)])
        pevS.bump(1)
        for bi in range(NBQ):
            bank = ps[bi]
            ub = ubf[:, bi * 768:bi * 768 + 768]
            for fi in range(FTB2):
                ft = bi * FTB2 + fi
                w = [(pq.h, pq.n), (sev.h, SEV), (pev.h, pev.n)]
                if fi == 0 and ubc[bi] is not None:
                    w.append(ubc[bi])
                E("v", "tensor_scalar", ub[:, 0:W], bank[:, fi * 256:fi * 256 + W],
                  vcol["qs"][:, 0:1], None, ALU.mult,
                  waits=w, incs=[(pev.h, 1)])
                pev.bump(1)
                E("v", "tensor_tensor", ub[:, 256:256 + W], ub[:, 0:W],
                  csl[:, c0:c0 + W], ALU.mult,
                  waits=[(pev.h, pev.n), (vchain.h, ROTA_DONE)], incs=[(pev.h, 1)])
                pev.bump(1)
                E("v", "tensor_tensor", ub[:, 512:512 + W], ub[:, 0:W],
                  ssl[:, c0:c0 + W], ALU.mult,
                  waits=[(pev.h, pev.n)], incs=[(pev.h, 1)])
                pev.bump(1)
                w = [(pev.h, pev.n), (ib_ini.h, 2), (rsel.h, RSEL)]
                if fi == 0:
                    w.append((pevS.h, SQQ))
                E("t", "matmul", bank[:, fi * 256:fi * 256 + W], identb[:, :],
                  ub[:, 256:256 + W], start=True, stop=False, waits=w,
                  incs=[(pq.h, 1)])
                pq.bump(1)
                E("t", "matmul", bank[:, fi * 256:fi * 256 + W], Rt[:, :],
                  ub[:, 512:512 + W], start=False, stop=True, incs=[(pq.h, 1)])
                pq.bump(1)
                ubc[bi % 2] = (pq.h, pq.n)
            for fi in range(FTB2 if ntl == 1 else 1):
                ft0 = bi * FTB2 + fi
                wdt = FTB2 * 256 if ntl == 2 else W
                E("sc", "activation",
                  qpsl[:, (ppar * KD + ft0) * 256:(ppar * KD + ft0) * 256 + wdt],
                  bank[:, fi * 256:fi * 256 + wdt], AF.Copy, scale=1.0,
                  waits=[(pq.h, pq.n)], incs=[(pevS.h, 1)])
                pevS.bump(1)
            bankc[bi] = (pevS.h, pevS.n)
        for ft in range(KD):
            stream_qk(D + ft * 128, ps[ft // FTB2], (ft % FTB2) * 256)
        KMM = pq.n
        for bi in range(NBQ):
            if ntl == 2:
                E("sc", "activation",
                  sqa[:, bi * FTB2 * 256:(bi * FTB2 + FTB2) * 256],
                  ps[bi][:, 0:FTB2 * 256], AF.Square,
                  waits=[(pq.h, KMM), (pq.h, QSUM)], incs=[(pevS.h, 1)])
                pevS.bump(1)
            else:
                for fi in range(FTB2):
                    ft = bi * FTB2 + fi
                    E("sc", "activation", sqa[:, ft * 256:ft * 256 + W],
                      ps[bi][:, fi * 256:fi * 256 + W], AF.Square,
                      waits=[(pq.h, KMM), (pq.h, QSUM)], incs=[(pevS.h, 1)])
                    pevS.bump(1)
        SQK = pevS.n
        for ft in range(KD):
            w = []
            if ft == 0:
                w = [(pevS.h, SQK), (pev.h, NRQ)] + bwait(4)
            E("t", "matmul", ps[4][0:H, 0:W], qsel[:, ft * H:(ft + 1) * H],
              sqa[:, ft * 256:ft * 256 + W], start=(ft == 0),
              stop=(ft == KD - 1), waits=w,
              incs=([(pq.h, 1)] if ft == KD - 1 else []))
        pq.bump(1)
        KSUM = pq.n
        E("v", "tensor_scalar_max", nrowq[:, 0:W], ps[4][0:H, 0:W], 1e-24,
          waits=[(pq.h, KSUM)], incs=[(pev.h, 1)])
        pev.bump(1)
        bankc[4] = (pev.h, pev.n)
        E("sc", "activation", nrowq[:, 0:W], nrowq[:, 0:W], AF.Ln,
          waits=[(pev.h, pev.n)], incs=[(pevS.h, 1)])
        pevS.bump(1)
        E("sc", "activation", invkb[:, 0:W], nrowq[:, 0:W], AF.Exp, scale=-0.5,
          waits=[(pevS.h, pevS.n)], incs=[(pevS.h, 1)])
        pevS.bump(1)
        INVK = pevS.n
        for rnd in range(cdiv(KD, 2)):
            bnk = ps[5] if rnd % 2 == 0 else ps[4]
            for j in range(2):
                ft = rnd * 2 + j
                if ft >= KD:
                    continue
                w = []
                if j == 0:
                    w = [(pevS.h, INVK)] + bwait(5 if rnd % 2 == 0 else 4)
                E("t", "matmul", bnk[:, j * 256:j * 256 + W],
                  ksel[:, ft * 128:(ft + 1) * 128], invkb[0:H, 0:W],
                  start=True, stop=True, waits=w,
                  incs=([(pq.h, 1)] if (j == 1 or ft == KD - 1) else []))
            pq.bump(1)
            nf = min(2, KD - rnd * 2)
            if ntl == 2:
                E("v", "tensor_copy", ibsa[:, rnd * 512:rnd * 512 + nf * 256],
                  bnk[:, 0:nf * 256], waits=[(pq.h, pq.n)], incs=[(pev.h, 1)])
                pev.bump(1)
            else:
                for j in range(nf):
                    ft = rnd * 2 + j
                    E("v", "tensor_copy", ibsa[:, ft * 256:ft * 256 + W],
                      bnk[:, j * 256:j * 256 + W],
                      waits=[(pq.h, pq.n)], incs=[(pev.h, 1)])
                    pev.bump(1)
            bankc[5 if rnd % 2 == 0 else 4] = (pev.h, pev.n)
        IBSA = pev.n
        for bi in range(NBQ):
            bank = ps[bi]
            ub = ubf[:, bi * 768:bi * 768 + 768]
            for fi in range(FTB2):
                ft = bi * FTB2 + fi
                w = [(pev.h, IBSA), (pev.h, pev.n), (pq.h, pq.n)]
                if fi == 0 and ubc[bi] is not None:
                    w.append(ubc[bi])
                E("v", "scalar_tensor_tensor", ub[:, 0:W],
                  bank[:, fi * 256:fi * 256 + W], vcol["ks"][:, 0:1],
                  ibsa[:, ft * 256:ft * 256 + W],
                  ALU.mult, ALU.mult, waits=w, incs=[(pev.h, 1)])
                pev.bump(1)
                E("v", "tensor_tensor", ub[:, 256:256 + W], ub[:, 0:W],
                  cil[:, c0:c0 + W], ALU.mult,
                  waits=[(pev.h, pev.n)], incs=[(pev.h, 1)])
                pev.bump(1)
                E("v", "tensor_tensor", ub[:, 512:512 + W], ub[:, 0:W],
                  sil[:, c0:c0 + W], ALU.mult,
                  waits=[(pev.h, pev.n)], incs=[(pev.h, 1)])
                pev.bump(1)
                w = [(pev.h, pev.n)]
                if fi == 0:
                    w.append((pevS.h, SQK))
                E("t", "matmul", bank[:, fi * 256:fi * 256 + W], identb[:, :],
                  ub[:, 256:256 + W], start=True, stop=False, waits=w,
                  incs=[(pq.h, 1)])
                pq.bump(1)
                E("t", "matmul", bank[:, fi * 256:fi * 256 + W], Rt[:, :],
                  ub[:, 512:512 + W], start=False, stop=True, incs=[(pq.h, 1)])
                pq.bump(1)
                ubc[bi] = (pq.h, pq.n)
            for fi in range(FTB2 if ntl == 1 else 1):
                ft0 = bi * FTB2 + fi
                wdt = FTB2 * 256 if ntl == 2 else W
                wv = [(pq.h, pq.n)]
                if p >= NPAIR:
                    wv.append((apg.h, max(0, 2 * p - 4)))
                E("sc", "activation",
                  kpsl[:, (pslot * KD + ft0) * 256:(pslot * KD + ft0) * 256 + wdt],
                  bank[:, fi * 256:fi * 256 + wdt], AF.Copy, scale=1.0,
                  waits=wv, incs=[(pevS.h, 1)])
                pevS.bump(1)
            bankc[bi] = (pevS.h, pevS.n)
        NV = D // 256
        for nv in range(NV):
            for k in range(KD):
                ring = wring if k % 2 == 0 else wring2
                lw = [ring.last_cert] if ring.last_cert else []
                ap, ldw = ring.load(
                    wqkvb[k * 128:(k + 1) * 128,
                          2 * D + nv * 256:2 * D + (nv + 1) * 256],
                    transpose=False, waits=lw)
                for tl in range(ntl):
                    bank = ps[6 + tl]
                    w = [ldw] if tl == 0 else []
                    if k == 0:
                        if nv % 2 == 0:
                            w += bwait(6 + tl)
                        if nv == 0:
                            w.append((xbc.h, XBC))
                    incs = []
                    if k == KD - 1 and tl == ntl - 1:
                        pq.bump(1)
                        incs = [(pq.h, 1)]
                    E("t", "matmul", bank[:, (nv % 2) * 256:(nv % 2) * 256 + 256],
                      xbap(k, shifted=(k >= KD // 2), tl=tl), ap,
                      start=(k == 0), stop=(k == KD - 1), waits=w, incs=incs)
            wring.last_cert = (pq.h, pq.n)
            wring2.last_cert = (pq.h, pq.n)
            if nv % 2 == 1 or nv == NV - 1:
                nv0 = (nv // 2) * 2
                cw = (nv - nv0 + 1) * 256
                for tl in range(ntl):
                    wv = [(pq.h, pq.n)]
                    if p >= NPAIR:
                        wv.append((apg.h, max(0, 2 * p - 4)))
                    E("v", "tensor_copy",
                      vsl[:, (pslot * 2 + tl) * (H * 64) + nv0 * 256:
                          (pslot * 2 + tl) * (H * 64) + nv0 * 256 + cw],
                      ps[6 + tl][:, 0:cw], waits=wv, incs=[(pev.h, 1)])
                    pev.bump(1)
                    bankc[6 + tl] = (pev.h, pev.n)

    attc = {"aT": [None, None], "av": [None, None], "ex": [None, None],
            "ng": [None, None]}

    def attn_tile(gt):
        par2 = gt % 2
        pt = gt // 2
        ppar = pt % 2
        for hd in range(H):
            hp = hd % 2
            sl4 = hp
            sbank = ps[4 + hp]
            avb = ps[6 + hp]
            aq = attq[:, sl4 * 384:sl4 * 384 + 384]
            aT = attT[:, sl4 * 384:sl4 * 384 + 384]
            ft, fo = hd // 2, (hd % 2) * 64
            iqc = invq[:, (ppar * 2 + par2) * H + hd:(ppar * 2 + par2) * H + hd + 1]
            qap = qpsl[:, (ppar * KD + ft) * 256 + par2 * 128:
                       (ppar * KD + ft) * 256 + par2 * 128 + 128][fo:fo + 64, :]
            for d in range(3):
                t = gt - 2 + d
                slot = (t // 2) % NPAIR
                kap = kpsl[:, (slot * KD + ft) * 256 + (t % 2) * 128:
                           (slot * KD + ft) * 256 + (t % 2) * 128 + 128][fo:fo + 64, :]
                w = []
                if d == 0:
                    w += bwait(4 + hp)
                    if hd <= 1:
                        w.append((pev.h, pev.n))
                        w.append((pevS.h, pevS.n))
                E("t", "matmul", sbank[:, d * 128:(d + 1) * 128], qap, kap,
                  start=True, stop=True, waits=w,
                  incs=([(pq.h, 1)] if d == 2 else []))
            pq.bump(1)
            sc_mm = (pq.h, pq.n)
            E("v", "tensor_tensor", sbank[:, 0:384], sbank[:, 0:384], mask3[:, :],
              ALU.add, waits=[sc_mm, (mini.h, MINI)], incs=[(smx.h, 1)])
            smx.bump(1)
            if gt <= 4:
                E("v", "tensor_tensor", sbank[:, 0:384], sbank[:, 0:384],
                  lmq[:, (gt - 2) * 384:(gt - 1) * 384], ALU.add,
                  waits=[(smx.h, smx.n), (lmm.h, LMM)], incs=[(smx.h, 1)])
                smx.bump(1)
            mskc = (smx.h, smx.n)
            w = [mskc]
            if attc["ng"][hp] is not None:
                w.append(attc["ng"][hp])
            E("v", "reduce_max", negx[:, sl4:sl4 + 1], sbank[:, 0:384], AX.X,
              negate=True, waits=w, incs=[(smx.h, 1)])
            smx.bump(1)
            E("v", "tensor_scalar", negb[:, sl4:sl4 + 1], negx[:, sl4:sl4 + 1],
              1e19, iqc, ALU.min, ALU.mult,
              waits=[(smx.h, smx.n)], incs=[(smx.h, 1)])
            smx.bump(1)
            ngc = (smx.h, smx.n)
            w = [ngc]
            if attc["aT"][hp] is not None:
                w.append(attc["aT"][hp])
            E("sc", "activation", aq, sbank[:, 0:384], AF.Exp,
              bias=negb[:, sl4:sl4 + 1], scale=iqc,
              accum_out=rsv[:, sl4:sl4 + 1],
              waits=w, incs=[(sxp.h, 1)])
            sxp.bump(1)
            attc["ng"][hp] = (sxp.h, sxp.n)
            E("v", "tensor_scalar_max", rsv[:, sl4:sl4 + 1], rsv[:, sl4:sl4 + 1],
              1e-30, waits=[(sxp.h, sxp.n)], incs=[(smx.h, 1)])
            smx.bump(1)
            E("v", "reciprocal", rsv[:, sl4:sl4 + 1], rsv[:, sl4:sl4 + 1],
              waits=[(smx.h, smx.n)], incs=[(smx.h, 1)])
            smx.bump(1)
            E("v", "tensor_scalar", aq, aq, rsv[:, sl4:sl4 + 1], None, ALU.mult,
              waits=[(smx.h, smx.n)], incs=[(smx.h, 1)])
            smx.bump(1)
            aqrd = (smx.h, smx.n)
            for d in range(3):
                E("t", "transpose",
                  sbank[:, 0:192].bitcast(BF16)[:, d * 128:(d + 1) * 128],
                  aq[:, d * 128:(d + 1) * 128], identb[:, :],
                  waits=([aqrd] if d == 0 else []),
                  incs=[(pq.h, 1)])
                pq.bump(1)
            trc = (pq.h, pq.n)
            for d in range(3):
                w = []
                if d == 0:
                    w.append(trc)
                    if attc["av"][hp] is not None:
                        w.append(attc["av"][hp])
                E("v", "tensor_copy", aT[:, d * 128:(d + 1) * 128],
                  sbank[:, 0:192].bitcast(BF16)[:, d * 128:(d + 1) * 128],
                  waits=w, incs=[(att_s.h, 1)])
                att_s.bump(1)
            attc["aT"][hp] = (att_s.h, att_s.n)
            bankc[4 + hp] = (att_s.h, att_s.n)
            for d in range(3):
                t = gt - 2 + d
                slot = (t // 2) % NPAIR
                vap = vsl[:, (slot * 2 + t % 2) * (H * 64) + hd * 64:
                          (slot * 2 + t % 2) * (H * 64) + (hd + 1) * 64]
                w = [(att_s.h, att_s.n)] if d == 0 else []
                if d == 0:
                    w += bwait(6 + hp)
                E("t", "matmul", avb[0:64, 0:128], vap, aT[:, d * 128:(d + 1) * 128],
                  start=(d == 0), stop=(d == 2), waits=w,
                  incs=([(avm.h, 1)] if d == 2 else []))
            avm.bump(1)
            attc["av"][hp] = (avm.h, avm.n)
            oc = 0 if gt == 2 else (gt - 3) % 4
            E("v", "tensor_copy",
              Osl[:, ft * 512 + oc * 128:ft * 512 + oc * 128 + 128][fo:fo + 64, :],
              avb[0:64, 0:128],
              waits=[(avm.h, avm.n)], incs=[(aev.h, 1)])
            aev.bump(1)
            bankc[6 + hp] = (aev.h, aev.n)
        apg.bump(1)
        E("v", "tensor_copy", dum2[:, :], dum2[:, :],
          waits=[(aev.h, aev.n)], incs=[(apg.h, 1)])

    def wo_group(tiles):
        wdt = len(tiles) * 128
        for ogg in range(D // 256):
            for k in range(KD):
                ring = wring if k % 2 == 0 else wring2
                lw = [ring.last_cert] if ring.last_cert else []
                if ogg == 0 and k <= 1:
                    lw.append(cast_mark["wo"])
                ap, ldw = ring.load(
                    wob[k * 128:(k + 1) * 128, ogg * 256:(ogg + 1) * 256],
                    transpose=False, waits=lw)
                for m in range(2):
                    w = [ldw] if m == 0 else []
                    if k == 0 and m == 0:
                        w += bwait((ogg % 2) * 2 + 0) + bwait((ogg % 2) * 2 + 1)
                        if ogg == 0:
                            w.append((aev.h, aev.n))
                    incs = []
                    if k == KD - 1 and m == 1:
                        pq.bump(1)
                        incs = [(pq.h, 1)]
                    E("t", "matmul", ps[(ogg % 2) * 2 + m][:, 0:wdt],
                      ap[:, m * 128:(m + 1) * 128],
                      Osl[:, k * 512:k * 512 + wdt],
                      start=(k == 0), stop=(k == KD - 1), waits=w, incs=incs)
            wring.last_cert = (pq.h, pq.n)
            wring2.last_cert = (pq.h, pq.n)
            for m in range(2):
                r = ogg * 2 + m
                dstx = xap(r, tiles[0] * 128, wdt, shifted=False)
                w = [(pq.h, pq.n)] if m == 0 else []
                E("v", "tensor_add", dstx, ps[(ogg % 2) * 2 + m][:, 0:wdt], dstx,
                  waits=w, incs=[(woev.h, 1)])
                woev.bump(1)
                bankc[(ogg % 2) * 2 + m] = (woev.h, woev.n)

    bankc[5] = bankc5_init
    for pp in range(cdiv(NTT, 2)):
        ntl = min(2, NTT - 2 * pp)
        proj_pair(pp, ntl)
        for gt in range(2 * pp, 2 * pp + ntl):
            if gt == 3:
                wo_group([2])
            elif gt >= 7 and (gt - 3) % 4 == 0:
                wo_group([gt - 4, gt - 3, gt - 2, gt - 1])
            if gt >= 2:
                attn_tile(gt)
    wo_group(list(range(NTT - 4, NTT)))
    WOEV = woev.n

    zb2 = kpsl[:, 0:KD * 512]
    sq2 = vsl[:, 0:KD * 512]
    xn2 = Osl[:, 0:KD * 512]
    if KD * 129 >= 512:
        gel2 = xbt.bitcast(F32)[:, 0:512]
    else:
        gel2 = sbt("fgel", [128, 512], F32)
    if KD < 8:
        frow_s = sbt("frow", [1, 3 * 512], F32)
    onesf = sbt("onesf", [128, 1], BF16)
    fini = E.sem("fini")
    E("g", "memset", epsc2[:, :], LN_EPS)
    E("g", "memset", onesf[:, :], 1.0, incs=[(fini.h, 1)])
    fini.bump(1)
    fmm = E.sem("fmm")
    fev = E.sem("fev")
    fch = E.sem("fch")
    fsc = E.sem("fsc")

    for c in range(NCH):
        c0 = HB + c * TC
        prev_st = fmm.n
        for ft in range(KD):
            w = [(fch.h, fch.n)]
            if ft == 0:
                w.append((woev.h, WOEV))
                if prev_st > 0:
                    w.append((fmm.h, prev_st))
            E("v", "tensor_copy", zb2[:, ft * 512:(ft + 1) * 512],
              xap(ft, c0, TC, shifted=(ft >= KD // 2)), waits=w,
              incs=[(fch.h, 1)])
            fch.bump(1)
            E("sc", "activation", sq2[:, ft * 512:(ft + 1) * 512],
              zb2[:, ft * 512:(ft + 1) * 512], AF.Square,
              waits=[(fch.h, fch.n)], incs=[(fsc.h, 1)])
            fsc.bump(1)
        for ft in range(KD):
            E("t", "matmul", ps[0][0:1, 0:TC], onesf[:, :],
              zb2[:, ft * 512:(ft + 1) * 512], start=(ft == 0), stop=(ft == KD - 1),
              waits=([(fch.h, fch.n), (fini.h, 1)] + bwait(0)) if ft == 0 else [])
            E("t", "matmul", ps[1][0:1, 0:TC], onesf[:, :],
              sq2[:, ft * 512:(ft + 1) * 512], start=(ft == 0), stop=(ft == KD - 1),
              waits=([(fsc.h, fsc.n)] + bwait(1)) if ft == 0 else [],
              incs=([(fmm.h, 1)] if ft == KD - 1 else []))
        fmm.bump(1)
        if KD >= 8:
            mean = sqa.bitcast(F32)[0:1, 0:512]
            ex2 = ubf.bitcast(F32)[0:1, 0:512]
            var = qpsl.bitcast(F32)[0:1, 0:512]
        else:
            mean = frow_s[0:1, 0:512]
            ex2 = frow_s[0:1, 512:1024]
            var = frow_s[0:1, 1024:1536]
        rstd = var
        E("v", "tensor_scalar_mul", mean, ps[0][0:1, 0:TC], 1.0 / D,
          waits=[(fmm.h, fmm.n)], incs=[(fch.h, 1)])
        fch.bump(1)
        E("v", "tensor_scalar_mul", ex2, ps[1][0:1, 0:TC], 1.0 / D,
          waits=[(fch.h, fch.n)], incs=[(fch.h, 1)])
        fch.bump(1)
        E("v", "tensor_tensor", var, mean, mean, ALU.mult,
          waits=[(fch.h, fch.n)], incs=[(fch.h, 1)])
        fch.bump(1)
        E("v", "tensor_sub", var, ex2, var,
          waits=[(fch.h, fch.n)], incs=[(fch.h, 1)])
        fch.bump(1)
        E("sc", "activation", rstd, var, AF.Sqrt, bias=epsc2[0:1, 0:1],
          waits=[(fch.h, fch.n)], incs=[(fsc.h, 1)])
        fsc.bump(1)
        E("v", "reciprocal", rstd, rstd, waits=[(fsc.h, fsc.n)], incs=[(fch.h, 1)])
        fch.bump(1)
        bankc[0] = (fch.h, fch.n - 5)
        bankc[1] = (fch.h, fch.n - 4)
        E("t", "matmul", ps[2][:, 0:TC], onesr[:, :], rstd, start=True, stop=True,
          waits=[(fch.h, fch.n)] + bwait(2), incs=[(fmm.h, 1)])
        fmm.bump(1)
        E("t", "matmul", ps[3][:, 0:TC], onesr[:, :], mean, start=True, stop=True,
          waits=bwait(3), incs=[(fmm.h, 1)])
        fmm.bump(1)
        for ft in range(KD):
            E("v", "tensor_sub", xo2[:, ft * 512:(ft + 1) * 512],
              xap(ft, c0, TC, shifted=(ft >= KD // 2)), ps[3][:, 0:TC],
              waits=([(fmm.h, fmm.n)] if ft == 0 else []) + [(fch.h, fch.n)],
              incs=[(fch.h, 1)])
            fch.bump(1)
            E("v", "tensor_tensor", xo2[:, ft * 512:(ft + 1) * 512],
              xo2[:, ft * 512:(ft + 1) * 512], ps[2][:, 0:TC], ALU.mult,
              waits=[(fch.h, fch.n)], incs=[(fch.h, 1)])
            fch.bump(1)
            E("v", "tensor_scalar", xn2[:, ft * 512:(ft + 1) * 512],
              xo2[:, ft * 512:(ft + 1) * 512], vcol["g"][:, ft:ft + 1],
              vcol["b"][:, ft:ft + 1], ALU.mult, ALU.add,
              waits=[(fch.h, fch.n), (sev.h, SEV)], incs=[(fch.h, 1)])
            fch.bump(1)
        XN = fch.n
        bankc[2] = (fch.h, XN)
        bankc[3] = (fch.h, XN)
        for i in range(NAT):
            wd = 128 if i < NAT - 1 else LK
            par = i % 2
            bA, bG = ps[0 + par * 2], ps[1 + par * 2]
            for k in range(KD):
                lw = [wring.last_cert] if wring.last_cert else []
                if c == 0 and i == 0 and k == 0:
                    lw.append(cast_mark["w1"])
                apA, ldwA = wring.load(
                    w1b[k * 128:(k + 1) * 128, i * 128:i * 128 + wd],
                    transpose=False, waits=lw)
                lwG = [wring2.last_cert] if wring2.last_cert else []
                if c == 0 and i == 0 and k == 0:
                    lwG.append(cast_mark["w1"])
                apG, ldwG = wring2.load(
                    w1b[k * 128:(k + 1) * 128, DI + i * 128:DI + i * 128 + wd],
                    transpose=False, waits=lwG)
                w = [ldwA]
                if k == 0:
                    w += bwait(0 + par * 2) + [(fch.h, XN)]
                E("t", "matmul", bA[0:wd, 0:TC], apA, xn2[:, k * 512:(k + 1) * 512],
                  start=(k == 0), stop=(k == KD - 1), waits=w)
                w2_ = [ldwG]
                if k == 0:
                    w2_ += bwait(1 + par * 2)
                E("t", "matmul", bG[0:wd, 0:TC], apG, xn2[:, k * 512:(k + 1) * 512],
                  start=(k == 0), stop=(k == KD - 1), waits=w2_,
                  incs=([(fmm.h, 1)] if k == KD - 1 else []))
            fmm.bump(1)
            wring.last_cert = (fmm.h, fmm.n)
            wring2.last_cert = (fmm.h, fmm.n)
            E("sc", "activation", gel2[0:wd, 0:TC], bG[0:wd, 0:TC], AF.Gelu,
              bias=b1g[0:wd, i:i + 1],
              waits=[(fmm.h, fmm.n), (b1mm.h, B1LD), (fev.h, fev.n)],
              incs=[(fsc.h, 1)])
            fsc.bump(1)
            bankc[1 + par * 2] = (fsc.h, fsc.n)
            E("v", "scalar_tensor_tensor", msl2[0:wd, i * 512:i * 512 + TC],
              bA[0:wd, 0:TC], b1a[0:wd, i:i + 1], gel2[0:wd, 0:TC],
              ALU.add, ALU.mult,
              waits=[(fsc.h, fsc.n)], incs=[(fev.h, 1)])
            fev.bump(1)
            bankc[0 + par * 2] = (fev.h, fev.n)
        MDONE = fev.n
        for og in range(D // 256):
            par = og % 2
            for k in range(NAT):
                wd = 128 if k < NAT - 1 else LK
                ring = wring if k % 2 == 0 else wring2
                lw = [ring.last_cert] if ring.last_cert else []
                if c == 0 and og == 0 and k <= 1:
                    lw.append(cast_mark["w2"])
                ap, ldw = ring.load(
                    w2b[k * 128:k * 128 + wd, og * 256:(og + 1) * 256],
                    transpose=False, waits=lw)
                for m in range(2):
                    w = [ldw] if m == 0 else []
                    if k == 0 and m == 0:
                        w += bwait(4 + par * 2) + bwait(5 + par * 2) + [(fev.h, MDONE)]
                    E("t", "matmul", ps[4 + par * 2 + m][:, 0:TC],
                      ap[:, m * 128:(m + 1) * 128],
                      msl2[0:wd, k * 512:(k + 1) * 512],
                      start=(k == 0), stop=(k == NAT - 1),
                      waits=w, incs=([(fmm.h, 1)] if (k == NAT - 1 and m == 1) else []))
            fmm.bump(1)
            wring.last_cert = (fmm.h, fmm.n)
            wring2.last_cert = (fmm.h, fmm.n)
            for m in range(2):
                r = og * 2 + m
                E("v", "scalar_tensor_tensor", xo2[:, r * 512:r * 512 + TC],
                  ps[4 + par * 2 + m][:, 0:TC], vcol["b2"][:, r:r + 1],
                  xap(r, c0, TC, shifted=False), ALU.add, ALU.add,
                  waits=([(fmm.h, fmm.n)] if m == 0 else []), incs=[(fev.h, 1)])
                fev.bump(1)
                bankc[4 + par * 2 + m] = (fev.h, fev.n)
                stq.bump(16)
                E("sy", "dma_start", out=out[r * 128:(r + 1) * 128, c * TC:(c + 1) * TC],
                  in_=xo2[:, r * 512:r * 512 + TC],
                  waits=[(fev.h, fev.n)], incs=[(stq.h, 16)])
    E.wait("sy", stq.h, stq.n)
    E.finalize()



def emit_pc(nc, ctx, *, D, T, a, b, out, c=None, bias=None):
    KD = D // 128
    TC = 512
    NCH = T // TC
    E = Emit(nc, ctx)
    sbt = lambda name, shape, dt: ctx.enter_context(
        nc.sbuf_tensor("pc_" + name, shape, dt))
    ps = [ctx.enter_context(nc.psum_tensor(f"pc_ps{i}", [128, 512], F32))
          for i in range(5)]
    xa = sbt("xa", [128, KD * TC], F32)
    xb = sbt("xb", [128, KD * TC], F32)
    if c is not None:
        xc = sbt("xc", [128, KD * TC], F32)
    zb = sbt("zb", [128, KD * TC], BF16)
    sq = sbt("sq", [128, KD * TC], BF16)
    row = sbt("row", [1, 4 * TC], F32)
    onesc = sbt("onesc", [128, 1], BF16)
    onesr = sbt("onesr", [1, 128], F32)
    epsc = sbt("eps", [1, 1], F32)
    ini = E.sem("ini")
    E("g", "memset", onesc[:, :], 1.0)
    E("g", "memset", onesr[:, :], 1.0)
    E("g", "memset", epsc[:, :], LN_EPS, incs=[(ini.h, 1)])
    ini.bump(1)
    ident, ident_ini = make_identity(E)
    if bias is not None:
        braw = sbt("braw", [KD, 128], F32)
        bcl = sbt("bcl", [128, KD], F32)
        bld = E.sem("bld")
        bld.bump(16)
        E("g", "dma_start", out=braw[:, :],
          in_=bias.rearrange("(r p) -> r p", p=128), incs=[(bld.h, 16)])
        bmm = E.sem("bmm")
        E("t", "transpose", ps[4][:, 0:KD], braw[:, :], ident[0:KD, 0:KD],
          waits=[(bld.h, 16), (ident_ini.h, 2)], incs=[(bmm.h, 1)])
        bmm.bump(1)
        E("v", "tensor_copy", bcl[:, :], ps[4][:, 0:KD],
          waits=[(bmm.h, 1)], incs=[(bmm.h, 1)])
        bmm.bump(1)
        BCL = bmm.n
    scr = dict(zb=zb, sq=sq, ones_col=onesc, ones_row=onesr, row=row,
               eps=epsc[0:1, 0:1], ini=(ini, ini.n))
    ald = E.sem("ald")
    vadd = E.sem("vadd")
    stq = E.sem("stq")
    lastfin = None
    for cc in range(NCH):
        c0 = cc * TC
        stq_prior = stq.n
        for k in range(KD):
            ald.bump(16)
            E("sy", "dma_start", out=xa[:, k * TC:(k + 1) * TC],
              in_=a[k * 128:(k + 1) * 128, c0:c0 + TC],
              waits=([(stq.h, stq_prior)] if (k == 0 and cc > 0) else []),
              incs=[(ald.h, 16)])
            ald.bump(16)
            E("sy", "dma_start", out=xb[:, k * TC:(k + 1) * TC],
              in_=b[k * 128:(k + 1) * 128, c0:c0 + TC], incs=[(ald.h, 16)])
            if c is not None:
                ald.bump(16)
                E("sy", "dma_start", out=xc[:, k * TC:(k + 1) * TC],
                  in_=c[k * 128:(k + 1) * 128, c0:c0 + TC], incs=[(ald.h, 16)])
        ALD = ald.n
        for k in range(KD):
            E("v", "tensor_add", xa[:, k * TC:(k + 1) * TC],
              xa[:, k * TC:(k + 1) * TC], xb[:, k * TC:(k + 1) * TC],
              waits=([(ald.h, ALD)] if k == 0 else []), incs=[(vadd.h, 1)])
            vadd.bump(1)
            if c is not None:
                E("v", "tensor_add", xa[:, k * TC:(k + 1) * TC],
                  xa[:, k * TC:(k + 1) * TC], xc[:, k * TC:(k + 1) * TC],
                  waits=[(vadd.h, vadd.n)], incs=[(vadd.h, 1)])
                vadd.bump(1)
            if bias is not None:
                E("v", "tensor_scalar_add", xa[:, k * TC:(k + 1) * TC],
                  xa[:, k * TC:(k + 1) * TC], bcl[:, k:k + 1],
                  waits=[(vadd.h, vadd.n), (bmm.h, BCL)], incs=[(vadd.h, 1)])
                vadd.bump(1)
        guard = [(vadd.h, vadd.n)]
        if lastfin is not None:
            guard.append(lastfin)
        scr2 = dict(scr)
        scr2["guard"] = guard
        fin, fv = ln_fm(E, "ln", lambda s: xa[:, s * TC:(s + 1) * TC], KD, TC,
                        ps, [0, 1, 2, 3], scr2, D)
        lastfin = (fin.h, fv)
        for k in range(KD):
            stq.bump(16)
            E("sy", "dma_start", out=out[k * 128:(k + 1) * 128, c0:c0 + TC],
              in_=xa[:, k * TC:(k + 1) * TC],
              waits=[(fin.h, fv)], incs=[(stq.h, 16)])
    E.wait("sy", stq.h, stq.n)
    E.finalize()


def emit_pd2(nc, ctx, *, D, DQ, S, x2, out, wq, wk, wv, wo,
             bq=None, bk=None, bv=None, wc=None, sw=None, gate=None):
    l0 = gate is not None
    HD = 64
    H = DQ // HD
    KD = D // 128
    KQ = DQ // 128
    NB = S // 128
    NQG = S // 512
    TGW = 256
    NTG = S // TGW

    E = Emit(nc, ctx)
    sbt = lambda name, shape, dt: ctx.enter_context(nc.sbuf_tensor("p2_" + name, shape, dt))
    ps = [ctx.enter_context(nc.psum_tensor(f"p2_ps{i}", [128, 512], F32)) for i in range(8)]
    dram = lambda name, shape: nc.dram_tensor("p2_" + name, shape, BF16).ap()

    x2b = dram("x2b", [D, S])
    wqb, wkb = dram("wqb", [DQ, D]), dram("wkb", [DQ, D])
    wvb = dram("wvb", [DQ, D])
    wob = dram("wob", [D, DQ])

    cast_mark = {}

    def cast(name, src, dst, rpc=512):
        cs = E.sem("cast_" + name)
        for i in range(cdiv(src.shape[0], rpc)):
            r0, r1 = i * rpc, min((i + 1) * rpc, src.shape[0])
            cs.bump(16)
            E("g", "dma_start", out=dst[r0:r1], in_=src[r0:r1], incs=[(cs.h, 16)])
        cast_mark[name] = (cs.h, cs.n)

    cast("x2", x2, x2b, rpc=1024)
    cast("wq", wq, wqb)
    cast("wk", wk, wkb)
    cast("wv", wv, wvb)
    cast("wo", wo, wob)

    ident, ident_ini = make_identity(E)

    Ksl = sbt("K", [128, KQ * S], BF16)
    Vsl = sbt("V", [128, (S // 128) * (H * 65)], BF16)
    Qq = sbt("Qq", [128, KQ * 512], BF16)
    Qblk = sbt("Qblk", [128, 2 * KQ * 128], BF16)
    Qblk2 = sbt("Qblk2", [128, 2 * KQ * 128], BF16)
    Osl = sbt("O", [128, KQ * 512], BF16)
    z3 = sbt("z3", [128, KD * 512], F32)
    x2g = sbt("x2g", [128, KD * TGW * 2], BF16)
    att = sbt("att", [128, 2 * 512], BF16)
    rbs = sbt("rbs", [128, 2 * 512], F32)
    tri = sbt("tri", [128, 128], F32)
    cb = sbt("cb", [128, H], F32)
    cmx = sbt("cmx", [128, 4 * H], F32)
    srow = sbt("srow", [1, 512 + 16], F32)
    scr = dict(
        ones_col=sbt("ones_col", [128, 1], BF16),
        ones_row=sbt("ones_row", [1, 128], F32),
        epsc=sbt("epsc", [1, 1], F32),
    )
    dum = sbt("dum", [1, 1], F32)
    Qdram = nc.dram_tensor("p2_Qdram", [DQ, S], BF16).ap()
    ini = E.sem("ini")
    E("g", "memset", scr["ones_col"][:, :], 1.0)
    E("g", "memset", scr["epsc"][:, :], LN_EPS)
    E("g", "memset", Vsl[:, :], 1.0, incs=[(ini.h, 1)])
    ini.bump(1)
    E("g", "memset", scr["ones_row"][:, :], 1.0, incs=[(ini.h, 1)])
    ini.bump(1)
    E("g", "memset", tri[:, :], 0.0, incs=[(ini.h, 1)])
    ini.bump(1)
    E("g", "affine_select", tri[:, :], tri[:, :], pattern=[[1, 128]],
      compare_op=ALU.is_ge, fill=-1e30, base=0, channel_multiplier=-1,
      waits=[(ini.h, 3)], incs=[(ini.h, 1)])
    ini.bump(1)
    scr["ini"] = (ini, ini.n)
    scr["eps"] = scr["epsc"][0:1, 0:1]
    INI = ini.n

    bcol = {}
    if l0:
        sldA, smmA, sevA = E.sem("sldA"), E.sem("smmA"), E.sem("sevA")
        brawls = {}
        for nm, bsrc in [("bq", bq), ("bk", bk), ("bv", bv)]:
            rows = KQ
            braw = ctx.enter_context(nc.sbuf_tensor("p2_braw_" + nm, [rows, 128], F32))
            bcol[nm] = ctx.enter_context(nc.sbuf_tensor("p2_bcol_" + nm, [128, rows], F32))
            sldA.bump(16)
            E("g", "dma_start", out=braw[:, :],
              in_=bsrc.rearrange("(r p) -> r p", p=128), incs=[(sldA.h, 16)])
            brawls[nm] = braw
        bn = 0
        for nm in ("bq", "bk", "bv"):
            rows = KQ
            prev = sevA.n
            smmA.bump(1)
            E("t", "transpose", ps[0][0:128, 0:rows], brawls[nm][:, :],
              ident[0:rows, 0:rows],
              waits=[(sldA.h, 16 * 3), (ident_ini.h, 2), (sevA.h, prev)],
              incs=[(smmA.h, 1)])
            sevA.bump(1)
            E("v", "tensor_copy", bcol[nm][:, :], ps[0][:, 0:rows],
              waits=[(smmA.h, smmA.n)], incs=[(sevA.h, 1)])
        BIAS_DONE = sevA.n
        NHP = H // 2
        Asl = ctx.enter_context(nc.sbuf_tensor("p2_A", [128, NHP * 64], BF16))
        Msl = ctx.enter_context(nc.sbuf_tensor("p2_M", [128, NHP * 64], F32))
        Wcl = ctx.enter_context(nc.sbuf_tensor("p2_Wc", [128, NHP * 64], F32))
        gts = ctx.enter_context(nc.sbuf_tensor("p2_gt", [1, H], F32))
        gtc = ctx.enter_context(nc.sbuf_tensor("p2_gtc", [128, NHP], F32))
        Atmp = ctx.enter_context(nc.sbuf_tensor("p2_At", [128, 64], F32))
        aini = E.sem("aini")
        adma = E.sem("adma")
        agm = E.sem("agm")
        E("g", "memset", Msl[:, :], 0.0, incs=[(aini.h, 1)])
        aini.bump(1)
        for e in range(2):
            adma.bump(16)
            E("g", "dma_start",
              out=Wcl[e * 64:(e + 1) * 64, :].rearrange("f (j g) -> f j g", g=64),
              in_=wc.rearrange("(j e) f g -> e f j g", e=2)[e],
              incs=[(adma.h, 16)])
        adma.bump(16)
        E("g", "dma_start", out=gts[:, :], in_=gate[None, :], incs=[(adma.h, 16)])
        E("sc", "activation", gts[:, :], gts[:, :], AF.Sigmoid,
          waits=[(adma.h, adma.n)], incs=[(aini.h, 1)])
        aini.bump(1)
        msw = E.sem("msw")
        MS = sw.shape[1]
        for f in range(64):
            L = min(MS, 64 - f)
            for e in range(2):
                msw.bump(16)
                dst = Msl[e * 64 + f:e * 64 + f + 1, :].rearrange(
                    "p (j g) -> p j g", g=64)[:, :, f:f + L]
                E("sc", "dma_start", out=dst,
                  in_=sw.rearrange("(j e) t -> e j t", e=2)[e][None, :, 0:L],
                  allow_slow_non_contiguous=True,
                  waits=([(aini.h, 1)] if (f == 0 and e == 0) else []),
                  incs=[(msw.h, 16)])
        for j in range(NHP):
            for e in range(2):
                agm.bump(1)
                E("t", "matmul", ps[1][e * 64:e * 64 + 64, j:j + 1],
                  scr["ones_row"][0:1, 0:64], gts[0:1, 2 * j + e:2 * j + e + 1],
                  start=True, stop=True,
                  waits=([(aini.h, aini.n), (ini.h, INI)] if (j == 0 and e == 0) else []),
                  incs=[(agm.h, 1)])
        E("v", "tensor_copy", gtc[:, :], ps[1][:, 0:NHP],
          waits=[(agm.h, agm.n)], incs=[(agm.h, 1)])
        agm.bump(1)
        AGM_DONE = agm.n
        acb = E.sem("acb")
        for j in range(NHP):
            w = [(agm.h, agm.n)]
            if j == 0:
                w += [(msw.h, msw.n), (aini.h, aini.n)]
            else:
                w.append((acb.h, acb.n))
            E("v", "tensor_sub", Atmp[:, :], Wcl[:, j * 64:(j + 1) * 64],
              Msl[:, j * 64:(j + 1) * 64], waits=w, incs=[(acb.h, 1)])
            acb.bump(1)
            E("v", "tensor_scalar", Atmp[:, :], Atmp[:, :], gtc[:, j:j + 1], 0.125,
              ALU.mult, ALU.mult, waits=[(acb.h, acb.n)], incs=[(acb.h, 1)])
            acb.bump(1)
            E("v", "scalar_tensor_tensor", Asl[:, j * 64:(j + 1) * 64],
              Msl[:, j * 64:(j + 1) * 64], 0.125, Atmp[:, :],
              ALU.mult, ALU.add, waits=[(acb.h, acb.n)], incs=[(acb.h, 1)])
            acb.bump(1)
        A_DONE = acb.n



    wring = WStream(E, "p2_wring", 256, 16)

    OGG = 256
    NOG = DQ // OGG
    wqkvsb = sbt("wqkv_sb", [128, 3 * NOG * KD * OGG], BF16)
    wld = [E.sem(f"wld{i}") for i in range(3)]

    def wsb_ap(wi, ogg, k):
        return wqkvsb[:, ((wi * NOG + ogg) * KD + k) * OGG:
                      ((wi * NOG + ogg) * KD + k) * OGG + OGG]

    for wi, (wb, nm) in enumerate(((wqb, "wq"), (wkb, "wk"), (wvb, "wv"))):
        for ogg in range(NOG):
            for k in range(KD):
                wld[wi].bump(16)
                E("sy", "dma_start", out=wsb_ap(wi, ogg, k),
                  in_=wb[ogg * OGG:(ogg + 1) * OGG, k * 128:(k + 1) * 128],
                  transpose=True,
                  waits=([cast_mark[nm]] if (ogg == 0 and k == 0) else []),
                  incs=[(wld[wi].h, 16)])
    WLD = [s.n for s in wld]

    xld = [E.sem("xld0"), E.sem("xld1")]
    xrel = E.sem("xrel")
    xg_lv = {}

    def sched_x2_group(tg):
        s = xld[tg % 2]
        for k in range(KD):
            s.bump(16)
            E("g", "dma_start",
              out=x2g[:, ((tg % 2) * KD + k) * TGW:((tg % 2) * KD + k) * TGW + TGW],
              in_=x2b[k * 128:(k + 1) * 128, tg * TGW:(tg + 1) * TGW],
              waits=([cast_mark["x2"]] if tg == 0 and k == 0 else
                     ([(xrel.h, tg - 1)] if (tg >= 2 and k == 0) else [])),
              incs=[(s.h, 16)])
        xg_lv[tg] = (s.h, s.n)

    def xg_ap(tg, k):
        return x2g[:, ((tg % 2) * KD + k) * TGW:((tg % 2) * KD + k) * TGW + TGW]

    qev, kev, vev = E.sem("qev"), E.sem("kev"), E.sem("vev")
    qst = [E.sem("qst0"), E.sem("qst1")]
    pmm = E.sem("pmm")

    for tg in range(NTG):
        sched_x2_group(tg)
        for ogg in range(DQ // OGG):
            for k in range(KD):
                ap = wsb_ap(1, ogg, k)
                ldw = (wld[1].h, WLD[1])
                for m in range(2):
                    w = [ldw] if m == 0 else []
                    if k == 0 and m == 0:
                        if ogg < 2:
                            w.append((qev.h, qev.n))
                        else:
                            w.append((kev.h, kev.n - 2))
                        w.append(xg_lv[tg])
                        if l0 and tg == 0 and ogg == 0:
                            w += [(sevA.h, BIAS_DONE), (agm.h, AGM_DONE)]
                    incs = []
                    if k == KD - 1 and m == 1:
                        pmm.bump(1)
                        incs = [(pmm.h, 1)]
                    E("t", "matmul", ps[(ogg % 2) * 2 + m][:, 0:TGW],
                      ap[:, m * 128:(m + 1) * 128], xg_ap(tg, k),
                      start=(k == 0), stop=(k == KD - 1), waits=w, incs=incs)
            for m in range(2):
                r = ogg * 2 + m
                kw = [(pmm.h, pmm.n)] if m == 0 else []
                if l0:
                    if m == 0:
                        kw.append((sevA.h, BIAS_DONE))
                    E("v", "tensor_scalar_add",
                      Ksl[:, r * S + tg * TGW:r * S + tg * TGW + TGW],
                      ps[(ogg % 2) * 2 + m][:, 0:TGW], bcol["bk"][:, r:r + 1],
                      waits=kw, incs=[(kev.h, 1)])
                else:
                    E("v", "tensor_copy",
                      Ksl[:, r * S + tg * TGW:r * S + tg * TGW + TGW],
                      ps[(ogg % 2) * 2 + m][:, 0:TGW],
                      waits=kw, incs=[(kev.h, 1)])
                kev.bump(1)
        NV = cdiv(DQ, 256)
        VW = min(256, DQ)
        HPT = VW // 64
        for tt in range(TGW // 128):
            gt = tg * (TGW // 128) + tt
            for nv in range(NV):
                vbank = ps[4 + (vev.n % 2)]
                for k in range(KD):
                    ap = wsb_ap(2, nv, k)
                    w = [(wld[2].h, WLD[2])]
                    if k == 0:
                        w.append((vev.h, max(0, vev.n - 1)))
                        if l0 and tg == 0 and tt == 0 and nv == 0:
                            w.append((agm.h, AGM_DONE))
                    incs = []
                    if k == KD - 1:
                        pmm.bump(1)
                        incs = [(pmm.h, 1)]
                    E("t", "matmul", vbank[:, 0:VW],
                      xg_ap(tg, k)[:, tt * 128:(tt + 1) * 128], ap,
                      start=(k == 0), stop=(k == KD - 1), waits=w, incs=incs)
                hbase = nv * HPT
                dst = Vsl[:, gt * (H * 65) + hbase * 65:gt * (H * 65) + hbase * 65 + HPT * 65]
                E("v", "tensor_copy",
                  dst.rearrange("p (h c) -> p h c", c=65)[:, :, 0:64],
                  vbank[:, 0:VW].rearrange("p (h c) -> p h c", c=64),
                  waits=[(pmm.h, pmm.n), (ini.h, INI)], incs=[(vev.h, 1)])
                vev.bump(1)
        for tt in range(TGW // 128):
            gb = tg * (TGW // 128) + tt
            ob = gb
            for ogg in range(DQ // OGG):
                for k in range(KD):
                    ap = wsb_ap(0, ogg, k)
                    ldw = (wld[0].h, WLD[0])
                    for m in range(2):
                        w = [ldw] if m == 0 else []
                        if k == 0 and m == 0:
                            if ogg < 2:
                                w.append((kev.h, kev.n))
                            delta = 2 if (DQ // OGG) >= 2 else 0
                            w.append((qev.h, max(0, qev.n - delta)))
                        incs = []
                        if k == KD - 1 and m == 1:
                            pmm.bump(1)
                            incs = [(pmm.h, 1)]
                        E("t", "matmul", ps[(ogg % 2) * 2 + m][:, 0:128],
                          ap[:, m * 128:(m + 1) * 128],
                          xg_ap(tg, k)[:, tt * 128:(tt + 1) * 128],
                          start=(k == 0), stop=(k == KD - 1), waits=w, incs=incs)
                pb = ob % 2
                for m in range(2):
                    r = ogg * 2 + m
                    w = []
                    if m == 0:
                        w.append((pmm.h, pmm.n))
                        if ogg == 0:
                            w.append((qst[pb].h, qst[pb].n))
                    if l0:
                        if ogg == 0 and m == 0:
                            w.append((sevA.h, BIAS_DONE))
                        E("v", "tensor_scalar_add",
                          Qblk[:, (pb * KQ + r) * 128:(pb * KQ + r) * 128 + 128],
                          ps[(ogg % 2) * 2 + m][:, 0:128],
                          bcol["bq"][:, r:r + 1], waits=w, incs=[(qev.h, 1)])
                    else:
                        E("sc", "activation",
                          Qblk[:, (pb * KQ + r) * 128:(pb * KQ + r) * 128 + 128],
                          ps[(ogg % 2) * 2 + m][:, 0:128], AF.Copy, scale=0.125,
                          waits=w, incs=[(qev.h, 1)])
                    qev.bump(1)
            pb = ob % 2
            if l0:
                qpm = E.sem("qpm")
                for hd in range(H):
                    ft, fo = hd // 2, (hd % 2) * 64
                    j = hd // 2
                    if hd >= 2:
                        whd = qev.n - 1
                    else:
                        whd = max(0, qev.n - (2 * KQ + 1) + hd)
                    w = [(qev.h, qev.n if hd == 0 else whd)]
                    if ob < 2 and hd == 0:
                        w.append((acb.h, A_DONE))
                    qpm.bump(1)
                    E("t", "matmul", ps[6 + (hd % 2)][0:64, 0:128],
                      Asl[:, j * 64:(j + 1) * 64][fo:fo + 64, :],
                      Qblk[:, (pb * KQ + ft) * 128:(pb * KQ + ft) * 128 + 128][fo:fo + 64, :],
                      start=True, stop=True, waits=w, incs=[(qpm.h, 1)])
                    cw = [(qpm.h, qpm.n)]
                    if hd == 0:
                        cw.append((qst[pb].h, qst[pb].n))
                    E("v", "tensor_copy",
                      Qblk2[:, (pb * KQ + ft) * 128:(pb * KQ + ft) * 128 + 128][fo:fo + 64, :],
                      ps[6 + (hd % 2)][0:64, 0:128],
                      waits=cw, incs=[(qev.h, 1)])
                    qev.bump(1)
                qsrc = Qblk2
            else:
                qsrc = Qblk
            qst[pb].bump(16)
            E("g", "dma_start",
              out=Qdram.rearrange("(r p) t -> p r t", p=128)[:, :, ob * 128:(ob + 1) * 128],
              in_=qsrc[:, pb * KQ * 128:(pb + 1) * KQ * 128].rearrange("p (r c) -> p r c", c=128),
              waits=[(qev.h, qev.n)], incs=[(qst[pb].h, 16)])
        xrel.bump(1)
        E("v", "tensor_copy", dum[0:1, 0:1], scr["epsc"][0:1, 0:1],
          waits=[(pmm.h, pmm.n)], incs=[(xrel.h, 1)])
    KEV, VEV, QEV = kev.n, vev.n, qev.n

    p0r = E.sem("p0r")
    p0t = E.sem("p0t")
    p0c = E.sem("p0c")
    smm = E.sem("smm")
    sxp = E.sem("sxp")
    msk = E.sem("msk")
    avm = E.sem("avm")
    aev = E.sem("aev")
    rr = E.sem("rr")
    qql = E.sem("qql")
    womm = E.sem("womm")
    woev = E.sem("woev")
    x2ld = E.sem("x2ld")
    stq = E.sem("stq")
    kpw = E.sem("kpw")

    def qg_qap(hd, qg):
        ft, fo = hd // 2, (hd % 2) * 64
        return Qq[:, ft * 512:ft * 512 + 512][fo:fo + 64, :]

    for qg in range(NQG):
        w = [(qst[0].h, qst[0].n), (qst[1].h, qst[1].n)]
        if qg > 0:
            w += [(smm.h, smm.n), (p0t.h, p0t.n)]
        for k in range(KQ):
            qql.bump(16)
            E("g", "dma_start", out=Qq[:, k * 512:(k + 1) * 512],
              in_=Qdram[k * 128:(k + 1) * 128, qg * 512:(qg + 1) * 512],
              waits=(w if k == 0 else []), incs=[(qql.h, 16)])
        qqlv = qql.n
        for hd in range(H):
            ft, fo = hd // 2, (hd % 2) * 64
            for tt in range(4):
                gb = 4 * qg + tt
                kap = Ksl[:, ft * S + gb * 128:ft * S + gb * 128 + 128][fo:fo + 64, :]
                qap = Qq[:, ft * 512 + tt * 128:ft * 512 + tt * 128 + 128][fo:fo + 64, :]
                w = [(p0r.h, p0r.n)] if tt == 0 else []
                if hd == 0 and tt == 0:
                    w.append((qql.h, qqlv))
                    if qg == 0:
                        w += [(kev.h, KEV), (vev.h, VEV)]
                    else:
                        w.append((woev.h, woev.n))
                E("t", "matmul", ps[6][:, tt * 128:(tt + 1) * 128], qap, kap,
                  start=True, stop=True, waits=w,
                  incs=([(p0t.h, 1)] if tt == 3 else []))
            p0t.bump(1)
            E("v", "tensor_reduce", cmx[:, hd * 4:hd * 4 + 4],
              ps[6][:, 0:512].rearrange("p (t c) -> p t c", c=128),
              AX.X, ALU.max, waits=[(p0t.h, p0t.n)], incs=[(p0r.h, 1)])
            p0r.bump(1)
            E("v", "reduce_max", cmx[:, hd * 4:hd * 4 + 1],
              cmx[:, hd * 4:hd * 4 + 4], AX.X,
              waits=[(p0r.h, p0r.n)], incs=[(p0r.h, 1)])
            p0r.bump(1)
            E("t", "transpose", ps[7][0:1, 0:128], cmx[:, hd * 4:hd * 4 + 1],
              ident[:, :], waits=[(p0r.h, p0r.n), (ident_ini.h, 2)],
              incs=[(p0t.h, 1)])
            p0t.bump(1)
            E("v", "reduce_max", srow[0:1, 512 + hd:513 + hd],
              ps[7][0:1, 0:128], AX.X, negate=True,
              waits=[(p0t.h, p0t.n)], incs=[(p0r.h, 1)])
            p0r.bump(1)
            E("t", "matmul", ps[7][:, 128:129], scr["ones_row"][:, :],
              srow[0:1, 512 + hd:513 + hd], start=True, stop=True,
              waits=[(p0r.h, p0r.n), (ini.h, INI)], incs=[(p0t.h, 1)])
            p0t.bump(1)
            E("v", "tensor_copy", cb[:, hd:hd + 1], ps[7][:, 128:129],
              waits=[(p0t.h, p0t.n), (sxp.h, sxp.n)], incs=[(p0c.h, 1)])
            p0c.bump(1)
        womm_prev = womm.n
        for hd in range(H):
            ft, fo = hd // 2, (hd % 2) * 64
            nchunk = 4 * qg + 4
            avbank = ps[2 + (hd % 2)]
            for kb in range(nchunk):
                kap = Ksl[:, ft * S + kb * 128:ft * S + kb * 128 + 128][fo:fo + 64, :]
                sbank = ps[kb % 2]
                w = [(sxp.h, max(0, sxp.n - 1))]
                if kb == 0:
                    w.append((p0c.h, p0c.n))
                    w.append((aev.h, max(0, aev.n - 1)))
                E("t", "matmul", sbank[:, 0:512], kap, qg_qap(hd, qg),
                  start=True, stop=True, waits=w, incs=[(smm.h, 1)])
                smm.bump(1)
                r = kb - 4 * qg
                if 0 <= r <= 3:
                    t0 = r
                    E("v", "tensor_tensor",
                      sbank[:, t0 * 128:(t0 + 1) * 128],
                      sbank[:, t0 * 128:(t0 + 1) * 128], tri[:, :], ALU.add,
                      waits=[(smm.h, smm.n), (ini.h, INI)], incs=[(msk.h, 1)])
                    msk.bump(1)
                    ew = [(msk.h, msk.n)]
                else:
                    ew = [(smm.h, smm.n)]
                aslot = att[:, (kb % 2) * 512:(kb % 2) * 512 + 512]
                E("sc", "activation", aslot, sbank[:, 0:512], AF.Exp,
                  bias=cb[:, hd:hd + 1], waits=ew, incs=[(sxp.h, 1)])
                sxp.bump(1)
                nzero = min(4, r) if r >= 1 else 0
                for t0 in range(nzero):
                    E("v", "memset", aslot[:, t0 * 128:(t0 + 1) * 128], 0.0,
                      waits=[(sxp.h, sxp.n)], incs=[(msk.h, 1)])
                    msk.bump(1)
                avw = [(sxp.h, sxp.n)]
                if nzero:
                    avw.append((msk.h, msk.n))
                vap = Vsl[:, kb * (H * 65) + hd * 65:kb * (H * 65) + (hd + 1) * 65]
                E("t", "matmul", avbank[0:65, 0:512], vap, aslot,
                  start=(kb == 0), stop=(kb == nchunk - 1),
                  waits=avw, incs=[(avm.h, 1)])
                avm.bump(1)
            sslot = srow[0:1, 0:512]
            E("v", "tensor_copy", sslot, avbank[64:65, 0:512],
              waits=[(avm.h, avm.n), (p0t.h, p0t.n)], incs=[(rr.h, 1)])
            rr.bump(1)
            E("sc", "activation", sslot, sslot, AF.Ln,
              waits=[(rr.h, rr.n)], incs=[(rr.h, 1)])
            rr.bump(1)
            E("sc", "activation", sslot, sslot, AF.Exp, scale=-1.0,
              waits=[(rr.h, rr.n)], incs=[(rr.h, 1)])
            rr.bump(1)
            wb = [(rr.h, rr.n), (aev.h, max(0, aev.n - 1))]
            E("t", "matmul", ps[4 + (hd % 2)][0:64, 0:512],
              scr["ones_row"][0:1, 0:64], sslot, start=True, stop=True,
              waits=wb, incs=[(p0t.h, 1)])
            p0t.bump(1)
            E("v", "tensor_copy", rbs[0:64, (hd % 2) * 512:(hd % 2) * 512 + 512],
              ps[4 + (hd % 2)][0:64, 0:512],
              waits=[(p0t.h, p0t.n)], incs=[(rr.h, 1)])
            rr.bump(1)
            ow = [(rr.h, rr.n)]
            if qg > 0 and hd == 0:
                ow.append((womm.h, womm_prev))
            if l0:
                E("v", "tensor_tensor",
                  Osl[:, ft * 512:ft * 512 + 512][fo:fo + 64, :],
                  avbank[0:64, 0:512], rbs[0:64, (hd % 2) * 512:(hd % 2) * 512 + 512],
                  ALU.mult, waits=ow, incs=[(rr.h, 1)])
                rr.bump(1)
                E("v", "tensor_scalar_add",
                  Osl[:, ft * 512:ft * 512 + 512][fo:fo + 64, :],
                  Osl[:, ft * 512:ft * 512 + 512][fo:fo + 64, :],
                  bcol["bv"][fo:fo + 64, ft:ft + 1],
                  waits=[(rr.h, rr.n)], incs=[(aev.h, 1)])
            else:
                E("v", "tensor_tensor",
                  Osl[:, ft * 512:ft * 512 + 512][fo:fo + 64, :],
                  avbank[0:64, 0:512], rbs[0:64, (hd % 2) * 512:(hd % 2) * 512 + 512],
                  ALU.mult, waits=ow, incs=[(aev.h, 1)])
            aev.bump(1)
        def evacwo(ogg, m2, banks, mm_val, qg=qg):
            for m in range(m2):
                idx = ogg * m2 + m
                w = [(stq.h, stq.n)]
                if m == 0:
                    w.append((womm.h, mm_val))
                E("v", "tensor_copy",
                  z3[:, idx * 512:(idx + 1) * 512], banks[m][:, 0:512],
                  waits=w, incs=[(woev.h, 1)])
                woev.bump(1)

        mm_stage(E, ws=wring, nw=1,
                 wsrc_fn=lambda w, ogg, k: wob[ogg * OGG:(ogg + 1) * OGG,
                                               k * 128:(k + 1) * 128],
                 n_ogg=D // OGG, OGG=OGG, kc=KQ,
                 rhs_fn=lambda k: Osl[:, k * 512:(k + 1) * 512],
                 width=512, ps_pool=ps, bank_base=0, mm_sem=womm, ev_sem=woev,
                 evac_fn=evacwo, ev_per_ogg=2, kprog=kpw,
                 pre_waits=[(aev.h, aev.n)],
                 first_load_waits=[[cast_mark["wo"]]])
        WOE = woev.n
        for s in range(KD):
            stq.bump(16)
            E("sy", "dma_start",
              out=out[s * 128:(s + 1) * 128, qg * 512:(qg + 1) * 512],
              in_=z3[:, s * 512:(s + 1) * 512],
              waits=[(woev.h, WOE - KD + s + 1)], incs=[(stq.h, 16)])

    E.wait("sy", stq.h, stq.n)
    E.finalize()
    return E




_D, _S, _B, _H = 1024, 4096, 4, 16
_DIFF = 2730
_HID = 4096
_HB = 384


def _build_p2_prog(l0):
    DQ = _D // 2
    H2 = DQ // 64
    nc = bass.Bass("TRN2", target_bir_lowering=False)
    ap = lambda n, s: nc.dram_tensor(n, s, F32, kind="ExternalInput").ap()
    x2 = ap("x2", [_D, _S])
    wq, wk, wv = [ap(n, [DQ, _D]) for n in ("wq", "wk", "wv")]
    wo = ap("wo", [_D, DQ])
    kw = {}
    if l0:
        kw = dict(bq=ap("bq", [DQ]), bk=ap("bk", [DQ]), bv=ap("bv", [DQ]),
                  wc=ap("wc", [H2, 64, 64]), sw=ap("sw", [H2, 32]),
                  gate=ap("gate", [H2]))
    out = nc.dram_tensor("out", [_D, _S], F32, kind="ExternalOutput").ap()
    ctx = ExitStack()
    with ctx:
        emit_pd2(nc, ctx, D=_D, DQ=DQ, S=_S, x2=x2, out=out,
                 wq=wq, wk=wk, wv=wv, wo=wo, **kw)
    return nc


def _build_pb_prog():
    T = _S // 2
    nc = bass.Bass("TRN2", target_bir_lowering=False)
    ap = lambda n, s: nc.dram_tensor(n, s, F32, kind="ExternalInput").ap()
    TT = _HB + T
    x_in = ap("x_in", [_D, TT])
    wqkv = ap("wqkv", [_D, 3 * _D])
    wo = ap("wo", [_D, _D])
    w1 = ap("w1", [_D, 2 * _DIFF])
    b1 = ap("b1", [2 * _DIFF])
    w2 = ap("w2", [_DIFF, _D])
    b2 = ap("b2", [_D])
    ln_g, ln_b = ap("ln_g", [_D]), ap("ln_b", [_D])
    qs, ks = ap("qs", [64]), ap("ks", [64])
    posb = ap("posb", [1])
    lmask = ap("lmask", [TT])
    out = nc.dram_tensor("out", [_D, T], F32, kind="ExternalOutput").ap()
    ctx = ExitStack()
    with ctx:
        emit_pb(nc, ctx, D=_D, T=T, DI=_DIFF, x_in=x_in, out=out, wqkv=wqkv,
                wo=wo, w1=w1, b1=b1, w2=w2, b2=b2, ln_g=ln_g, ln_b=ln_b,
                qs=qs, ks=ks, posb=posb, lmask=lmask, SHALF=_S // 2)
    return nc


def _build_pc_prog(three, bias):
    T = _B * _S // 8
    nc = bass.Bass("TRN2", target_bir_lowering=False)
    a = nc.dram_tensor("a", [_D, T], F32, kind="ExternalInput").ap()
    b = nc.dram_tensor("b", [_D, T], F32, kind="ExternalInput").ap()
    kw = {}
    if three:
        kw["c"] = nc.dram_tensor("c", [_D, T], F32, kind="ExternalInput").ap()
    if bias:
        kw["bias"] = nc.dram_tensor("bias", [_D], F32, kind="ExternalInput").ap()
    out = nc.dram_tensor("out", [_D, T], F32, kind="ExternalOutput").ap()
    ctx = ExitStack()
    with ctx:
        emit_pc(nc, ctx, D=_D, T=T, a=a, b=b, out=out, **kw)
    return nc


def _build_pe_prog():
    T = _B * _S // 8
    nc = bass.Bass("TRN2", target_bir_lowering=False)
    ap = lambda n, s: nc.dram_tensor(n, s, F32, kind="ExternalInput").ap()
    x3 = ap("x3", [_D, T])
    w1, w1s = ap("w1", [_HID, _D]), ap("w1s", [_HID])
    w2, w2s = ap("w2", [_HID, _D]), ap("w2s", [_HID])
    h1, h1s = ap("h1", [_HID, _HID]), ap("h1s", [_HID])
    h2, h2s = ap("h2", [_HID, _HID]), ap("h2s", [_HID])
    w3, w3s = ap("w3", [_D, _HID]), ap("w3s", [_D])
    out = nc.dram_tensor("out", [_D, T], F32, kind="ExternalOutput").ap()
    ctx = ExitStack()
    with ctx:
        emit_pe(nc, ctx, D=_D, HID=_HID, T=T, TC=512, x3=x3, out=out,
                w1=w1, w1s=w1s, w2=w2, w2s=w2s, h1=h1, h1s=h1s,
                h2=h2, h2s=h2s, w3=w3, w3s=w3s)
    return nc


_PROGS = {}


def kernel(**inputs):
    from concourse.bass_utils import run_bass_kernel_spmd

    f32 = np.float32
    c = lambda a: np.ascontiguousarray(np.asarray(a, dtype=f32))
    if not _PROGS:
        _PROGS["pa"] = _build_p2_prog(True)
        _PROGS["pb"] = _build_pb_prog()
        _PROGS["pc2"] = _build_pc_prog(False, False)
        _PROGS["pc3b"] = _build_pc_prog(True, True)
        _PROGS["pc3"] = _build_pc_prog(True, False)
        _PROGS["pd"] = _build_p2_prog(False)
        _PROGS["pe"] = _build_pe_prog()

    x = c(inputs["L0_attn_x"])
    xfm = np.ascontiguousarray(x.transpose(0, 2, 1))

    DQ = _D // 2
    Tc = _B * _S // 8

    def attn_phase(key, xin_fm, wl_fn):
        feeds = []
        for b in range(_B):
            for hh in range(2):
                feeds.append(dict(x2=np.ascontiguousarray(xin_fm[b]),
                                  **wl_fn(hh)))
        res = run_bass_kernel_spmd(_PROGS[key], feeds, core_ids=list(range(8)))
        ya = np.concatenate([res.results[2 * b]["out"] for b in range(_B)], axis=1)
        yb = np.concatenate([res.results[2 * b + 1]["out"] for b in range(_B)],
                            axis=1)
        return ya, yb

    def ln_combine(key, acat, bcat, ccat=None, bias=None):
        feeds = []
        for j in range(8):
            fd = dict(a=np.ascontiguousarray(acat[:, j * Tc:(j + 1) * Tc]),
                      b=np.ascontiguousarray(bcat[:, j * Tc:(j + 1) * Tc]))
            if ccat is not None:
                fd["c"] = np.ascontiguousarray(ccat[:, j * Tc:(j + 1) * Tc])
            if bias is not None:
                fd["bias"] = bias
            feeds.append(fd)
        res = run_bass_kernel_spmd(_PROGS[key], feeds, core_ids=list(range(8)))
        cat = np.concatenate([res.results[j]["out"] for j in range(8)], axis=1)
        return np.stack([cat[:, b * _S:(b + 1) * _S] for b in range(_B)])

    def wl0(hh):
        r = slice(hh * DQ, (hh + 1) * DQ)
        h = slice(hh * _H // 2, (hh + 1) * _H // 2)
        return dict(wq=c(inputs["L0_attn_Wq"][r]), wk=c(inputs["L0_attn_Wk"][r]),
                    wv=c(inputs["L0_attn_Wv"][r]),
                    wo=c(inputs["L0_attn_Wo"][:, r]),
                    bq=c(inputs["L0_attn_bq"][r]), bk=c(inputs["L0_attn_bk"][r]),
                    bv=c(inputs["L0_attn_bv"][r]),
                    wc=c(inputs["L0_attn_W_cross"][h]),
                    sw=c(inputs["L0_attn_shift_weights"][h]),
                    gate=c(inputs["L0_attn_gate"][h]))
    ya, yb = attn_phase("pa", xfm, wl0)
    xcat = np.concatenate([xfm[b] for b in range(_B)], axis=1)
    x1 = ln_combine("pc3b", xcat, ya, yb, bias=c(inputs["L0_attn_bo"]))

    T = _S // 2
    TT = _HB + T
    xcur = x1
    for i in range(6):
        wl = dict(wqkv=c(inputs["L0_ffn_Wqkv"][i]), wo=c(inputs["L0_ffn_Wo"][i]),
                  w1=c(inputs["L0_ffn_W1"][i]), b1=c(inputs["L0_ffn_b1"][i]),
                  w2=c(inputs["L0_ffn_W2"][i]), b2=c(inputs["L0_ffn_b2"][i]),
                  ln_g=c(inputs["L0_ffn_ln_g"][i]),
                  ln_b=c(inputs["L0_ffn_ln_b"][i]),
                  qs=c(inputs["L0_ffn_q_scale"][i]),
                  ks=c(inputs["L0_ffn_k_scale"][i]))
        feeds = []
        for b in range(_B):
            for h2 in range(2):
                t0 = h2 * T
                lo = t0 - _HB
                xin = np.zeros((_D, TT), f32)
                lm = np.full(TT, -1e20, f32)
                s0 = max(0, lo)
                xin[:, s0 - lo:] = xcur[b][:, s0:t0 + T]
                lm[s0 - lo:] = 0.0
                feeds.append(dict(x_in=xin, posb=np.array([lo], f32),
                                  lmask=lm, **wl))
        res = run_bass_kernel_spmd(_PROGS["pb"], feeds, core_ids=list(range(8)))
        nxt = np.empty((_B, _D, _S), f32)
        ci = 0
        for b in range(_B):
            for h2 in range(2):
                nxt[b][:, h2 * T:(h2 + 1) * T] = res.results[ci]["out"]
                ci += 1
        xcur = nxt

    x1cat = np.concatenate([x1[b] for b in range(_B)], axis=1)
    fcat = np.concatenate([xcur[b] for b in range(_B)], axis=1)
    x2fm = ln_combine("pc2", x1cat, fcat)

    def wl1(hh):
        r = slice(hh * DQ, (hh + 1) * DQ)
        return dict(wq=c(inputs["L1_attn_wq"][r]), wk=c(inputs["L1_attn_wk"][r]),
                    wv=c(inputs["L1_attn_wv"][r]),
                    wo=c(inputs["L1_attn_wo"][:, r]))
    ya, yb = attn_phase("pd", x2fm, wl1)
    x2cat = np.concatenate([x2fm[b] for b in range(_B)], axis=1)
    x3 = ln_combine("pc3", x2cat, ya, yb)

    x3cat = np.concatenate([x3[b] for b in range(_B)], axis=1)
    wle = dict(w1=c(inputs["L1_ffn_w1_w"]), w1s=c(inputs["L1_ffn_w1_s"]),
               w2=c(inputs["L1_ffn_w2_w"]), w2s=c(inputs["L1_ffn_w2_s"]),
               h1=c(inputs["L1_ffn_h1_w"]), h1s=c(inputs["L1_ffn_h1_s"]),
               h2=c(inputs["L1_ffn_h2_w"]), h2s=c(inputs["L1_ffn_h2_s"]),
               w3=c(inputs["L1_ffn_w3_w"]), w3s=c(inputs["L1_ffn_w3_s"]))
    feeds = [dict(x3=np.ascontiguousarray(x3cat[:, j * Tc:(j + 1) * Tc]), **wle)
             for j in range(8)]
    res = run_bass_kernel_spmd(_PROGS["pe"], feeds, core_ids=list(range(8)))
    ocat = np.concatenate([res.results[j]["out"] for j in range(8)], axis=1)
    return np.ascontiguousarray(
        np.stack([ocat[:, b * _S:(b + 1) * _S].T for b in range(_B)])).astype(f32)


# version 11
# speedup vs baseline: 1.3632x; 1.0712x over previous
import numpy as np
from contextlib import ExitStack

import concourse.bass as bass
import concourse.mybir as mybir

F32 = mybir.dt.float32
BF16 = mybir.dt.bfloat16
AF = mybir.ActivationFunctionType
ALU = mybir.AluOpType
AX = mybir.AxisListType

LN_EPS = 1e-5


def cdiv(a, b):
    return (a + b - 1) // b


class Sem:

    def __init__(self, nc, ctx, name):
        self.h = ctx.enter_context(nc.semaphore(name))
        self.n = 0

    def bump(self, k=1):
        self.n += k
        return self.n


class Emit:

    def __init__(self, nc, ctx):
        self.nc = nc
        self.ctx = ctx
        self.q = {"g": [], "sy": [], "t": [], "v": [], "sc": []}
        self.sems = {}

    def sem(self, name):
        if name not in self.sems:
            self.sems[name] = Sem(self.nc, self.ctx, name)
        return self.sems[name]

    def __call__(self, eng, method, *args, waits=(), incs=(), **kw):
        def f(e, method=method, args=args, kw=kw, waits=tuple(waits), incs=tuple(incs)):
            for h, v in waits:
                if v > 0:
                    e.wait_ge(h, v)
            ins = getattr(e, method)(*args, **kw)
            for h, k in incs:
                ins.then_inc(h, k)
        self.q[eng].append(f)

    def raw(self, eng, fn):
        self.q[eng].append(fn)

    def wait(self, eng, h, v):
        if v > 0:
            self.raw(eng, lambda e, h=h, v=v: e.wait_ge(h, v))

    def finalize(self):
        block = self.ctx.enter_context(self.nc.Block())

        @block.gpsimd
        def _(g):
            for f in self.q["g"]:
                f(g)

        @block.sync
        def _(sy):
            for f in self.q["sy"]:
                f(sy)

        @block.tensor
        def _(t):
            for f in self.q["t"]:
                f(t)

        @block.vector
        def _(v):
            for f in self.q["v"]:
                f(v)

        @block.scalar
        def _(sc):
            for f in self.q["sc"]:
                f(sc)


class WStream:

    def __init__(self, E, name, free, ring=16, queue="sy"):
        self.E = E
        self.buf = E.ctx.enter_context(E.nc.sbuf_tensor(name, [128, ring * free], BF16))
        self.free = free
        self.ring = ring
        self.queues = [queue] if isinstance(queue, str) else list(queue)
        self.loaded = [E.sem(f"{name}_ld{j}") for j in range(ring)]
        self.i_load = 0
        self.last_cert = None

    def load(self, src_ap, transpose=True, waits=()):
        i = self.i_load
        self.i_load += 1
        slot = i % self.ring
        lv = self.loaded[slot].bump(16)
        if transpose:
            prows, wdt = src_ap.shape[-1], src_ap.shape[0]
        else:
            prows, wdt = src_ap.shape[0], src_ap.shape[-1]
        ap = self.buf[0:prows, slot * self.free:slot * self.free + wdt]
        self.E(self.queues[slot % len(self.queues)], "dma_start", out=ap,
               in_=src_ap, transpose=transpose,
               waits=list(waits), incs=[(self.loaded[slot].h, 16)])
        return ap, (self.loaded[slot].h, lv)


def make_identity(E, dtype=F32):
    ident = E.ctx.enter_context(E.nc.sbuf_tensor("ident", [128, 128], dtype))
    ini = E.sem("ident_ini")
    E("g", "memset", ident[:, :], 1.0, incs=[(ini.h, 1)])
    ini.bump(1)
    E("g", "affine_select", ident[:, :], ident[:, :], pattern=[[-1, 128]],
      compare_op=ALU.is_equal, fill=0.0, base=0, channel_multiplier=1,
      waits=[(ini.h, 1)], incs=[(ini.h, 1)])
    ini.bump(1)
    return ident, ini


def mm_stage(E, *, ws, nw, wsrc_fn, n_ogg, OGG, kc, rhs_fn, width,
             ps_pool, bank_base, mm_sem, ev_sem, evac_fn, pre_waits=(),
             first_load_waits=(), ev_per_ogg=None, kprog=None):
    m2 = OGG // 128
    nlive = nw * m2
    assert bank_base + 2 * nlive <= 8
    if ev_per_ogg is None:
        ev_per_ogg = nlive
    ev_base = ev_sem.n
    mm_base = mm_sem.n
    KG = 8
    ring = ws.ring
    ngroups = (kc - KG) // KG if kc > KG else 0
    if kc * nw > ring // 2:
        assert kprog is not None, "need kprog for ring reuse within an ogg"
    kp_base = kprog.n if kprog is not None else 0

    def cert(ogg_i, k_i):
        if ogg_i < 0:
            return None
        grp = k_i // KG + 1
        if grp <= ngroups:
            return (kprog.h, kp_base + ogg_i * ngroups + grp)
        return (mm_sem.h, mm_base + ogg_i + 1)

    for ogg in range(n_ogg):
        par = ogg & 1
        banks = [ps_pool[bank_base + par * nlive + j] for j in range(nlive)]
        tiles = []
        for k in range(kc):
            row = []
            for w in range(nw):
                gi = (ogg * kc + k) * nw + w
                lw = list(first_load_waits[w]) if ogg == 0 else []
                pi = gi - ring
                if pi >= 0:
                    kk = pi // nw
                    c = cert(kk // kc, kk % kc)
                    if c is not None:
                        lw.append(c)
                elif ws.last_cert is not None:
                    lw.append(ws.last_cert)
                row.append(ws.load(wsrc_fn(w, ogg, k), waits=lw))
            tiles.append(row)
        for k in range(kc):
            for w in range(nw):
                ap, ldw = tiles[k][w]
                for m in range(m2):
                    waits = []
                    if k == 0 and w == 0 and m == 0:
                        waits += list(pre_waits)
                        bank_wait = ev_base + ev_per_ogg * (ogg - 1)
                        if ogg >= 2 and bank_wait > 0:
                            waits.append((ev_sem.h, bank_wait))
                    if m == 0:
                        waits.append(ldw)
                    incs = []
                    if (kprog is not None and w == 0 and m == 0 and k >= KG
                            and k % KG == 0 and k // KG <= ngroups):
                        kprog.bump(1)
                        incs.append((kprog.h, 1))
                    if k == kc - 1 and w == nw - 1 and m == m2 - 1:
                        mm_sem.bump(1)
                        incs.append((mm_sem.h, 1))
                    E("t", "matmul", banks[w * m2 + m][:, 0:width],
                      ap[:, m * 128:(m + 1) * 128], rhs_fn(k),
                      start=(k == 0), stop=(k == kc - 1), waits=waits, incs=incs)
        evac_fn(ogg, m2, banks, mm_sem.n)
    ws.last_cert = (mm_sem.h, mm_sem.n)


def ln_fm(E, name, seg_ap, segs, width, ps_pool, bank_ids, scr, D):
    zb, sq = scr["zb"], scr["sq"]
    ones_col, ones_row, row = scr["ones_col"], scr["ones_row"], scr["row"]
    ini_sem, ini_val = scr["ini"]
    guard = list(scr.get("guard", ()))
    lnz, lnsq = E.sem(name + "_z"), E.sem(name + "_sq")
    lnst, lnv1 = E.sem(name + "_st"), E.sem(name + "_v1")
    lnsd, lnv2 = E.sem(name + "_v2a"), E.sem(name + "_v2")
    lnbc, lnfin = E.sem(name + "_bc"), E.sem(name + "_fin")
    bsum, bsq, bA, bB = [ps_pool[b] for b in bank_ids]
    prev_st = lnst.n

    for s in range(segs):
        w = guard if s == 0 else []
        if s == 0 and prev_st > 0:
            w = w + [(lnst.h, prev_st)]
        E("g", "tensor_copy", zb[:, s * width:(s + 1) * width], seg_ap(s),
          waits=w, incs=[(lnz.h, 1)])
        lnz.bump(1)
        E("sc", "activation", sq[:, s * width:(s + 1) * width],
          zb[:, s * width:(s + 1) * width], AF.Square,
          waits=[(lnz.h, lnz.n)], incs=[(lnsq.h, 1)])
        lnsq.bump(1)
    for s in range(segs):
        E("t", "matmul", bsum[0:1, 0:width], ones_col[:, :],
          zb[:, s * width:(s + 1) * width], start=(s == 0), stop=(s == segs - 1),
          waits=([(ini_sem.h, ini_val), (lnz.h, lnz.n)] if s == 0 else []))
        E("t", "matmul", bsq[0:1, 0:width], ones_col[:, :],
          sq[:, s * width:(s + 1) * width], start=(s == 0), stop=(s == segs - 1),
          waits=([(lnsq.h, lnsq.n)] if s == 0 else []),
          incs=([(lnst.h, 1)] if s == segs - 1 else []))
    lnst.bump(1)
    mean = row[:, 0:width]
    ex2 = row[:, width:2 * width]
    var = row[:, 2 * width:3 * width]
    rstd = row[:, 3 * width:4 * width]
    vc = E.sem(name + "_vc")
    b = vc.n
    E("v", "tensor_scalar_mul", mean, bsum[0:1, 0:width], 1.0 / D,
      waits=[(lnst.h, lnst.n)], incs=[(vc.h, 1)])
    vc.bump(1)
    E("v", "tensor_scalar_mul", ex2, bsq[0:1, 0:width], 1.0 / D,
      waits=[(vc.h, b + 1)], incs=[(vc.h, 1)])
    vc.bump(1)
    E("v", "tensor_tensor", var, mean, mean, ALU.mult,
      waits=[(vc.h, b + 2)], incs=[(vc.h, 1)])
    vc.bump(1)
    E("v", "tensor_sub", var, ex2, var,
      waits=[(vc.h, b + 3)], incs=[(vc.h, 1)])
    vc.bump(1)
    E("sc", "activation", rstd, var, AF.Sqrt, bias=scr["eps"],
      waits=[(vc.h, b + 4), (ini_sem.h, ini_val)], incs=[(lnsd.h, 1)])
    lnsd.bump(1)
    E("v", "reciprocal", rstd, rstd, waits=[(lnsd.h, lnsd.n)], incs=[(vc.h, 1)])
    vc.bump(1)
    E("v", "tensor_tensor", mean, mean, rstd, ALU.mult,
      waits=[(vc.h, b + 5)], incs=[(vc.h, 1)])
    vc.bump(1)
    E("v", "tensor_scalar_mul", mean, mean, -1.0,
      waits=[(vc.h, b + 6)], incs=[(vc.h, 1)])
    vc.bump(1)
    E("t", "matmul", bA[:, 0:width], ones_row[:, :], rstd, start=True, stop=True,
      waits=[(vc.h, vc.n)])
    E("t", "matmul", bB[:, 0:width], ones_row[:, :], mean, start=True, stop=True,
      incs=[(lnbc.h, 1)])
    lnbc.bump(1)
    ac = E.sem(name + "_ac")
    for s in range(segs):
        E("v", "tensor_tensor", seg_ap(s), seg_ap(s), bA[:, 0:width], ALU.mult,
          waits=([(lnbc.h, lnbc.n)] if s == 0 else []), incs=[(ac.h, 1)])
        E("v", "tensor_tensor", seg_ap(s), seg_ap(s), bB[:, 0:width], ALU.add,
          waits=[(ac.h, ac.bump(1))], incs=[(lnfin.h, 1)])
        lnfin.bump(1)
    return lnfin, lnfin.n


def emit_pe(nc, ctx, *, D, HID, T, TC, x3, out,
            w1, w1s, w2, w2s, h1, h1s, h2, h2s, w3, w3s):
    assert TC <= 512 and T % TC == 0 and D % 256 == 0 and HID % 256 == 0
    KD, KH = D // 128, HID // 128
    NTC = T // TC
    OGG = 256

    E = Emit(nc, ctx)
    sbt = lambda name, shape, dt: ctx.enter_context(nc.sbuf_tensor("pe_" + name, shape, dt))
    ps_pool = [ctx.enter_context(nc.psum_tensor(f"pe_ps{i}", [128, 512], F32))
               for i in range(8)]
    dram = lambda name, shape: nc.dram_tensor("pe_" + name, shape, BF16).ap()

    w1b, w2b = dram("w1b", [HID, D]), dram("w2b", [HID, D])
    h1b, h2b = dram("h1b", [HID, HID]), dram("h2b", [HID, HID])
    w3b = dram("w3b", [D, HID])
    x3b = dram("x3b", [D, T])

    cast_mark = {}

    def cast(name, src, dst, rpc=512):
        cs = E.sem("cast_" + name)
        for i in range(cdiv(src.shape[0], rpc)):
            r0, r1 = i * rpc, min((i + 1) * rpc, src.shape[0])
            cs.bump(16)
            E("g", "dma_start", out=dst[r0:r1], in_=src[r0:r1],
              incs=[(cs.h, 16)])
        cast_mark[name] = (cs.h, cs.n)

    cast("x3", x3, x3b, rpc=1024)
    cast("w1", w1, w1b)
    cast("w2", w2, w2b)
    cast("w3", w3, w3b)
    cast("h1", h1, h1b)
    cast("h2", h2, h2b)

    ident, ident_ini = make_identity(E)
    sld, smm, sev = E.sem("sld"), E.sem("smm"), E.sem("sev")
    scale_sb = {}
    for nm, src, n in [("s1", w1s, HID), ("s2", w2s, HID), ("hs1", h1s, HID),
                       ("hs2", h2s, HID), ("s3", w3s, D)]:
        rows = n // 128
        raw = sbt("sraw_" + nm, [rows, 128], F32)
        scale_sb[nm] = sbt("scol_" + nm, [128, rows], F32)
        sld.bump(16)
        E("g", "dma_start", out=raw[:, :],
          in_=src.rearrange("(r p) -> r p", p=128), incs=[(sld.h, 16)])
        prev_ev = sev.n
        mmv = smm.bump(1)
        E("t", "transpose", ps_pool[0][:, 0:rows], raw[:, :], ident[0:rows, 0:rows],
          waits=[(sld.h, 16 * 5), (ident_ini.h, 2), (sev.h, prev_ev)],
          incs=[(smm.h, 1)])
        sev.bump(1)
        E("v", "tensor_copy", scale_sb[nm][:, :], ps_pool[0][:, 0:rows],
          waits=[(smm.h, mmv)], incs=[(sev.h, 1)])
    scales_done = sev.n

    zbf = sbt("zbf", [128, KD * T], BF16)
    hsl = sbt("h", [128, KH * TC], BF16)
    gsl = sbt("g1", [128, KH * TC], BF16)
    z3 = sbt("z3", [128, KD * TC], F32)
    x3r = sbt("x3r", [128, 8 * TC], F32)
    tmp = sbt("tmp", [128, 4 * 512], F32)
    scr = dict(
        zb=sbt("lnzb", [128, KD * 512], BF16),
        sq=sbt("lnsq", [128, KD * 512], BF16),
        ones_col=sbt("ones_col", [128, 1], BF16),
        ones_row=sbt("ones_row", [1, 128], F32),
        row=sbt("lnrow", [1, 4 * 512], F32),
        epsc=sbt("epsc", [1, 1], F32),
    )
    dum = sbt("dum", [1, 1], F32)
    ini = E.sem("ini")
    E("g", "memset", scr["ones_col"][:, :], 1.0)
    E("g", "memset", scr["epsc"][:, :], LN_EPS)
    E("g", "memset", scr["ones_row"][:, :], 1.0, incs=[(ini.h, 1)])
    ini.bump(1)
    scr["ini"] = (ini, 1)
    scr["eps"] = scr["epsc"][0:1, 0:1]

    zld = E.sem("zld")
    for k in range(KD):
        zld.bump(16)
        E("g", "dma_start", out=zbf[:, k * T:(k + 1) * T],
          in_=x3b[k * 128:(k + 1) * 128, :],
          waits=([cast_mark["x3"]] if k == 0 else []),
          incs=[(zld.h, 16)])
    zbf_ready = zld.n

    x3ld = E.sem("x3ld")
    x3_lv = {}

    def sched_x3_loads(c):
        for k in range(KD):
            gseg = c * KD + k
            x3ld.bump(16)
            E("g", "dma_start", out=x3r[:, (gseg % 8) * TC:(gseg % 8 + 1) * TC],
              in_=x3[k * 128:(k + 1) * 128, c * TC:(c + 1) * TC],
              waits=([(ev4.h, KD * c)] if (c > 0 and k == 0) else []),
              incs=[(x3ld.h, 16)])
        for k in range(KD):
            x3_lv[c * KD + k] = x3ld.n

    wring = WStream(E, "pe_wring", OGG, 16, queue=("sy", "sc"))

    mm1, mm2, mm3, mm4 = [E.sem(f"mm{i}") for i in range(1, 5)]
    kp1, kp2, kp3, kp4 = [E.sem(f"kp{i}") for i in range(1, 5)]
    ev1, ev2, ev3, ev4 = [E.sem(f"ev{i}") for i in range(1, 5)]
    si1 = E.sem("si1")
    stq = E.sem("stq")
    lnfin_val = 0

    for c in range(NTC):
        sched_x3_loads(c)
        def evac1(ogg, m2, banks, mm_val, c=c):
            G = c * (HID // OGG) + ogg
            for m in range(m2):
                idx = ogg * m2 + m
                tslot = tmp[:, ((G % 2) * 2 + m) * 512:((G % 2) * 2 + m) * 512 + TC]
                w = [(mm1.h, mm_val)] if m == 0 else []
                if m == 0:
                    w.append((sev.h, scales_done))
                if G >= 2 and m == 0:
                    w.append((ev1.h, 2 * (G - 1)))
                E("sc", "activation", tslot, banks[m][:, 0:TC], AF.Silu,
                  scale=scale_sb["s1"][:, idx:idx + 1],
                  waits=w, incs=[(si1.h, 1)])
                si1.bump(1)
                E("v", "scalar_tensor_tensor",
                  hsl[:, idx * TC:(idx + 1) * TC], banks[m2 + m][:, 0:TC],
                  scale_sb["s2"][:, idx:idx + 1], tslot, ALU.mult, ALU.mult,
                  waits=[(si1.h, si1.n)], incs=[(ev1.h, 1)])
                ev1.bump(1)

        pre = [(zld.h, zbf_ready), (sev.h, scales_done)]
        if lnfin_val > 0:
            pre.append((E.sem("ln_fin").h, lnfin_val))
        mm_stage(E, ws=wring, nw=2,
                 wsrc_fn=lambda w, ogg, k: (w1b if w == 0 else w2b)[
                     ogg * OGG:(ogg + 1) * OGG, k * 128:(k + 1) * 128],
                 n_ogg=HID // OGG, OGG=OGG, kc=KD,
                 rhs_fn=lambda k, c=c: zbf[:, k * T + c * TC:k * T + (c + 1) * TC],
                 width=TC, ps_pool=ps_pool, bank_base=0, mm_sem=mm1, ev_sem=ev1,
                 evac_fn=evac1, pre_waits=pre, ev_per_ogg=2, kprog=kp1,
                 first_load_waits=[[cast_mark["w1"]],
                                   [cast_mark["w2"]]])

        def evac2(ogg, m2, banks, mm_val):
            for m in range(m2):
                idx = ogg * m2 + m
                E("sc", "activation", gsl[:, idx * TC:(idx + 1) * TC],
                  banks[m][:, 0:TC], AF.Gelu,
                  scale=scale_sb["hs1"][:, idx:idx + 1],
                  waits=([(mm2.h, mm_val), (sev.h, scales_done)] if m == 0 else []),
                  incs=[(ev2.h, 1)])
                ev2.bump(1)

        mm_stage(E, ws=wring, nw=1,
                 wsrc_fn=lambda w, ogg, k: h1b[ogg * OGG:(ogg + 1) * OGG,
                                               k * 128:(k + 1) * 128],
                 n_ogg=HID // OGG, OGG=OGG, kc=KH,
                 rhs_fn=lambda k: hsl[:, k * TC:(k + 1) * TC],
                 width=TC, ps_pool=ps_pool, bank_base=0, mm_sem=mm2, ev_sem=ev2,
                 evac_fn=evac2, kprog=kp2, pre_waits=[(ev1.h, ev1.n)],
                 first_load_waits=[[cast_mark["h1"]]])

        def evac3(ogg, m2, banks, mm_val):
            for m in range(m2):
                idx = ogg * m2 + m
                E("sc", "activation", hsl[:, idx * TC:(idx + 1) * TC],
                  banks[m][:, 0:TC], AF.Gelu,
                  scale=scale_sb["hs2"][:, idx:idx + 1],
                  waits=([(mm3.h, mm_val), (sev.h, scales_done)] if m == 0 else []),
                  incs=[(ev3.h, 1)])
                ev3.bump(1)

        mm_stage(E, ws=wring, nw=1,
                 wsrc_fn=lambda w, ogg, k: h2b[ogg * OGG:(ogg + 1) * OGG,
                                               k * 128:(k + 1) * 128],
                 n_ogg=HID // OGG, OGG=OGG, kc=KH,
                 rhs_fn=lambda k: gsl[:, k * TC:(k + 1) * TC],
                 width=TC, ps_pool=ps_pool, bank_base=4, mm_sem=mm3, ev_sem=ev3,
                 evac_fn=evac3, kprog=kp3, pre_waits=[(ev2.h, ev2.n)],
                 first_load_waits=[[cast_mark["h2"]]])

        def evac4(ogg, m2, banks, mm_val, c=c):
            for m in range(m2):
                idx = ogg * m2 + m
                gseg = c * KD + idx
                w = [(x3ld.h, x3_lv[gseg])]
                if m == 0:
                    w.append((mm4.h, mm_val))
                    w.append((sev.h, scales_done))
                if c > 0:
                    w.append((stq.h, 16 * KD * c))
                E("v", "scalar_tensor_tensor",
                  z3[:, idx * TC:(idx + 1) * TC], banks[m][:, 0:TC],
                  scale_sb["s3"][:, idx:idx + 1],
                  x3r[:, (gseg % 8) * TC:(gseg % 8) * TC + TC],
                  ALU.mult, ALU.add,
                  waits=w, incs=[(ev4.h, 1)])
                ev4.bump(1)

        mm_stage(E, ws=wring, nw=1,
                 wsrc_fn=lambda w, ogg, k: w3b[ogg * OGG:(ogg + 1) * OGG,
                                               k * 128:(k + 1) * 128],
                 n_ogg=D // OGG, OGG=OGG, kc=KH,
                 rhs_fn=lambda k: hsl[:, k * TC:(k + 1) * TC],
                 width=TC, ps_pool=ps_pool, bank_base=0, mm_sem=mm4, ev_sem=ev4,
                 evac_fn=evac4, kprog=kp4, pre_waits=[(ev3.h, ev3.n)],
                 first_load_waits=[[cast_mark["w3"]]])

        scr["guard"] = [(ev4.h, ev4.n)]
        lnfin, lnfin_val = ln_fm(
            E, "ln", lambda s: z3[:, s * TC:(s + 1) * TC], KD, TC,
            ps_pool, [4, 5, 6, 7], scr, D)
        for s in range(KD):
            stq.bump(16)
            E("sy", "dma_start", out=out[s * 128:(s + 1) * 128, c * TC:(c + 1) * TC],
              in_=z3[:, s * TC:(s + 1) * TC],
              waits=[(lnfin.h, lnfin_val - KD + s + 1)], incs=[(stq.h, 16)])

    E.wait("sy", stq.h, stq.n)
    E.finalize()
    return E


def emit_pd(nc, ctx, *, D, S, par, x2, out, wq, wk, wv, wo,
            bq=None, bk=None, bv=None, bo=None, wc=None, sw=None, gate=None):
    l0 = gate is not None
    HD = 64
    H = D // HD
    KD = D // 128
    NB = S // 128
    NOWN = NB // 2
    TOWN = NOWN * 128
    NQG = TOWN // 512
    TGW = 256
    NTG = S // TGW

    E = Emit(nc, ctx)
    sbt = lambda name, shape, dt: ctx.enter_context(nc.sbuf_tensor("pd_" + name, shape, dt))
    ps = [ctx.enter_context(nc.psum_tensor(f"pd_ps{i}", [128, 512], F32)) for i in range(8)]
    dram = lambda name, shape: nc.dram_tensor("pd_" + name, shape, BF16).ap()

    x2b = dram("x2b", [D, S])
    wqb, wkb = dram("wqb", [D, D]), dram("wkb", [D, D])
    wvb, wob = dram("wvb", [D, D]), dram("wob", [D, D])

    cast_mark = {}

    def cast(name, src, dst, rpc=512):
        cs = E.sem("cast_" + name)
        for i in range(cdiv(src.shape[0], rpc)):
            r0, r1 = i * rpc, min((i + 1) * rpc, src.shape[0])
            cs.bump(16)
            E("g", "dma_start", out=dst[r0:r1], in_=src[r0:r1], incs=[(cs.h, 16)])
        cast_mark[name] = (cs.h, cs.n)

    cast("x2", x2, x2b, rpc=1024)
    cast("wq", wq, wqb)
    cast("wk", wk, wkb)
    cast("wv", wv, wvb)
    cast("wo", wo, wob)

    ident, ident_ini = make_identity(E)

    Ksl = sbt("K", [128, KD * S], BF16)
    Vsl = sbt("V", [128, (S // 128) * (H * 65)], BF16)
    Qq = sbt("Qq", [128, KD * 512], BF16)
    Qblk = sbt("Qblk", [128, 2 * KD * 128], BF16)
    Qblk2 = sbt("Qblk2", [128, 2 * KD * 128], BF16)
    Osl = sbt("O", [128, KD * 512], BF16)
    z3 = sbt("z3", [128, KD * 512], F32)
    x2g = sbt("x2g", [128, max(KD * TGW * 2, 2 * KD * 256)], BF16)
    att = sbt("att", [128, 2 * 512], BF16)
    rbs = sbt("rbs", [128, 2 * 512], F32)
    tri = sbt("tri", [128, 128], F32)
    cb = sbt("cb", [128, H], F32)
    cmx = sbt("cmx", [128, 4 * H], F32)
    srow = sbt("srow", [1, 512 + 16], F32)
    scr = dict(
        zb=x2g[:, 0:KD * 256],
        sq=x2g[:, KD * 256:2 * KD * 256],
        ones_col=sbt("ones_col", [128, 1], BF16),
        ones_row=sbt("ones_row", [1, 128], F32),
        row=sbt("lnrow", [1, 4 * 256], F32),
        epsc=sbt("epsc", [1, 1], F32),
    )
    dum = sbt("dum", [1, 1], F32)
    Qdram = nc.dram_tensor("pd_Qdram", [D, TOWN], BF16).ap()
    ini = E.sem("ini")
    E("g", "memset", scr["ones_col"][:, :], 1.0)
    E("g", "memset", scr["epsc"][:, :], LN_EPS)
    E("g", "memset", Vsl[:, :], 1.0, incs=[(ini.h, 1)])
    ini.bump(1)
    E("g", "memset", scr["ones_row"][:, :], 1.0, incs=[(ini.h, 1)])
    ini.bump(1)
    E("g", "memset", tri[:, :], 0.0, incs=[(ini.h, 1)])
    ini.bump(1)
    E("g", "affine_select", tri[:, :], tri[:, :], pattern=[[1, 128]],
      compare_op=ALU.is_ge, fill=-1e30, base=0, channel_multiplier=-1,
      waits=[(ini.h, 3)], incs=[(ini.h, 1)])
    ini.bump(1)
    scr["ini"] = (ini, ini.n)
    scr["eps"] = scr["epsc"][0:1, 0:1]
    INI = ini.n

    bcol = {}
    if l0:
        sldA, smmA, sevA = E.sem("sldA"), E.sem("smmA"), E.sem("sevA")
        brawls = {}
        for nm, bsrc in [("bq", bq), ("bk", bk), ("bv", bv), ("bo", bo)]:
            rows = D // 128
            braw = ctx.enter_context(nc.sbuf_tensor("pa_braw_" + nm, [rows, 128], F32))
            bcol[nm] = ctx.enter_context(nc.sbuf_tensor("pa_bcol_" + nm, [128, rows], F32))
            sldA.bump(16)
            E("g", "dma_start", out=braw[:, :],
              in_=bsrc.rearrange("(r p) -> r p", p=128), incs=[(sldA.h, 16)])
            brawls[nm] = braw
        bn = 0
        for nm in ("bq", "bk", "bv", "bo"):
            rows = D // 128
            prev = sevA.n
            smmA.bump(1)
            E("t", "transpose", ps[0][:, 0:rows], brawls[nm][:, :],
              ident[0:rows, 0:rows],
              waits=[(sldA.h, 16 * 4), (ident_ini.h, 2), (sevA.h, prev)],
              incs=[(smmA.h, 1)])
            sevA.bump(1)
            E("v", "tensor_copy", bcol[nm][:, :], ps[0][:, 0:rows],
              waits=[(smmA.h, smmA.n)], incs=[(sevA.h, 1)])
        BIAS_DONE = sevA.n
        NHP = H // 2
        Asl = ctx.enter_context(nc.sbuf_tensor("pa_A", [128, NHP * 64], BF16))
        Msl = ctx.enter_context(nc.sbuf_tensor("pa_M", [128, NHP * 64], F32))
        Wcl = ctx.enter_context(nc.sbuf_tensor("pa_Wc", [128, NHP * 64], F32))
        gts = ctx.enter_context(nc.sbuf_tensor("pa_gt", [1, H], F32))
        gtc = ctx.enter_context(nc.sbuf_tensor("pa_gtc", [128, NHP], F32))
        Atmp = ctx.enter_context(nc.sbuf_tensor("pa_At", [128, 64], F32))
        aini = E.sem("aini")
        adma = E.sem("adma")
        agm = E.sem("agm")
        E("g", "memset", Msl[:, :], 0.0, incs=[(aini.h, 1)])
        aini.bump(1)
        for e in range(2):
            adma.bump(16)
            E("g", "dma_start",
              out=Wcl[e * 64:(e + 1) * 64, :].rearrange("f (j g) -> f j g", g=64),
              in_=wc.rearrange("(j e) f g -> e f j g", e=2)[e],
              incs=[(adma.h, 16)])
        adma.bump(16)
        E("g", "dma_start", out=gts[:, :], in_=gate[None, :], incs=[(adma.h, 16)])
        E("sc", "activation", gts[:, :], gts[:, :], AF.Sigmoid,
          waits=[(adma.h, adma.n)], incs=[(aini.h, 1)])
        aini.bump(1)
        msw = E.sem("msw")
        MS = sw.shape[1]
        for f in range(64):
            L = min(MS, 64 - f)
            for e in range(2):
                msw.bump(16)
                dst = Msl[e * 64 + f:e * 64 + f + 1, :].rearrange(
                    "p (j g) -> p j g", g=64)[:, :, f:f + L]
                E("sc", "dma_start", out=dst,
                  in_=sw.rearrange("(j e) t -> e j t", e=2)[e][None, :, 0:L],
                  allow_slow_non_contiguous=True,
                  waits=([(aini.h, 1)] if (f == 0 and e == 0) else []),
                  incs=[(msw.h, 16)])
        for j in range(NHP):
            for e in range(2):
                agm.bump(1)
                E("t", "matmul", ps[1][e * 64:e * 64 + 64, j:j + 1],
                  scr["ones_row"][0:1, 0:64], gts[0:1, 2 * j + e:2 * j + e + 1],
                  start=True, stop=True,
                  waits=([(aini.h, aini.n), (ini.h, INI)] if (j == 0 and e == 0) else []),
                  incs=[(agm.h, 1)])
        E("v", "tensor_copy", gtc[:, :], ps[1][:, 0:NHP],
          waits=[(agm.h, agm.n)], incs=[(agm.h, 1)])
        agm.bump(1)
        AGM_DONE = agm.n
        acb = E.sem("acb")
        for j in range(NHP):
            w = [(agm.h, agm.n)]
            if j == 0:
                w += [(msw.h, msw.n), (aini.h, aini.n)]
            else:
                w.append((acb.h, acb.n))
            E("v", "tensor_sub", Atmp[:, :], Wcl[:, j * 64:(j + 1) * 64],
              Msl[:, j * 64:(j + 1) * 64], waits=w, incs=[(acb.h, 1)])
            acb.bump(1)
            E("v", "tensor_scalar", Atmp[:, :], Atmp[:, :], gtc[:, j:j + 1], 0.125,
              ALU.mult, ALU.mult, waits=[(acb.h, acb.n)], incs=[(acb.h, 1)])
            acb.bump(1)
            E("v", "scalar_tensor_tensor", Asl[:, j * 64:(j + 1) * 64],
              Msl[:, j * 64:(j + 1) * 64], 0.125, Atmp[:, :],
              ALU.mult, ALU.add, waits=[(acb.h, acb.n)], incs=[(acb.h, 1)])
            acb.bump(1)
        A_DONE = acb.n



    wring = WStream(E, "pd_wring", 256, 16)

    xld = [E.sem("xld0"), E.sem("xld1")]
    xrel = E.sem("xrel")
    xg_lv = {}

    def sched_x2_group(tg):
        s = xld[tg % 2]
        for k in range(KD):
            s.bump(16)
            E("g", "dma_start",
              out=x2g[:, ((tg % 2) * KD + k) * TGW:((tg % 2) * KD + k) * TGW + TGW],
              in_=x2b[k * 128:(k + 1) * 128, tg * TGW:(tg + 1) * TGW],
              waits=([cast_mark["x2"]] if tg == 0 and k == 0 else
                     ([(xrel.h, tg - 1)] if (tg >= 2 and k == 0) else [])),
              incs=[(s.h, 16)])
        xg_lv[tg] = (s.h, s.n)

    def xg_ap(tg, k):
        return x2g[:, ((tg % 2) * KD + k) * TGW:((tg % 2) * KD + k) * TGW + TGW]

    qev, kev, vev = E.sem("qev"), E.sem("kev"), E.sem("vev")
    qst = [E.sem("qst0"), E.sem("qst1")]
    pmm = E.sem("pmm")
    OGG = 256

    for tg in range(NTG):
        sched_x2_group(tg)
        for ogg in range(D // OGG):
            for k in range(KD):
                lw = [cast_mark["wk"]] if (tg == 0 and ogg == 0 and k == 0) else []
                if wring.last_cert is not None:
                    lw.append(wring.last_cert)
                ap, ldw = wring.load(
                    wkb[ogg * OGG:(ogg + 1) * OGG, k * 128:(k + 1) * 128], waits=lw)
                for m in range(2):
                    w = [ldw] if m == 0 else []
                    if k == 0 and m == 0:
                        if ogg < 2:
                            w.append((qev.h, qev.n))
                        else:
                            w.append((kev.h, kev.n - 2))
                        w.append(xg_lv[tg])
                        if l0 and tg == 0 and ogg == 0:
                            w += [(sevA.h, BIAS_DONE), (agm.h, AGM_DONE)]
                    incs = []
                    if k == KD - 1 and m == 1:
                        pmm.bump(1)
                        incs = [(pmm.h, 1)]
                    E("t", "matmul", ps[(ogg % 2) * 2 + m][:, 0:TGW],
                      ap[:, m * 128:(m + 1) * 128], xg_ap(tg, k),
                      start=(k == 0), stop=(k == KD - 1), waits=w, incs=incs)
            wring.last_cert = (pmm.h, pmm.n)
            for m in range(2):
                r = ogg * 2 + m
                kw = [(pmm.h, pmm.n)] if m == 0 else []
                if l0:
                    if m == 0:
                        kw.append((sevA.h, BIAS_DONE))
                    E("v", "tensor_scalar_add",
                      Ksl[:, r * S + tg * TGW:r * S + tg * TGW + TGW],
                      ps[(ogg % 2) * 2 + m][:, 0:TGW], bcol["bk"][:, r:r + 1],
                      waits=kw, incs=[(kev.h, 1)])
                else:
                    E("v", "tensor_copy",
                      Ksl[:, r * S + tg * TGW:r * S + tg * TGW + TGW],
                      ps[(ogg % 2) * 2 + m][:, 0:TGW],
                      waits=kw, incs=[(kev.h, 1)])
                kev.bump(1)
        NV = cdiv(D, 256)
        VW = min(256, D)
        HPT = VW // 64
        for tt in range(TGW // 128):
            gt = tg * (TGW // 128) + tt
            for nv in range(NV):
                vbank = ps[4 + (vev.n % 2)]
                for k in range(KD):
                    lw = [cast_mark["wv"]] if (tg == 0 and tt == 0 and nv == 0 and k == 0) else []
                    lw.append(wring.last_cert)
                    ap, ldw = wring.load(
                        wvb[nv * VW:(nv + 1) * VW, k * 128:(k + 1) * 128], waits=lw)
                    w = [ldw]
                    if k == 0:
                        w.append((vev.h, max(0, vev.n - 1)))
                        if l0 and tg == 0 and tt == 0 and nv == 0:
                            w.append((agm.h, AGM_DONE))
                    incs = []
                    if k == KD - 1:
                        pmm.bump(1)
                        incs = [(pmm.h, 1)]
                    E("t", "matmul", vbank[:, 0:VW],
                      xg_ap(tg, k)[:, tt * 128:(tt + 1) * 128], ap,
                      start=(k == 0), stop=(k == KD - 1), waits=w, incs=incs)
                wring.last_cert = (pmm.h, pmm.n)
                hbase = nv * HPT
                dst = Vsl[:, gt * (H * 65) + hbase * 65:gt * (H * 65) + hbase * 65 + HPT * 65]
                E("v", "tensor_copy",
                  dst.rearrange("p (h c) -> p h c", c=65)[:, :, 0:64],
                  vbank[:, 0:VW].rearrange("p (h c) -> p h c", c=64),
                  waits=[(pmm.h, pmm.n), (ini.h, INI)], incs=[(vev.h, 1)])
                vev.bump(1)
        for tt in range(TGW // 128):
            gb = tg * (TGW // 128) + tt
            if gb % 2 != par:
                continue
            ob = gb // 2
            for ogg in range(D // OGG):
                for k in range(KD):
                    lw = [cast_mark["wq"]] if (ogg == 0 and k == 0 and ob == 0) else []
                    lw.append(wring.last_cert)
                    ap, ldw = wring.load(
                        wqb[ogg * OGG:(ogg + 1) * OGG, k * 128:(k + 1) * 128], waits=lw)
                    for m in range(2):
                        w = [ldw] if m == 0 else []
                        if k == 0 and m == 0:
                            if ogg < 2:
                                w.append((kev.h, kev.n))
                            delta = 2 if (D // OGG) >= 2 else 0
                            w.append((qev.h, max(0, qev.n - delta)))
                        incs = []
                        if k == KD - 1 and m == 1:
                            pmm.bump(1)
                            incs = [(pmm.h, 1)]
                        E("t", "matmul", ps[(ogg % 2) * 2 + m][:, 0:128],
                          ap[:, m * 128:(m + 1) * 128],
                          xg_ap(tg, k)[:, tt * 128:(tt + 1) * 128],
                          start=(k == 0), stop=(k == KD - 1), waits=w, incs=incs)
                wring.last_cert = (pmm.h, pmm.n)
                pb = ob % 2
                for m in range(2):
                    r = ogg * 2 + m
                    w = []
                    if m == 0:
                        w.append((pmm.h, pmm.n))
                        if ogg == 0:
                            w.append((qst[pb].h, qst[pb].n))
                    if l0:
                        if ogg == 0 and m == 0:
                            w.append((sevA.h, BIAS_DONE))
                        E("v", "tensor_scalar_add",
                          Qblk[:, (pb * KD + r) * 128:(pb * KD + r) * 128 + 128],
                          ps[(ogg % 2) * 2 + m][:, 0:128],
                          bcol["bq"][:, r:r + 1], waits=w, incs=[(qev.h, 1)])
                    else:
                        E("sc", "activation",
                          Qblk[:, (pb * KD + r) * 128:(pb * KD + r) * 128 + 128],
                          ps[(ogg % 2) * 2 + m][:, 0:128], AF.Copy, scale=0.125,
                          waits=w, incs=[(qev.h, 1)])
                    qev.bump(1)
            pb = ob % 2
            if l0:
                qpm = E.sem("qpm")
                for hd in range(H):
                    ft, fo = hd // 2, (hd % 2) * 64
                    j = hd // 2
                    if hd >= 2:
                        whd = qev.n - 1
                    else:
                        whd = max(0, qev.n - 9 + hd)
                    w = [(qev.h, qev.n if hd == 0 else whd)]
                    if ob < 2 and hd == 0:
                        w.append((acb.h, A_DONE))
                    qpm.bump(1)
                    E("t", "matmul", ps[6 + (hd % 2)][0:64, 0:128],
                      Asl[:, j * 64:(j + 1) * 64][fo:fo + 64, :],
                      Qblk[:, (pb * KD + ft) * 128:(pb * KD + ft) * 128 + 128][fo:fo + 64, :],
                      start=True, stop=True, waits=w, incs=[(qpm.h, 1)])
                    cw = [(qpm.h, qpm.n)]
                    if hd == 0:
                        cw.append((qst[pb].h, qst[pb].n))
                    E("v", "tensor_copy",
                      Qblk2[:, (pb * KD + ft) * 128:(pb * KD + ft) * 128 + 128][fo:fo + 64, :],
                      ps[6 + (hd % 2)][0:64, 0:128],
                      waits=cw, incs=[(qev.h, 1)])
                    qev.bump(1)
                qsrc = Qblk2
            else:
                qsrc = Qblk
            qst[pb].bump(16)
            E("g", "dma_start",
              out=Qdram.rearrange("(r p) t -> p r t", p=128)[:, :, ob * 128:(ob + 1) * 128],
              in_=qsrc[:, pb * KD * 128:(pb + 1) * KD * 128].rearrange("p (r c) -> p r c", c=128),
              waits=[(qev.h, qev.n)], incs=[(qst[pb].h, 16)])
        xrel.bump(1)
        E("v", "tensor_copy", dum[0:1, 0:1], scr["epsc"][0:1, 0:1],
          waits=[(pmm.h, pmm.n)], incs=[(xrel.h, 1)])
    KEV, VEV, QEV = kev.n, vev.n, qev.n

    p0r = E.sem("p0r")
    p0t = E.sem("p0t")
    p0c = E.sem("p0c")
    smm = E.sem("smm")
    sxp = E.sem("sxp")
    msk = E.sem("msk")
    avm = E.sem("avm")
    aev = E.sem("aev")
    rr = E.sem("rr")
    qql = E.sem("qql")
    womm = E.sem("womm")
    woev = E.sem("woev")
    x2ld = E.sem("x2ld")
    stq = E.sem("stq")
    kpw = E.sem("kpw")
    lnstS = E.sem("ln_st")
    lnfinS = E.sem("ln_fin")

    def qg_qap(hd, qg):
        ft, fo = hd // 2, (hd % 2) * 64
        return Qq[:, ft * 512:ft * 512 + 512][fo:fo + 64, :]

    lnfin_val = 0
    for qg in range(NQG):
        w = [(qst[0].h, qst[0].n), (qst[1].h, qst[1].n)]
        if qg > 0:
            w += [(smm.h, smm.n), (p0t.h, p0t.n)]
        for k in range(KD):
            qql.bump(16)
            E("g", "dma_start", out=Qq[:, k * 512:(k + 1) * 512],
              in_=Qdram[k * 128:(k + 1) * 128, qg * 512:(qg + 1) * 512],
              waits=(w if k == 0 else []), incs=[(qql.h, 16)])
        qqlv = qql.n
        for hd in range(H):
            ft, fo = hd // 2, (hd % 2) * 64
            for tt in range(4):
                gb = 8 * qg + 2 * tt + par
                kap = Ksl[:, ft * S + gb * 128:ft * S + gb * 128 + 128][fo:fo + 64, :]
                qap = Qq[:, ft * 512 + tt * 128:ft * 512 + tt * 128 + 128][fo:fo + 64, :]
                w = [(p0r.h, p0r.n)] if tt == 0 else []
                if hd == 0 and tt == 0:
                    w.append((qql.h, qqlv))
                    if qg == 0:
                        w += [(kev.h, KEV), (vev.h, VEV)]
                    else:
                        w.append((lnfinS.h, lnfin_val))
                E("t", "matmul", ps[6][:, tt * 128:(tt + 1) * 128], qap, kap,
                  start=True, stop=True, waits=w,
                  incs=([(p0t.h, 1)] if tt == 3 else []))
            p0t.bump(1)
            E("v", "tensor_reduce", cmx[:, hd * 4:hd * 4 + 4],
              ps[6][:, 0:512].rearrange("p (t c) -> p t c", c=128),
              AX.X, ALU.max, waits=[(p0t.h, p0t.n)], incs=[(p0r.h, 1)])
            p0r.bump(1)
            E("v", "reduce_max", cmx[:, hd * 4:hd * 4 + 1],
              cmx[:, hd * 4:hd * 4 + 4], AX.X,
              waits=[(p0r.h, p0r.n)], incs=[(p0r.h, 1)])
            p0r.bump(1)
            E("t", "transpose", ps[7][0:1, 0:128], cmx[:, hd * 4:hd * 4 + 1],
              ident[:, :], waits=[(p0r.h, p0r.n), (ident_ini.h, 2)],
              incs=[(p0t.h, 1)])
            p0t.bump(1)
            E("v", "reduce_max", srow[0:1, 512 + hd:513 + hd],
              ps[7][0:1, 0:128], AX.X, negate=True,
              waits=[(p0t.h, p0t.n)], incs=[(p0r.h, 1)])
            p0r.bump(1)
            E("t", "matmul", ps[7][:, 128:129], scr["ones_row"][:, :],
              srow[0:1, 512 + hd:513 + hd], start=True, stop=True,
              waits=[(p0r.h, p0r.n), (ini.h, INI)], incs=[(p0t.h, 1)])
            p0t.bump(1)
            E("v", "tensor_copy", cb[:, hd:hd + 1], ps[7][:, 128:129],
              waits=[(p0t.h, p0t.n), (sxp.h, sxp.n)], incs=[(p0c.h, 1)])
            p0c.bump(1)
        womm_prev = womm.n
        for hd in range(H):
            ft, fo = hd // 2, (hd % 2) * 64
            nchunk = 8 * qg + 7 + par
            avbank = ps[2 + (hd % 2)]
            for kb in range(nchunk):
                kap = Ksl[:, ft * S + kb * 128:ft * S + kb * 128 + 128][fo:fo + 64, :]
                sbank = ps[kb % 2]
                w = [(sxp.h, max(0, sxp.n - 1))]
                if kb == 0:
                    w.append((p0c.h, p0c.n))
                    w.append((aev.h, max(0, aev.n - 1)))
                E("t", "matmul", sbank[:, 0:512], kap, qg_qap(hd, qg),
                  start=True, stop=True, waits=w, incs=[(smm.h, 1)])
                smm.bump(1)
                r = kb - (8 * qg + par)
                if 0 <= r <= 6 and r % 2 == 0:
                    t0 = r // 2
                    E("v", "tensor_tensor",
                      sbank[:, t0 * 128:(t0 + 1) * 128],
                      sbank[:, t0 * 128:(t0 + 1) * 128], tri[:, :], ALU.add,
                      waits=[(smm.h, smm.n), (ini.h, INI)], incs=[(msk.h, 1)])
                    msk.bump(1)
                    ew = [(msk.h, msk.n)]
                else:
                    ew = [(smm.h, smm.n)]
                aslot = att[:, (kb % 2) * 512:(kb % 2) * 512 + 512]
                E("sc", "activation", aslot, sbank[:, 0:512], AF.Exp,
                  bias=cb[:, hd:hd + 1], waits=ew, incs=[(sxp.h, 1)])
                sxp.bump(1)
                nzero = min(4, (r + 1) // 2) if r >= 1 else 0
                for t0 in range(nzero):
                    E("v", "memset", aslot[:, t0 * 128:(t0 + 1) * 128], 0.0,
                      waits=[(sxp.h, sxp.n)], incs=[(msk.h, 1)])
                    msk.bump(1)
                avw = [(sxp.h, sxp.n)]
                if nzero:
                    avw.append((msk.h, msk.n))
                vap = Vsl[:, kb * (H * 65) + hd * 65:kb * (H * 65) + (hd + 1) * 65]
                E("t", "matmul", avbank[0:65, 0:512], vap, aslot,
                  start=(kb == 0), stop=(kb == nchunk - 1),
                  waits=avw, incs=[(avm.h, 1)])
                avm.bump(1)
            sslot = srow[0:1, 0:512]
            E("v", "tensor_copy", sslot, avbank[64:65, 0:512],
              waits=[(avm.h, avm.n), (p0t.h, p0t.n)], incs=[(rr.h, 1)])
            rr.bump(1)
            E("sc", "activation", sslot, sslot, AF.Ln,
              waits=[(rr.h, rr.n)], incs=[(rr.h, 1)])
            rr.bump(1)
            E("sc", "activation", sslot, sslot, AF.Exp, scale=-1.0,
              waits=[(rr.h, rr.n)], incs=[(rr.h, 1)])
            rr.bump(1)
            wb = [(rr.h, rr.n), (aev.h, max(0, aev.n - 1)), (lnstS.h, lnstS.n)]
            E("t", "matmul", ps[4 + (hd % 2)][0:64, 0:512],
              scr["ones_row"][0:1, 0:64], sslot, start=True, stop=True,
              waits=wb, incs=[(p0t.h, 1)])
            p0t.bump(1)
            E("v", "tensor_copy", rbs[0:64, (hd % 2) * 512:(hd % 2) * 512 + 512],
              ps[4 + (hd % 2)][0:64, 0:512],
              waits=[(p0t.h, p0t.n)], incs=[(rr.h, 1)])
            rr.bump(1)
            ow = [(rr.h, rr.n)]
            if qg > 0 and hd == 0:
                ow.append((womm.h, womm_prev))
            if l0:
                E("v", "tensor_tensor",
                  Osl[:, ft * 512:ft * 512 + 512][fo:fo + 64, :],
                  avbank[0:64, 0:512], rbs[0:64, (hd % 2) * 512:(hd % 2) * 512 + 512],
                  ALU.mult, waits=ow, incs=[(rr.h, 1)])
                rr.bump(1)
                E("v", "tensor_scalar_add",
                  Osl[:, ft * 512:ft * 512 + 512][fo:fo + 64, :],
                  Osl[:, ft * 512:ft * 512 + 512][fo:fo + 64, :],
                  bcol["bv"][fo:fo + 64, ft:ft + 1],
                  waits=[(rr.h, rr.n)], incs=[(aev.h, 1)])
            else:
                E("v", "tensor_tensor",
                  Osl[:, ft * 512:ft * 512 + 512][fo:fo + 64, :],
                  avbank[0:64, 0:512], rbs[0:64, (hd % 2) * 512:(hd % 2) * 512 + 512],
                  ALU.mult, waits=ow, incs=[(aev.h, 1)])
            aev.bump(1)
        for k in range(KD):
            x2ld.bump(16)
            lw = []
            if k == 0:
                lw.append((aev.h, aev.n))
                lw.append((stq.h, stq.n))
            srcap = x2[k * 128:(k + 1) * 128, :].rearrange(
                "p (b c) -> p b c", c=128)[:, par::2, :][:, 4 * qg:4 * qg + 4, :]
            E("g", "dma_start", out=z3[:, k * 512:(k + 1) * 512],
              in_=srcap, waits=lw, incs=[(x2ld.h, 16)])
        x2lv = x2ld.n

        def evacwo(ogg, m2, banks, mm_val, qg=qg, x2lv=x2lv):
            for m in range(m2):
                idx = ogg * m2 + m
                w = [(x2ld.h, x2lv)]
                if m == 0:
                    w.append((womm.h, mm_val))
                if l0:
                    E("v", "scalar_tensor_tensor",
                      z3[:, idx * 512:(idx + 1) * 512], banks[m][:, 0:512],
                      bcol["bo"][:, idx:idx + 1],
                      z3[:, idx * 512:(idx + 1) * 512],
                      ALU.add, ALU.add, waits=w, incs=[(woev.h, 1)])
                else:
                    E("v", "tensor_add",
                      z3[:, idx * 512:(idx + 1) * 512], banks[m][:, 0:512],
                      z3[:, idx * 512:(idx + 1) * 512],
                      waits=w, incs=[(woev.h, 1)])
                woev.bump(1)

        mm_stage(E, ws=wring, nw=1,
                 wsrc_fn=lambda w, ogg, k: wob[ogg * OGG:(ogg + 1) * OGG,
                                               k * 128:(k + 1) * 128],
                 n_ogg=D // OGG, OGG=OGG, kc=KD,
                 rhs_fn=lambda k: Osl[:, k * 512:(k + 1) * 512],
                 width=512, ps_pool=ps, bank_base=0, mm_sem=womm, ev_sem=woev,
                 evac_fn=evacwo, ev_per_ogg=2, kprog=kpw,
                 pre_waits=[(aev.h, aev.n)],
                 first_load_waits=[[cast_mark["wo"]]])

        for half in range(2):
            scr["guard"] = [(woev.h, woev.n)]
            lnfin, lnfin_val = ln_fm(
                E, "ln", lambda s, half=half: z3[:, s * 512 + half * 256:s * 512 + half * 256 + 256],
                KD, 256, ps, [4, 5, 6, 7], scr, D)
            for s in range(KD):
                stq.bump(16)
                E("sy", "dma_start",
                  out=out[s * 128:(s + 1) * 128, qg * 512 + half * 256:qg * 512 + half * 256 + 256],
                  in_=z3[:, s * 512 + half * 256:s * 512 + half * 256 + 256],
                  waits=[(lnfin.h, lnfin_val - KD + s + 1)], incs=[(stq.h, 16)])

    E.wait("sy", stq.h, stq.n)
    E.finalize()
    return E


def emit_pb(nc, ctx, *, D, T, DI, x_in, out, wqkv, wo, w1, b1, w2, b2,
            ln_g, ln_b, qs, ks, posb, lmask, SHALF=2048):
    HD = 64
    H = D // HD
    KD = D // 128
    HB = 384
    TT = HB + T
    NTT = TT // 128
    XTS = 2433 if T == 2048 else (TT + 1)
    W = 256
    DI2 = 2 * DI
    NAT = cdiv(DI, 128)
    LK = DI - (NAT - 1) * 128
    TC = 512
    NCH = T // TC
    QK_SCALE = 8.0

    E = Emit(nc, ctx)
    sbt = lambda name, shape, dt: ctx.enter_context(nc.sbuf_tensor("pb_" + name, shape, dt))
    ps = [ctx.enter_context(nc.psum_tensor(f"pb_ps{i}", [128, 512], F32)) for i in range(8)]
    dram = lambda name, shape: nc.dram_tensor("pb_" + name, shape, BF16).ap()

    wqkvb = dram("wqkvb", [D, 3 * D])
    wob = dram("wob", [D, D])
    w1b = dram("w1b", [D, DI2])
    w2b = dram("w2b", [DI, D])

    cast_mark = {}

    def cast(name, src, dst, rpc=512):
        cs = E.sem("cast_" + name)
        for i in range(cdiv(src.shape[0], rpc)):
            r0, r1 = i * rpc, min((i + 1) * rpc, src.shape[0])
            cs.bump(16)
            E("g", "dma_start", out=dst[r0:r1], in_=src[r0:r1], incs=[(cs.h, 16)])
        cast_mark[name] = (cs.h, cs.n)

    cast("wqkv", wqkv, wqkvb)
    cast("wo", wo, wob)
    cast("w1", w1, w1b)
    cast("w2", w2, w2b)

    ident, ident_ini = make_identity(E)
    identb = sbt("identb", [128, 128], BF16)
    ib_ini = E.sem("ib_ini")
    E("g", "memset", identb[:, :], 1.0, incs=[(ib_ini.h, 1)])
    ib_ini.bump(1)
    E("g", "affine_select", identb[:, :], identb[:, :], pattern=[[-1, 128]],
      compare_op=ALU.is_equal, fill=0.0, base=0, channel_multiplier=1,
      waits=[(ib_ini.h, 1)], incs=[(ib_ini.h, 1)])
    ib_ini.bump(1)

    xsl = sbt("x", [128, KD * XTS], F32)
    xld = E.sem("xld")
    ini = E.sem("ini")
    for k in range(KD):
        E("g", "memset", xsl[:, k * XTS:k * XTS + 1], 0.0,
          incs=([(ini.h, 1)] if k == KD - 1 else []))
    ini.bump(1)
    for k in range(KD):
        xld.bump(16)
        E("g", "dma_start", out=xsl[:, k * XTS + 1:k * XTS + 1 + TT],
          in_=x_in[k * 128:(k + 1) * 128, :],
          waits=([(ini.h, 1)] if k == 0 else []), incs=[(xld.h, 16)])
    XLD = xld.n

    def xap(k, c0, w, shifted):
        off = k * XTS + 1 + c0 - (1 if shifted else 0)
        return xsl[:, off:off + w]

    sld, smm, sev = E.sem("sld"), E.sem("smm"), E.sem("sev")
    vraw = {}
    vcol = {}
    specs = [("qs", qs, 64), ("ks", ks, 64), ("g", ln_g, D), ("b", ln_b, D),
             ("b2", b2, D), ("lm", lmask, TT)]
    for nm, src, n in specs:
        rows = cdiv(n, 128)
        pr = 128 if n >= 128 else n
        vraw[nm] = sbt("vraw_" + nm, [rows, pr], F32)
        vcol[nm] = sbt("vcol_" + nm, [128, rows], F32)
        sld.bump(16)
        E("g", "dma_start", out=vraw[nm][:, :],
          in_=src.rearrange("(r p) -> r p", p=pr), incs=[(sld.h, 16)])
    SLD = sld.n
    for nm, src, n in specs:
        rows = cdiv(n, 128)
        pr = 128 if n >= 128 else n
        prev = sev.n
        smm.bump(1)
        E("t", "transpose", ps[0][0:pr, 0:rows], vraw[nm][:, :],
          ident[0:rows, 0:rows],
          waits=[(sld.h, SLD), (ident_ini.h, 1), (sev.h, prev)],
          incs=[(smm.h, 1)])
        sev.bump(1)
        E("v", "tensor_copy", vcol[nm][0:pr, :], ps[0][0:pr, 0:rows],
          waits=[(smm.h, smm.n)], incs=[(sev.h, 1)])
    for nm in ("qs", "ks"):
        prev = sev.n
        sev.bump(1)
        E("v", "tensor_copy", vcol[nm][64:128, :], vcol[nm][0:64, :],
          waits=[(sev.h, prev)], incs=[(sev.h, 1)])
    b1a = sbt("b1a", [128, NAT], F32)
    b1g = sbt("b1g", [128, NAT], F32)
    b1raw = sbt("b1raw", [NAT - 1, 128], F32)
    b1ld = E.sem("b1ld")
    b1mm = E.sem("b1mm")
    nfull = NAT - 1
    for dst, off in ((b1a, 0), (b1g, DI)):
        lv = b1ld.bump(16)
        E("g", "dma_start", out=b1raw[:, :],
          in_=b1[off:off + nfull * 128].rearrange("(r p) -> r p", p=128),
          waits=[(b1mm.h, b1mm.n)], incs=[(b1ld.h, 16)])
        b1mm.bump(1)
        E("t", "transpose", ps[2][:, 0:nfull], b1raw[:, :],
          ident[0:nfull, 0:nfull],
          waits=[(b1ld.h, lv), (ident_ini.h, 1), (b1mm.h, b1mm.n - 1)],
          incs=[(b1mm.h, 1)])
        b1mm.bump(1)
        E("v", "tensor_copy", dst[:, 0:nfull], ps[2][:, 0:nfull],
          waits=[(b1mm.h, b1mm.n - 1)], incs=[(b1mm.h, 1)])
        lv = b1ld.bump(16)
        E("g", "dma_start", out=b1raw[0:1, 0:LK],
          in_=b1[off + nfull * 128:off + DI][None, :],
          waits=[(b1mm.h, b1mm.n)], incs=[(b1ld.h, 16)])
        b1mm.bump(1)
        E("t", "transpose", ps[2][0:LK, 0:1], b1raw[0:1, 0:LK],
          ident[0:1, 0:1],
          waits=[(b1ld.h, lv), (b1mm.h, b1mm.n - 1)], incs=[(b1mm.h, 1)])
        b1mm.bump(1)
        E("v", "tensor_copy", dst[0:LK, nfull:NAT], ps[2][0:LK, 0:1],
          waits=[(b1mm.h, b1mm.n - 1)], incs=[(b1mm.h, 1)])
    B1LD = b1mm.n
    SEV = sev.n

    xo2 = sbt("fxo", [128, KD * 512], F32)
    msl2 = sbt("fm", [128, NAT * 512], BF16)
    csl = sbt("cs", [128, TT], BF16)
    ssl = sbt("ss", [128, TT], BF16)
    cil = sbt("ci", [128, TT], BF16)
    sil = sbt("si", [128, TT], BF16)
    rota = E.sem("rota")
    rcol = sbt("rcol", [128, 4], F32)
    if KD * 512 >= 3584:
        rrow = xo2[:, 0:3072]
        rib = xo2[:, 3072:3584].bitcast(mybir.dt.int32)
    else:
        rrow = sbt("rrow", [128, 512 * 6], F32)
        rib = sbt("rib", [128, 512], mybir.dt.int32)
    posr = sbt("posr", [1, 1], F32)
    rota16 = E.sem("rota16")
    rota16.bump(16)
    E("g", "dma_start", out=posr[:, :], in_=posb[None, :], incs=[(rota16.h, 16)])
    E("g", "iota", rcol[:, 0:1], pattern=[[0, 1]], base=0, channel_multiplier=1,
      allow_small_or_imprecise_dtypes=True, incs=[(rota.h, 1)])
    rota.bump(1)
    vchain = E.sem("rvch")

    def vop(method, *a, w=(), **kw):
        pv = vchain.n
        vchain.bump(1)
        E("v", method, *a, waits=list(w) + [(vchain.h, pv)], incs=[(vchain.h, 1)], **kw)

    def scop(method, *a, w=(), **kw):
        pv = vchain.n
        vchain.bump(1)
        E("sc", method, *a, waits=list(w) + [(vchain.h, pv)], incs=[(vchain.h, 1)], **kw)

    ri = sbt("ri", [128, 1], mybir.dt.int32)
    vop("tensor_scalar_mul", rcol[:, 1:2], rcol[:, 0:1], 1.0 / 32,
        w=[(rota.h, 1)])
    vop("tensor_copy", ri[:, :], rcol[:, 1:2])
    vop("tensor_copy", rcol[:, 1:2], ri[:, :])
    vop("scalar_tensor_tensor", rcol[:, 1:2], rcol[:, 1:2], -32.0, rcol[:, 0:1],
        ALU.mult, ALU.add)
    vop("tensor_scalar", rcol[:, 2:3], rcol[:, 1:2], 1.0 / 32, -0.499,
        ALU.mult, ALU.add)
    vop("tensor_copy", ri[:, :], rcol[:, 2:3])
    vop("tensor_copy", rcol[:, 2:3], ri[:, :])
    vop("scalar_tensor_tensor", rcol[:, 1:2], rcol[:, 2:3], -32.0, rcol[:, 1:2],
        ALU.mult, ALU.add)
    scop("activation", rcol[:, 2:3], rcol[:, 1:2], AF.Exp,
         scale=-float(np.log(10000.0) / 32.0))
    vop("tensor_scalar", rcol[:, 3:4], rcol[:, 1:2], 2.0 / 89.6, 25.6 / 89.6,
        ALU.mult, ALU.add)
    scop("activation", rcol[:, 3:4], rcol[:, 3:4], AF.Ln)
    RCOL = vchain.n

    TWO_PI = float(2 * np.pi)
    posc = sbt("posc", [128, 1], F32)
    onesr = sbt("onesr", [1, 128], F32)
    E("g", "memset", onesr[:, :], 1.0, incs=[(rota.h, 1)])
    rota.bump(1)
    E("t", "matmul", ps[1][:, 0:1], onesr[:, :], posr[0:1, 0:1],
      start=True, stop=True,
      waits=[(rota16.h, 16), (rota.h, rota.n)], incs=[(rota.h, 1)])
    rota.bump(1)
    vop("tensor_copy", posc[:, :], ps[1][:, 0:1], w=[(rota.h, rota.n)])
    nrt = cdiv(TT, 512)
    for rc in range(nrt):
        c0 = rc * 512
        wdt = min(512, TT - c0)
        A = rrow[:, 0:wdt]
        B = rrow[:, 512:512 + wdt]
        C = rrow[:, 1024:1024 + wdt]
        Dw = rrow[:, 1536:1536 + wdt]
        Ew = rrow[:, 2048:2048 + wdt]
        Fw = rrow[:, 2560:2560 + wdt]
        E("g", "iota", A, pattern=[[1, wdt]], base=0, channel_multiplier=0,
          allow_small_or_imprecise_dtypes=True,
          waits=[(vchain.h, vchain.n)], incs=[(rota.h, 1)])
        rota.bump(1)
        vop("tensor_scalar_add", A, A, float(c0), w=[(rota.h, rota.n)])
        vop("tensor_scalar", A, A, posc[:, 0:1], None, ALU.add)
        vop("tensor_scalar", B, A, 1.0 / 128.0, -float(SHALF) / 128.0,
            ALU.mult, ALU.add)
        vop("tensor_scalar", B, B, rcol[:, 3:4], None, ALU.mult,
            w=[(vchain.h, RCOL)])
        scop("activation", Ew, B, AF.Exp)
        scop("activation", Fw, B, AF.Exp, scale=-1.0)
        vop("tensor_scalar", A, A, rcol[:, 2:3], None, ALU.mult)
        vop("tensor_scalar_mul", B, A, 1.0 / TWO_PI)
        vop("tensor_copy", rib[:, 0:wdt], B)
        vop("tensor_copy", B, rib[:, 0:wdt])
        vop("scalar_tensor_tensor", A, B, -TWO_PI, A, ALU.mult, ALU.add)
        scop("activation", B, A, AF.Sin, scale=0.5)
        scop("activation", C, A, AF.Sin, scale=0.25)
        scop("activation", C, C, AF.Square)
        vop("tensor_scalar", C, C, -2.0, 1.0, ALU.mult, ALU.add)
        vop("scalar_tensor_tensor", Dw, B, 2.0, C, ALU.mult, ALU.mult)
        scop("activation", B, B, AF.Square)
        vop("tensor_scalar", B, B, -2.0, 1.0, ALU.mult, ALU.add)
        vop("tensor_tensor", csl[:, c0:c0 + wdt], B, Ew, ALU.mult)
        vop("tensor_tensor", ssl[:, c0:c0 + wdt], Dw, Ew, ALU.mult)
        vop("tensor_tensor", cil[:, c0:c0 + wdt], B, Fw, ALU.mult)
        vop("tensor_tensor", sil[:, c0:c0 + wdt], Dw, Fw, ALU.mult)
    ROTA_DONE = vchain.n

    Rt = sbt("R", [128, 128], BF16)
    rsel = E.sem("rsel")
    E("g", "memset", Rt[:, :], 0.0, incs=[(rsel.h, 1)])
    rsel.bump(1)
    neg1 = sbt("neg1", [64, 64], BF16)
    E("g", "memset", neg1[0:32, :], 1.0, incs=[(rsel.h, 1)])
    rsel.bump(1)
    E("g", "memset", neg1[32:64, :], -1.0, incs=[(rsel.h, 1)])
    rsel.bump(1)
    E("g", "affine_select", Rt[0:32, 0:64], neg1[0:32, 0:64],
      pattern=[[1, 64]], compare_op=ALU.is_equal, fill=0.0, base=-32,
      channel_multiplier=-1, waits=[(rsel.h, 3)], incs=[(rsel.h, 1)])
    rsel.bump(1)
    E("g", "affine_select", Rt[32:64, 0:64], neg1[32:64, 0:64],
      pattern=[[1, 64]], compare_op=ALU.is_equal, fill=0.0, base=0,
      channel_multiplier=-1, waits=[(rsel.h, rsel.n)], incs=[(rsel.h, 1)])
    rsel.bump(1)
    E("v", "tensor_copy", Rt[64:128, 64:128], Rt[0:64, 0:64],
      waits=[(rsel.h, rsel.n)], incs=[(rsel.h, 1)])
    rsel.bump(1)
    RSEL = rsel.n

    NPAIR = 2
    FTB2 = max(1, KD // 4)
    NBQ = KD // FTB2
    LN8 = float(np.log(8.0))
    kpsl = sbt("kp", [128, NPAIR * KD * 256], BF16)
    vsl = sbt("vh", [128, NPAIR * 2 * H * 64], BF16)
    qpsl = sbt("qp", [128, 2 * KD * 256], BF16)
    xbt = sbt("xbt", [128, KD * 257], BF16)
    sqa = sbt("sqa", [128, KD * 256], BF16)
    ibsa = sqa
    invq = sbt("invq", [128, 4 * H], F32)
    nrowq = sbt("nrowq", [H, 256], F32)
    invkb = sbt("invkb", [H, 256], BF16)
    qsel = sbt("qsel", [128, KD * H], BF16)
    ksel = sbt("ksel", [H, KD * 128], BF16)
    ln8c = sbt("ln8c", [128, 1], F32)
    ubf = sbt("ubf", [128, 4 * 768], BF16)
    attq = sbt("attq", [128, 2 * 384], BF16)
    attT = sbt("attT", [128, 2 * 384], BF16)
    negx = sbt("negx", [128, 2], F32)
    negb = sbt("negb", [128, 2], F32)
    rsv = sbt("rsv", [128, 2], F32)
    Osl = sbt("O", [128, KD * 512], BF16)
    mask3 = sbt("mask3", [128, 384], F32)
    lmq = sbt("lmq", [128, 3 * 384], BF16)
    dum2 = sbt("dum2", [1, 1], F32)
    epsc2 = sbt("epsc2", [1, 1], F32)

    mini = E.sem("mini")
    E("g", "memset", mask3[:, 0:384], 0.0, incs=[(mini.h, 1)])
    mini.bump(1)
    E("g", "memset", ln8c[:, :], LN8, incs=[(mini.h, 1)])
    mini.bump(1)
    E("g", "memset", dum2[:, :], 0.0, incs=[(mini.h, 1)])
    mini.bump(1)
    E("g", "affine_select", mask3[:, 0:128], mask3[:, 0:128], pattern=[[1, 128]],
      compare_op=ALU.is_ge, fill=-1e20, base=0, channel_multiplier=-1,
      waits=[(mini.h, mini.n)], incs=[(mini.h, 1)])
    mini.bump(1)
    E("g", "affine_select", mask3[:, 256:384], mask3[:, 256:384], pattern=[[-1, 128]],
      compare_op=ALU.is_ge, fill=-1e20, base=0, channel_multiplier=1,
      waits=[(mini.h, mini.n)], incs=[(mini.h, 1)])
    mini.bump(1)
    E("g", "memset", qsel[:, :], 1.0, incs=[(mini.h, 1)])
    mini.bump(1)
    E("g", "memset", ksel[:, :], 1.0, incs=[(mini.h, 1)])
    mini.bump(1)
    for ft in range(KD):
        qs_t = qsel[:, ft * H:(ft + 1) * H]
        E("g", "affine_select", qs_t[0:64, :], qs_t[0:64, :], pattern=[[1, H]],
          compare_op=ALU.is_equal, fill=0.0, base=-2 * ft, channel_multiplier=0,
          waits=[(mini.h, mini.n)], incs=[(mini.h, 1)])
        mini.bump(1)
        E("g", "affine_select", qs_t[64:128, :], qs_t[64:128, :], pattern=[[1, H]],
          compare_op=ALU.is_equal, fill=0.0, base=-(2 * ft + 1),
          channel_multiplier=0, waits=[(mini.h, mini.n)], incs=[(mini.h, 1)])
        mini.bump(1)
        ks_t = ksel[:, ft * 128:(ft + 1) * 128]
        E("g", "affine_select", ks_t[:, 0:64], ks_t[:, 0:64], pattern=[[0, 64]],
          compare_op=ALU.is_equal, fill=0.0, base=-2 * ft, channel_multiplier=1,
          waits=[(mini.h, mini.n)], incs=[(mini.h, 1)])
        mini.bump(1)
        E("g", "affine_select", ks_t[:, 64:128], ks_t[:, 64:128], pattern=[[0, 64]],
          compare_op=ALU.is_equal, fill=0.0, base=-(2 * ft + 1),
          channel_multiplier=1, waits=[(mini.h, mini.n)], incs=[(mini.h, 1)])
        mini.bump(1)
    MINI = mini.n
    lmm = E.sem("lmm")
    lmr = msl2.bitcast(F32)[0:1, 0:TT]
    lmld = E.sem("lmld")
    lmld.bump(16)
    E("g", "dma_start", out=lmr[:, :], in_=lmask[None, :], incs=[(lmld.h, 16)])
    for i, t in enumerate((2, 3, 4)):
        prev = lmm.n
        lmm.bump(1)
        E("t", "matmul", ps[5][:, 0:384], onesr[:, :],
          lmr[0:1, (t - 2) * 128:(t + 1) * 128],
          start=True, stop=True,
          waits=[(lmld.h, 16), (rota.h, 2), (lmm.h, prev)], incs=[(lmm.h, 1)])
        E("v", "tensor_copy", lmq[:, i * 384:(i + 1) * 384], ps[5][:, 0:384],
          waits=[(lmm.h, lmm.n)], incs=[(lmm.h, 1)])
        lmm.bump(1)
    LMM = lmm.n
    bankc5_init = (lmm.h, LMM)

    wring = WStream(E, "pb_wring", 256, 11)
    wring2 = WStream(E, "pb_wring2", 256, 11, queue="g")

    bankc = [None] * 8
    ubc = [None, None, None, None]

    def bwait(b):
        return [bankc[b]] if bankc[b] is not None else []

    pq = E.sem("pq")
    xbc = E.sem("xbc")
    pev = E.sem("pev")
    pevS = E.sem("pevS")
    apg = E.sem("apg")
    smx = E.sem("smx")
    sxp = E.sem("sxp")
    att_s = E.sem("att_s")
    avm = E.sem("avm")
    aev = E.sem("aev")
    womm = E.sem("womm")
    woev = E.sem("woev")
    ffst = E.sem("ffst")
    stq = E.sem("stq")
    pgv = E.sem("pgv")

    def proj_pair(p, ntl):
        W = 128 * ntl
        c0 = p * 256
        pslot = p % NPAIR
        ppar = p % 2
        for k in range(KD):
            w = [(pq.h, pq.n)] if k == 0 else []
            if p == 0 and k == 0:
                w.append((xld.h, XLD))
            E("g", "tensor_copy",
              xbt[:, k * 257:k * 257 + W + 1],
              xap(k, c0, W + 1, shifted=True), waits=w, incs=[(xbc.h, 1)])
            xbc.bump(1)
        XBC = xbc.n

        def xbap(k, shifted, tl=None):
            base = k * 257
            b = base if shifted else base + 1
            if tl is None:
                return xbt[:, b:b + W]
            return xbt[:, b + tl * 128:b + tl * 128 + 128]

        def stream_qk(coloff, bank, colb):
            for k in range(KD):
                ring = wring if k % 2 == 0 else wring2
                lw = []
                if p == 0 and coloff == 0 and colb == 0 and k <= 1:
                    lw.append(cast_mark["wqkv"])
                if ring.last_cert:
                    lw.append(ring.last_cert)
                ap, ldw = ring.load(
                    wqkvb[k * 128:(k + 1) * 128, coloff:coloff + 128],
                    transpose=False, waits=lw)
                w = [ldw]
                if k == 0:
                    w += bwait(ps.index(bank))
                    w.append((xbc.h, XBC))
                incs = []
                if k == KD - 1:
                    pq.bump(1)
                    incs = [(pq.h, 1)]
                E("t", "matmul", bank[:, colb:colb + W], ap,
                  xbap(k, shifted=(k >= KD // 2)),
                  start=(k == 0), stop=(k == KD - 1), waits=w, incs=incs)
            wring.last_cert = (pq.h, pq.n)
            wring2.last_cert = (pq.h, pq.n)

        for ft in range(KD):
            stream_qk(ft * 128, ps[ft // FTB2], (ft % FTB2) * 256)
        QMM = pq.n
        for bi in range(NBQ):
            if ntl == 2:
                E("sc", "activation",
                  sqa[:, bi * FTB2 * 256:(bi * FTB2 + FTB2) * 256],
                  ps[bi][:, 0:FTB2 * 256], AF.Square,
                  waits=[(pq.h, QMM)], incs=[(pevS.h, 1)])
                pevS.bump(1)
            else:
                for fi in range(FTB2):
                    ft = bi * FTB2 + fi
                    E("sc", "activation", sqa[:, ft * 256:ft * 256 + W],
                      ps[bi][:, fi * 256:fi * 256 + W], AF.Square,
                      waits=[(pq.h, QMM)], incs=[(pevS.h, 1)])
                    pevS.bump(1)
        SQQ = pevS.n
        for ft in range(KD):
            w = []
            if ft == 0:
                w = [(pevS.h, SQQ), (mini.h, MINI)] + bwait(4)
            E("t", "matmul", ps[4][0:H, 0:W], qsel[:, ft * H:(ft + 1) * H],
              sqa[:, ft * 256:ft * 256 + W], start=(ft == 0),
              stop=(ft == KD - 1), waits=w,
              incs=([(pq.h, 1)] if ft == KD - 1 else []))
        pq.bump(1)
        QSUM = pq.n
        E("v", "tensor_copy", nrowq[:, 0:W], ps[4][0:H, 0:W],
          waits=[(pq.h, QSUM)], incs=[(pev.h, 1)])
        pev.bump(1)
        bankc[4] = (pev.h, pev.n)
        NRQ = pev.n
        for tl in range(ntl):
            w = [(pev.h, NRQ), (ident_ini.h, 2)]
            if tl == 0:
                w += bwait(5)
            E("t", "transpose", ps[5][:, tl * H:(tl + 1) * H],
              nrowq[:, tl * 128:(tl + 1) * 128], ident[0:H, 0:H],
              waits=w, incs=[(pq.h, 1)])
            pq.bump(1)
        iq = invq[:, (ppar * 2) * H:(ppar * 2 + ntl) * H]
        E("v", "tensor_scalar_max", iq, ps[5][:, 0:ntl * H], 1e-24,
          waits=[(pq.h, pq.n), (sxp.h, sxp.n)], incs=[(pev.h, 1)])
        pev.bump(1)
        bankc[5] = (pev.h, pev.n)
        E("sc", "activation", iq, iq, AF.Ln,
          waits=[(pev.h, pev.n)], incs=[(pevS.h, 1)])
        pevS.bump(1)
        E("sc", "activation", iq, iq, AF.Exp, scale=-0.5, bias=ln8c[:, 0:1],
          waits=[(pevS.h, pevS.n), (mini.h, MINI)], incs=[(pevS.h, 1)])
        pevS.bump(1)
        for bi in range(NBQ):
            bank = ps[bi]
            ub = ubf[:, bi * 768:bi * 768 + 768]
            for fi in range(FTB2):
                ft = bi * FTB2 + fi
                w = [(pq.h, pq.n), (sev.h, SEV), (pev.h, pev.n)]
                if fi == 0 and ubc[bi] is not None:
                    w.append(ubc[bi])
                E("v", "tensor_scalar", ub[:, 0:W], bank[:, fi * 256:fi * 256 + W],
                  vcol["qs"][:, 0:1], None, ALU.mult,
                  waits=w, incs=[(pev.h, 1)])
                pev.bump(1)
                E("v", "tensor_tensor", ub[:, 256:256 + W], ub[:, 0:W],
                  csl[:, c0:c0 + W], ALU.mult,
                  waits=[(pev.h, pev.n), (vchain.h, ROTA_DONE)], incs=[(pev.h, 1)])
                pev.bump(1)
                E("v", "tensor_tensor", ub[:, 512:512 + W], ub[:, 0:W],
                  ssl[:, c0:c0 + W], ALU.mult,
                  waits=[(pev.h, pev.n)], incs=[(pev.h, 1)])
                pev.bump(1)
                w = [(pev.h, pev.n), (ib_ini.h, 2), (rsel.h, RSEL)]
                if fi == 0:
                    w.append((pevS.h, SQQ))
                E("t", "matmul", bank[:, fi * 256:fi * 256 + W], identb[:, :],
                  ub[:, 256:256 + W], start=True, stop=False, waits=w,
                  incs=[(pq.h, 1)])
                pq.bump(1)
                E("t", "matmul", bank[:, fi * 256:fi * 256 + W], Rt[:, :],
                  ub[:, 512:512 + W], start=False, stop=True, incs=[(pq.h, 1)])
                pq.bump(1)
                ubc[bi % 2] = (pq.h, pq.n)
            for fi in range(FTB2 if ntl == 1 else 1):
                ft0 = bi * FTB2 + fi
                wdt = FTB2 * 256 if ntl == 2 else W
                E("sc", "activation",
                  qpsl[:, (ppar * KD + ft0) * 256:(ppar * KD + ft0) * 256 + wdt],
                  bank[:, fi * 256:fi * 256 + wdt], AF.Copy, scale=1.0,
                  waits=[(pq.h, pq.n)], incs=[(pevS.h, 1)])
                pevS.bump(1)
            bankc[bi] = (pevS.h, pevS.n)
        for ft in range(KD):
            stream_qk(D + ft * 128, ps[ft // FTB2], (ft % FTB2) * 256)
        KMM = pq.n
        for bi in range(NBQ):
            if ntl == 2:
                E("sc", "activation",
                  sqa[:, bi * FTB2 * 256:(bi * FTB2 + FTB2) * 256],
                  ps[bi][:, 0:FTB2 * 256], AF.Square,
                  waits=[(pq.h, KMM), (pq.h, QSUM)], incs=[(pevS.h, 1)])
                pevS.bump(1)
            else:
                for fi in range(FTB2):
                    ft = bi * FTB2 + fi
                    E("sc", "activation", sqa[:, ft * 256:ft * 256 + W],
                      ps[bi][:, fi * 256:fi * 256 + W], AF.Square,
                      waits=[(pq.h, KMM), (pq.h, QSUM)], incs=[(pevS.h, 1)])
                    pevS.bump(1)
        SQK = pevS.n
        for ft in range(KD):
            w = []
            if ft == 0:
                w = [(pevS.h, SQK), (pev.h, NRQ)] + bwait(4)
            E("t", "matmul", ps[4][0:H, 0:W], qsel[:, ft * H:(ft + 1) * H],
              sqa[:, ft * 256:ft * 256 + W], start=(ft == 0),
              stop=(ft == KD - 1), waits=w,
              incs=([(pq.h, 1)] if ft == KD - 1 else []))
        pq.bump(1)
        KSUM = pq.n
        E("v", "tensor_scalar_max", nrowq[:, 0:W], ps[4][0:H, 0:W], 1e-24,
          waits=[(pq.h, KSUM)], incs=[(pev.h, 1)])
        pev.bump(1)
        bankc[4] = (pev.h, pev.n)
        E("sc", "activation", nrowq[:, 0:W], nrowq[:, 0:W], AF.Ln,
          waits=[(pev.h, pev.n)], incs=[(pevS.h, 1)])
        pevS.bump(1)
        E("sc", "activation", invkb[:, 0:W], nrowq[:, 0:W], AF.Exp, scale=-0.5,
          waits=[(pevS.h, pevS.n)], incs=[(pevS.h, 1)])
        pevS.bump(1)
        INVK = pevS.n
        for rnd in range(cdiv(KD, 2)):
            bnk = ps[5] if rnd % 2 == 0 else ps[4]
            for j in range(2):
                ft = rnd * 2 + j
                if ft >= KD:
                    continue
                w = []
                if j == 0:
                    w = [(pevS.h, INVK)] + bwait(5 if rnd % 2 == 0 else 4)
                E("t", "matmul", bnk[:, j * 256:j * 256 + W],
                  ksel[:, ft * 128:(ft + 1) * 128], invkb[0:H, 0:W],
                  start=True, stop=True, waits=w,
                  incs=([(pq.h, 1)] if (j == 1 or ft == KD - 1) else []))
            pq.bump(1)
            nf = min(2, KD - rnd * 2)
            if ntl == 2:
                E("v", "tensor_copy", ibsa[:, rnd * 512:rnd * 512 + nf * 256],
                  bnk[:, 0:nf * 256], waits=[(pq.h, pq.n)], incs=[(pev.h, 1)])
                pev.bump(1)
            else:
                for j in range(nf):
                    ft = rnd * 2 + j
                    E("v", "tensor_copy", ibsa[:, ft * 256:ft * 256 + W],
                      bnk[:, j * 256:j * 256 + W],
                      waits=[(pq.h, pq.n)], incs=[(pev.h, 1)])
                    pev.bump(1)
            bankc[5 if rnd % 2 == 0 else 4] = (pev.h, pev.n)
        IBSA = pev.n
        for bi in range(NBQ):
            bank = ps[bi]
            ub = ubf[:, bi * 768:bi * 768 + 768]
            for fi in range(FTB2):
                ft = bi * FTB2 + fi
                w = [(pev.h, IBSA), (pev.h, pev.n), (pq.h, pq.n)]
                if fi == 0 and ubc[bi] is not None:
                    w.append(ubc[bi])
                E("v", "scalar_tensor_tensor", ub[:, 0:W],
                  bank[:, fi * 256:fi * 256 + W], vcol["ks"][:, 0:1],
                  ibsa[:, ft * 256:ft * 256 + W],
                  ALU.mult, ALU.mult, waits=w, incs=[(pev.h, 1)])
                pev.bump(1)
                E("v", "tensor_tensor", ub[:, 256:256 + W], ub[:, 0:W],
                  cil[:, c0:c0 + W], ALU.mult,
                  waits=[(pev.h, pev.n)], incs=[(pev.h, 1)])
                pev.bump(1)
                E("v", "tensor_tensor", ub[:, 512:512 + W], ub[:, 0:W],
                  sil[:, c0:c0 + W], ALU.mult,
                  waits=[(pev.h, pev.n)], incs=[(pev.h, 1)])
                pev.bump(1)
                w = [(pev.h, pev.n)]
                if fi == 0:
                    w.append((pevS.h, SQK))
                E("t", "matmul", bank[:, fi * 256:fi * 256 + W], identb[:, :],
                  ub[:, 256:256 + W], start=True, stop=False, waits=w,
                  incs=[(pq.h, 1)])
                pq.bump(1)
                E("t", "matmul", bank[:, fi * 256:fi * 256 + W], Rt[:, :],
                  ub[:, 512:512 + W], start=False, stop=True, incs=[(pq.h, 1)])
                pq.bump(1)
                ubc[bi] = (pq.h, pq.n)
            for fi in range(FTB2 if ntl == 1 else 1):
                ft0 = bi * FTB2 + fi
                wdt = FTB2 * 256 if ntl == 2 else W
                wv = [(pq.h, pq.n)]
                if p >= NPAIR:
                    wv.append((apg.h, max(0, 2 * p - 4)))
                E("sc", "activation",
                  kpsl[:, (pslot * KD + ft0) * 256:(pslot * KD + ft0) * 256 + wdt],
                  bank[:, fi * 256:fi * 256 + wdt], AF.Copy, scale=1.0,
                  waits=wv, incs=[(pevS.h, 1)])
                pevS.bump(1)
            bankc[bi] = (pevS.h, pevS.n)
        NV = D // 256
        for nv in range(NV):
            for k in range(KD):
                ring = wring if k % 2 == 0 else wring2
                lw = [ring.last_cert] if ring.last_cert else []
                ap, ldw = ring.load(
                    wqkvb[k * 128:(k + 1) * 128,
                          2 * D + nv * 256:2 * D + (nv + 1) * 256],
                    transpose=False, waits=lw)
                for tl in range(ntl):
                    bank = ps[6 + tl]
                    w = [ldw] if tl == 0 else []
                    if k == 0:
                        if nv % 2 == 0:
                            w += bwait(6 + tl)
                        if nv == 0:
                            w.append((xbc.h, XBC))
                    incs = []
                    if k == KD - 1 and tl == ntl - 1:
                        pq.bump(1)
                        incs = [(pq.h, 1)]
                    E("t", "matmul", bank[:, (nv % 2) * 256:(nv % 2) * 256 + 256],
                      xbap(k, shifted=(k >= KD // 2), tl=tl), ap,
                      start=(k == 0), stop=(k == KD - 1), waits=w, incs=incs)
            wring.last_cert = (pq.h, pq.n)
            wring2.last_cert = (pq.h, pq.n)
            if nv % 2 == 1 or nv == NV - 1:
                nv0 = (nv // 2) * 2
                cw = (nv - nv0 + 1) * 256
                for tl in range(ntl):
                    wv = [(pq.h, pq.n)]
                    if p >= NPAIR:
                        wv.append((apg.h, max(0, 2 * p - 4)))
                    E("v", "tensor_copy",
                      vsl[:, (pslot * 2 + tl) * (H * 64) + nv0 * 256:
                          (pslot * 2 + tl) * (H * 64) + nv0 * 256 + cw],
                      ps[6 + tl][:, 0:cw], waits=wv, incs=[(pev.h, 1)])
                    pev.bump(1)
                    bankc[6 + tl] = (pev.h, pev.n)

    attc = {"aT": [None, None], "av": [None, None], "ex": [None, None],
            "ng": [None, None]}

    def attn_tile(gt):
        par2 = gt % 2
        pt = gt // 2
        ppar = pt % 2
        for hd in range(H):
            hp = hd % 2
            sl4 = hp
            sbank = ps[4 + hp]
            avb = ps[6 + hp]
            aq = attq[:, sl4 * 384:sl4 * 384 + 384]
            aT = attT[:, sl4 * 384:sl4 * 384 + 384]
            ft, fo = hd // 2, (hd % 2) * 64
            iqc = invq[:, (ppar * 2 + par2) * H + hd:(ppar * 2 + par2) * H + hd + 1]
            qap = qpsl[:, (ppar * KD + ft) * 256 + par2 * 128:
                       (ppar * KD + ft) * 256 + par2 * 128 + 128][fo:fo + 64, :]
            for d in range(3):
                t = gt - 2 + d
                slot = (t // 2) % NPAIR
                kap = kpsl[:, (slot * KD + ft) * 256 + (t % 2) * 128:
                           (slot * KD + ft) * 256 + (t % 2) * 128 + 128][fo:fo + 64, :]
                w = []
                if d == 0:
                    w += bwait(4 + hp)
                    if hd <= 1:
                        w.append((pev.h, pev.n))
                        w.append((pevS.h, pevS.n))
                E("t", "matmul", sbank[:, d * 128:(d + 1) * 128], qap, kap,
                  start=True, stop=True, waits=w,
                  incs=([(pq.h, 1)] if d == 2 else []))
            pq.bump(1)
            sc_mm = (pq.h, pq.n)
            E("v", "tensor_tensor", sbank[:, 0:384], sbank[:, 0:384], mask3[:, :],
              ALU.add, waits=[sc_mm, (mini.h, MINI)], incs=[(smx.h, 1)])
            smx.bump(1)
            if gt <= 4:
                E("v", "tensor_tensor", sbank[:, 0:384], sbank[:, 0:384],
                  lmq[:, (gt - 2) * 384:(gt - 1) * 384], ALU.add,
                  waits=[(smx.h, smx.n), (lmm.h, LMM)], incs=[(smx.h, 1)])
                smx.bump(1)
            mskc = (smx.h, smx.n)
            w = [mskc]
            if attc["ng"][hp] is not None:
                w.append(attc["ng"][hp])
            E("v", "reduce_max", negx[:, sl4:sl4 + 1], sbank[:, 0:384], AX.X,
              negate=True, waits=w, incs=[(smx.h, 1)])
            smx.bump(1)
            E("v", "tensor_scalar", negb[:, sl4:sl4 + 1], negx[:, sl4:sl4 + 1],
              1e19, iqc, ALU.min, ALU.mult,
              waits=[(smx.h, smx.n)], incs=[(smx.h, 1)])
            smx.bump(1)
            ngc = (smx.h, smx.n)
            w = [ngc]
            if attc["aT"][hp] is not None:
                w.append(attc["aT"][hp])
            E("sc", "activation", aq, sbank[:, 0:384], AF.Exp,
              bias=negb[:, sl4:sl4 + 1], scale=iqc,
              accum_out=rsv[:, sl4:sl4 + 1],
              waits=w, incs=[(sxp.h, 1)])
            sxp.bump(1)
            attc["ng"][hp] = (sxp.h, sxp.n)
            E("v", "tensor_scalar_max", rsv[:, sl4:sl4 + 1], rsv[:, sl4:sl4 + 1],
              1e-30, waits=[(sxp.h, sxp.n)], incs=[(smx.h, 1)])
            smx.bump(1)
            E("v", "reciprocal", rsv[:, sl4:sl4 + 1], rsv[:, sl4:sl4 + 1],
              waits=[(smx.h, smx.n)], incs=[(smx.h, 1)])
            smx.bump(1)
            E("v", "tensor_scalar", aq, aq, rsv[:, sl4:sl4 + 1], None, ALU.mult,
              waits=[(smx.h, smx.n)], incs=[(smx.h, 1)])
            smx.bump(1)
            aqrd = (smx.h, smx.n)
            for d in range(3):
                E("t", "transpose",
                  sbank[:, 0:192].bitcast(BF16)[:, d * 128:(d + 1) * 128],
                  aq[:, d * 128:(d + 1) * 128], identb[:, :],
                  waits=([aqrd] if d == 0 else []),
                  incs=[(pq.h, 1)])
                pq.bump(1)
            trc = (pq.h, pq.n)
            for d in range(3):
                w = []
                if d == 0:
                    w.append(trc)
                    if attc["av"][hp] is not None:
                        w.append(attc["av"][hp])
                E("v", "tensor_copy", aT[:, d * 128:(d + 1) * 128],
                  sbank[:, 0:192].bitcast(BF16)[:, d * 128:(d + 1) * 128],
                  waits=w, incs=[(att_s.h, 1)])
                att_s.bump(1)
            attc["aT"][hp] = (att_s.h, att_s.n)
            bankc[4 + hp] = (att_s.h, att_s.n)
            for d in range(3):
                t = gt - 2 + d
                slot = (t // 2) % NPAIR
                vap = vsl[:, (slot * 2 + t % 2) * (H * 64) + hd * 64:
                          (slot * 2 + t % 2) * (H * 64) + (hd + 1) * 64]
                w = [(att_s.h, att_s.n)] if d == 0 else []
                if d == 0:
                    w += bwait(6 + hp)
                E("t", "matmul", avb[0:64, 0:128], vap, aT[:, d * 128:(d + 1) * 128],
                  start=(d == 0), stop=(d == 2), waits=w,
                  incs=([(avm.h, 1)] if d == 2 else []))
            avm.bump(1)
            attc["av"][hp] = (avm.h, avm.n)
            oc = 0 if gt == 2 else (gt - 3) % 4
            E("v", "tensor_copy",
              Osl[:, ft * 512 + oc * 128:ft * 512 + oc * 128 + 128][fo:fo + 64, :],
              avb[0:64, 0:128],
              waits=[(avm.h, avm.n)], incs=[(aev.h, 1)])
            aev.bump(1)
            bankc[6 + hp] = (aev.h, aev.n)
        apg.bump(1)
        E("v", "tensor_copy", dum2[:, :], dum2[:, :],
          waits=[(aev.h, aev.n)], incs=[(apg.h, 1)])

    def wo_group(tiles):
        wdt = len(tiles) * 128
        for ogg in range(D // 256):
            for k in range(KD):
                ring = wring if k % 2 == 0 else wring2
                lw = [ring.last_cert] if ring.last_cert else []
                if ogg == 0 and k <= 1:
                    lw.append(cast_mark["wo"])
                ap, ldw = ring.load(
                    wob[k * 128:(k + 1) * 128, ogg * 256:(ogg + 1) * 256],
                    transpose=False, waits=lw)
                for m in range(2):
                    w = [ldw] if m == 0 else []
                    if k == 0 and m == 0:
                        w += bwait((ogg % 2) * 2 + 0) + bwait((ogg % 2) * 2 + 1)
                        if ogg == 0:
                            w.append((aev.h, aev.n))
                    incs = []
                    if k == KD - 1 and m == 1:
                        pq.bump(1)
                        incs = [(pq.h, 1)]
                    E("t", "matmul", ps[(ogg % 2) * 2 + m][:, 0:wdt],
                      ap[:, m * 128:(m + 1) * 128],
                      Osl[:, k * 512:k * 512 + wdt],
                      start=(k == 0), stop=(k == KD - 1), waits=w, incs=incs)
            wring.last_cert = (pq.h, pq.n)
            wring2.last_cert = (pq.h, pq.n)
            for m in range(2):
                r = ogg * 2 + m
                dstx = xap(r, tiles[0] * 128, wdt, shifted=False)
                w = [(pq.h, pq.n)] if m == 0 else []
                E("v", "tensor_add", dstx, ps[(ogg % 2) * 2 + m][:, 0:wdt], dstx,
                  waits=w, incs=[(woev.h, 1)])
                woev.bump(1)
                bankc[(ogg % 2) * 2 + m] = (woev.h, woev.n)

    bankc[5] = bankc5_init
    for pp in range(cdiv(NTT, 2)):
        ntl = min(2, NTT - 2 * pp)
        proj_pair(pp, ntl)
        for gt in range(2 * pp, 2 * pp + ntl):
            if gt == 3:
                wo_group([2])
            elif gt >= 7 and (gt - 3) % 4 == 0:
                wo_group([gt - 4, gt - 3, gt - 2, gt - 1])
            if gt >= 2:
                attn_tile(gt)
    wo_group(list(range(NTT - 4, NTT)))
    WOEV = woev.n

    zb2 = kpsl[:, 0:KD * 512]
    sq2 = vsl[:, 0:KD * 512]
    xn2 = Osl[:, 0:KD * 512]
    if KD * 129 >= 512:
        gel2 = xbt.bitcast(F32)[:, 0:512]
    else:
        gel2 = sbt("fgel", [128, 512], F32)
    if KD < 8:
        frow_s = sbt("frow", [1, 3 * 512], F32)
    onesf = sbt("onesf", [128, 1], BF16)
    fini = E.sem("fini")
    E("g", "memset", epsc2[:, :], LN_EPS)
    E("g", "memset", onesf[:, :], 1.0, incs=[(fini.h, 1)])
    fini.bump(1)
    fmm = E.sem("fmm")
    fev = E.sem("fev")
    fch = E.sem("fch")
    fsc = E.sem("fsc")

    for c in range(NCH):
        c0 = HB + c * TC
        prev_st = fmm.n
        for ft in range(KD):
            w = [(fch.h, fch.n)]
            if ft == 0:
                w.append((woev.h, WOEV))
                if prev_st > 0:
                    w.append((fmm.h, prev_st))
            E("v", "tensor_copy", zb2[:, ft * 512:(ft + 1) * 512],
              xap(ft, c0, TC, shifted=(ft >= KD // 2)), waits=w,
              incs=[(fch.h, 1)])
            fch.bump(1)
            E("sc", "activation", sq2[:, ft * 512:(ft + 1) * 512],
              zb2[:, ft * 512:(ft + 1) * 512], AF.Square,
              waits=[(fch.h, fch.n)], incs=[(fsc.h, 1)])
            fsc.bump(1)
        for ft in range(KD):
            E("t", "matmul", ps[0][0:1, 0:TC], onesf[:, :],
              zb2[:, ft * 512:(ft + 1) * 512], start=(ft == 0), stop=(ft == KD - 1),
              waits=([(fch.h, fch.n), (fini.h, 1)] + bwait(0)) if ft == 0 else [])
            E("t", "matmul", ps[1][0:1, 0:TC], onesf[:, :],
              sq2[:, ft * 512:(ft + 1) * 512], start=(ft == 0), stop=(ft == KD - 1),
              waits=([(fsc.h, fsc.n)] + bwait(1)) if ft == 0 else [],
              incs=([(fmm.h, 1)] if ft == KD - 1 else []))
        fmm.bump(1)
        if KD >= 8:
            mean = sqa.bitcast(F32)[0:1, 0:512]
            ex2 = ubf.bitcast(F32)[0:1, 0:512]
            var = qpsl.bitcast(F32)[0:1, 0:512]
        else:
            mean = frow_s[0:1, 0:512]
            ex2 = frow_s[0:1, 512:1024]
            var = frow_s[0:1, 1024:1536]
        rstd = var
        E("v", "tensor_scalar_mul", mean, ps[0][0:1, 0:TC], 1.0 / D,
          waits=[(fmm.h, fmm.n)], incs=[(fch.h, 1)])
        fch.bump(1)
        E("v", "tensor_scalar_mul", ex2, ps[1][0:1, 0:TC], 1.0 / D,
          waits=[(fch.h, fch.n)], incs=[(fch.h, 1)])
        fch.bump(1)
        E("v", "tensor_tensor", var, mean, mean, ALU.mult,
          waits=[(fch.h, fch.n)], incs=[(fch.h, 1)])
        fch.bump(1)
        E("v", "tensor_sub", var, ex2, var,
          waits=[(fch.h, fch.n)], incs=[(fch.h, 1)])
        fch.bump(1)
        E("sc", "activation", rstd, var, AF.Sqrt, bias=epsc2[0:1, 0:1],
          waits=[(fch.h, fch.n)], incs=[(fsc.h, 1)])
        fsc.bump(1)
        E("v", "reciprocal", rstd, rstd, waits=[(fsc.h, fsc.n)], incs=[(fch.h, 1)])
        fch.bump(1)
        bankc[0] = (fch.h, fch.n - 5)
        bankc[1] = (fch.h, fch.n - 4)
        E("t", "matmul", ps[2][:, 0:TC], onesr[:, :], rstd, start=True, stop=True,
          waits=[(fch.h, fch.n)] + bwait(2), incs=[(fmm.h, 1)])
        fmm.bump(1)
        E("t", "matmul", ps[3][:, 0:TC], onesr[:, :], mean, start=True, stop=True,
          waits=bwait(3), incs=[(fmm.h, 1)])
        fmm.bump(1)
        for ft in range(KD):
            E("v", "tensor_sub", xo2[:, ft * 512:(ft + 1) * 512],
              xap(ft, c0, TC, shifted=(ft >= KD // 2)), ps[3][:, 0:TC],
              waits=([(fmm.h, fmm.n)] if ft == 0 else []) + [(fch.h, fch.n)],
              incs=[(fch.h, 1)])
            fch.bump(1)
            E("v", "tensor_tensor", xo2[:, ft * 512:(ft + 1) * 512],
              xo2[:, ft * 512:(ft + 1) * 512], ps[2][:, 0:TC], ALU.mult,
              waits=[(fch.h, fch.n)], incs=[(fch.h, 1)])
            fch.bump(1)
            E("v", "tensor_scalar", xn2[:, ft * 512:(ft + 1) * 512],
              xo2[:, ft * 512:(ft + 1) * 512], vcol["g"][:, ft:ft + 1],
              vcol["b"][:, ft:ft + 1], ALU.mult, ALU.add,
              waits=[(fch.h, fch.n), (sev.h, SEV)], incs=[(fch.h, 1)])
            fch.bump(1)
        XN = fch.n
        bankc[2] = (fch.h, XN)
        bankc[3] = (fch.h, XN)
        for i in range(NAT):
            wd = 128 if i < NAT - 1 else LK
            par = i % 2
            bA, bG = ps[0 + par * 2], ps[1 + par * 2]
            for k in range(KD):
                lw = [wring.last_cert] if wring.last_cert else []
                if c == 0 and i == 0 and k == 0:
                    lw.append(cast_mark["w1"])
                apA, ldwA = wring.load(
                    w1b[k * 128:(k + 1) * 128, i * 128:i * 128 + wd],
                    transpose=False, waits=lw)
                lwG = [wring2.last_cert] if wring2.last_cert else []
                if c == 0 and i == 0 and k == 0:
                    lwG.append(cast_mark["w1"])
                apG, ldwG = wring2.load(
                    w1b[k * 128:(k + 1) * 128, DI + i * 128:DI + i * 128 + wd],
                    transpose=False, waits=lwG)
                w = [ldwA]
                if k == 0:
                    w += bwait(0 + par * 2) + [(fch.h, XN)]
                E("t", "matmul", bA[0:wd, 0:TC], apA, xn2[:, k * 512:(k + 1) * 512],
                  start=(k == 0), stop=(k == KD - 1), waits=w)
                w2_ = [ldwG]
                if k == 0:
                    w2_ += bwait(1 + par * 2)
                E("t", "matmul", bG[0:wd, 0:TC], apG, xn2[:, k * 512:(k + 1) * 512],
                  start=(k == 0), stop=(k == KD - 1), waits=w2_,
                  incs=([(fmm.h, 1)] if k == KD - 1 else []))
            fmm.bump(1)
            wring.last_cert = (fmm.h, fmm.n)
            wring2.last_cert = (fmm.h, fmm.n)
            E("sc", "activation", gel2[0:wd, 0:TC], bG[0:wd, 0:TC], AF.Gelu,
              bias=b1g[0:wd, i:i + 1],
              waits=[(fmm.h, fmm.n), (b1mm.h, B1LD), (fev.h, fev.n)],
              incs=[(fsc.h, 1)])
            fsc.bump(1)
            bankc[1 + par * 2] = (fsc.h, fsc.n)
            E("v", "scalar_tensor_tensor", msl2[0:wd, i * 512:i * 512 + TC],
              bA[0:wd, 0:TC], b1a[0:wd, i:i + 1], gel2[0:wd, 0:TC],
              ALU.add, ALU.mult,
              waits=[(fsc.h, fsc.n)], incs=[(fev.h, 1)])
            fev.bump(1)
            bankc[0 + par * 2] = (fev.h, fev.n)
        MDONE = fev.n
        for og in range(D // 256):
            par = og % 2
            for k in range(NAT):
                wd = 128 if k < NAT - 1 else LK
                ring = wring if k % 2 == 0 else wring2
                lw = [ring.last_cert] if ring.last_cert else []
                if c == 0 and og == 0 and k <= 1:
                    lw.append(cast_mark["w2"])
                ap, ldw = ring.load(
                    w2b[k * 128:k * 128 + wd, og * 256:(og + 1) * 256],
                    transpose=False, waits=lw)
                for m in range(2):
                    w = [ldw] if m == 0 else []
                    if k == 0 and m == 0:
                        w += bwait(4 + par * 2) + bwait(5 + par * 2) + [(fev.h, MDONE)]
                    E("t", "matmul", ps[4 + par * 2 + m][:, 0:TC],
                      ap[:, m * 128:(m + 1) * 128],
                      msl2[0:wd, k * 512:(k + 1) * 512],
                      start=(k == 0), stop=(k == NAT - 1),
                      waits=w, incs=([(fmm.h, 1)] if (k == NAT - 1 and m == 1) else []))
            fmm.bump(1)
            wring.last_cert = (fmm.h, fmm.n)
            wring2.last_cert = (fmm.h, fmm.n)
            for m in range(2):
                r = og * 2 + m
                E("v", "scalar_tensor_tensor", xo2[:, r * 512:r * 512 + TC],
                  ps[4 + par * 2 + m][:, 0:TC], vcol["b2"][:, r:r + 1],
                  xap(r, c0, TC, shifted=False), ALU.add, ALU.add,
                  waits=([(fmm.h, fmm.n)] if m == 0 else []), incs=[(fev.h, 1)])
                fev.bump(1)
                bankc[4 + par * 2 + m] = (fev.h, fev.n)
                stq.bump(16)
                E("sy", "dma_start", out=out[r * 128:(r + 1) * 128, c * TC:(c + 1) * TC],
                  in_=xo2[:, r * 512:r * 512 + TC],
                  waits=[(fev.h, fev.n)], incs=[(stq.h, 16)])
    E.wait("sy", stq.h, stq.n)
    E.finalize()



def emit_pc(nc, ctx, *, D, T, a, b, out, c=None, bias=None):
    KD = D // 128
    TC = 512
    NCH = T // TC
    E = Emit(nc, ctx)
    sbt = lambda name, shape, dt: ctx.enter_context(
        nc.sbuf_tensor("pc_" + name, shape, dt))
    ps = [ctx.enter_context(nc.psum_tensor(f"pc_ps{i}", [128, 512], F32))
          for i in range(5)]
    xa = sbt("xa", [128, KD * TC], F32)
    xb = sbt("xb", [128, KD * TC], F32)
    if c is not None:
        xc = sbt("xc", [128, KD * TC], F32)
    zb = sbt("zb", [128, KD * TC], BF16)
    sq = sbt("sq", [128, KD * TC], BF16)
    row = sbt("row", [1, 4 * TC], F32)
    onesc = sbt("onesc", [128, 1], BF16)
    onesr = sbt("onesr", [1, 128], F32)
    epsc = sbt("eps", [1, 1], F32)
    ini = E.sem("ini")
    E("g", "memset", onesc[:, :], 1.0)
    E("g", "memset", onesr[:, :], 1.0)
    E("g", "memset", epsc[:, :], LN_EPS, incs=[(ini.h, 1)])
    ini.bump(1)
    ident, ident_ini = make_identity(E)
    if bias is not None:
        braw = sbt("braw", [KD, 128], F32)
        bcl = sbt("bcl", [128, KD], F32)
        bld = E.sem("bld")
        bld.bump(16)
        E("g", "dma_start", out=braw[:, :],
          in_=bias.rearrange("(r p) -> r p", p=128), incs=[(bld.h, 16)])
        bmm = E.sem("bmm")
        E("t", "transpose", ps[4][:, 0:KD], braw[:, :], ident[0:KD, 0:KD],
          waits=[(bld.h, 16), (ident_ini.h, 2)], incs=[(bmm.h, 1)])
        bmm.bump(1)
        E("v", "tensor_copy", bcl[:, :], ps[4][:, 0:KD],
          waits=[(bmm.h, 1)], incs=[(bmm.h, 1)])
        bmm.bump(1)
        BCL = bmm.n
    scr = dict(zb=zb, sq=sq, ones_col=onesc, ones_row=onesr, row=row,
               eps=epsc[0:1, 0:1], ini=(ini, ini.n))
    ald = E.sem("ald")
    vadd = E.sem("vadd")
    stq = E.sem("stq")
    lastfin = None
    for cc in range(NCH):
        c0 = cc * TC
        stq_prior = stq.n
        for k in range(KD):
            ald.bump(16)
            E("sy", "dma_start", out=xa[:, k * TC:(k + 1) * TC],
              in_=a[k * 128:(k + 1) * 128, c0:c0 + TC],
              waits=([(stq.h, stq_prior)] if (k == 0 and cc > 0) else []),
              incs=[(ald.h, 16)])
            ald.bump(16)
            E("sy", "dma_start", out=xb[:, k * TC:(k + 1) * TC],
              in_=b[k * 128:(k + 1) * 128, c0:c0 + TC], incs=[(ald.h, 16)])
            if c is not None:
                ald.bump(16)
                E("sy", "dma_start", out=xc[:, k * TC:(k + 1) * TC],
                  in_=c[k * 128:(k + 1) * 128, c0:c0 + TC], incs=[(ald.h, 16)])
        ALD = ald.n
        for k in range(KD):
            E("v", "tensor_add", xa[:, k * TC:(k + 1) * TC],
              xa[:, k * TC:(k + 1) * TC], xb[:, k * TC:(k + 1) * TC],
              waits=([(ald.h, ALD)] if k == 0 else []), incs=[(vadd.h, 1)])
            vadd.bump(1)
            if c is not None:
                E("v", "tensor_add", xa[:, k * TC:(k + 1) * TC],
                  xa[:, k * TC:(k + 1) * TC], xc[:, k * TC:(k + 1) * TC],
                  waits=[(vadd.h, vadd.n)], incs=[(vadd.h, 1)])
                vadd.bump(1)
            if bias is not None:
                E("v", "tensor_scalar_add", xa[:, k * TC:(k + 1) * TC],
                  xa[:, k * TC:(k + 1) * TC], bcl[:, k:k + 1],
                  waits=[(vadd.h, vadd.n), (bmm.h, BCL)], incs=[(vadd.h, 1)])
                vadd.bump(1)
        guard = [(vadd.h, vadd.n)]
        if lastfin is not None:
            guard.append(lastfin)
        scr2 = dict(scr)
        scr2["guard"] = guard
        fin, fv = ln_fm(E, "ln", lambda s: xa[:, s * TC:(s + 1) * TC], KD, TC,
                        ps, [0, 1, 2, 3], scr2, D)
        lastfin = (fin.h, fv)
        for k in range(KD):
            stq.bump(16)
            E("sy", "dma_start", out=out[k * 128:(k + 1) * 128, c0:c0 + TC],
              in_=xa[:, k * TC:(k + 1) * TC],
              waits=[(fin.h, fv)], incs=[(stq.h, 16)])
    E.wait("sy", stq.h, stq.n)
    E.finalize()


def emit_pd2(nc, ctx, *, D, DQ, S, x2, out, wq, wk, wv, wo,
             bq=None, bk=None, bv=None, wc=None, sw=None, gate=None):
    l0 = gate is not None
    HD = 64
    H = DQ // HD
    KD = D // 128
    KQ = DQ // 128
    NB = S // 128
    NQG = S // 512
    TGW = 256
    NTG = S // TGW

    E = Emit(nc, ctx)
    sbt = lambda name, shape, dt: ctx.enter_context(nc.sbuf_tensor("p2_" + name, shape, dt))
    ps = [ctx.enter_context(nc.psum_tensor(f"p2_ps{i}", [128, 512], F32)) for i in range(8)]
    dram = lambda name, shape: nc.dram_tensor("p2_" + name, shape, BF16).ap()

    x2b = dram("x2b", [D, S])
    wqb, wkb = dram("wqb", [DQ, D]), dram("wkb", [DQ, D])
    wvb = dram("wvb", [DQ, D])
    wob = dram("wob", [D, DQ])

    cast_mark = {}

    def cast(name, src, dst, rpc=512):
        cs = E.sem("cast_" + name)
        for i in range(cdiv(src.shape[0], rpc)):
            r0, r1 = i * rpc, min((i + 1) * rpc, src.shape[0])
            cs.bump(16)
            E("g", "dma_start", out=dst[r0:r1], in_=src[r0:r1], incs=[(cs.h, 16)])
        cast_mark[name] = (cs.h, cs.n)

    cast("x2", x2, x2b, rpc=1024)
    cast("wq", wq, wqb)
    cast("wk", wk, wkb)
    cast("wv", wv, wvb)
    cast("wo", wo, wob)

    ident, ident_ini = make_identity(E)

    Ksl = sbt("K", [128, KQ * S], BF16)
    Vsl = sbt("V", [128, (S // 128) * (H * 65)], BF16)
    Qq = sbt("Qq", [128, KQ * 512], BF16)
    Qblk = sbt("Qblk", [128, 2 * KQ * 128], BF16)
    Qblk2 = sbt("Qblk2", [128, 2 * KQ * 128], BF16)
    Osl = sbt("O", [128, KQ * 512], BF16)
    z3 = sbt("z3", [128, KD * 512], F32)
    x2g = sbt("x2g", [128, KD * TGW * 2], BF16)
    att = sbt("att", [128, 2 * 512], BF16)
    rbs = sbt("rbs", [128, 2 * 512], F32)
    tri = sbt("tri", [128, 128], F32)
    cb = sbt("cb", [128, H], F32)
    cmx = sbt("cmx", [128, 4 * H], F32)
    srow = sbt("srow", [1, 512 + 16], F32)
    scr = dict(
        ones_col=sbt("ones_col", [128, 1], BF16),
        ones_row=sbt("ones_row", [1, 128], F32),
        epsc=sbt("epsc", [1, 1], F32),
    )
    dum = sbt("dum", [1, 1], F32)
    Qdram = nc.dram_tensor("p2_Qdram", [DQ, S], BF16).ap()
    ini = E.sem("ini")
    E("g", "memset", scr["ones_col"][:, :], 1.0)
    E("g", "memset", scr["epsc"][:, :], LN_EPS)
    E("g", "memset", Vsl[:, :], 1.0, incs=[(ini.h, 1)])
    ini.bump(1)
    E("g", "memset", scr["ones_row"][:, :], 1.0, incs=[(ini.h, 1)])
    ini.bump(1)
    E("g", "memset", tri[:, :], 0.0, incs=[(ini.h, 1)])
    ini.bump(1)
    E("g", "affine_select", tri[:, :], tri[:, :], pattern=[[1, 128]],
      compare_op=ALU.is_ge, fill=-1e30, base=0, channel_multiplier=-1,
      waits=[(ini.h, 3)], incs=[(ini.h, 1)])
    ini.bump(1)
    scr["ini"] = (ini, ini.n)
    scr["eps"] = scr["epsc"][0:1, 0:1]
    INI = ini.n

    bcol = {}
    if l0:
        sldA, smmA, sevA = E.sem("sldA"), E.sem("smmA"), E.sem("sevA")
        brawls = {}
        for nm, bsrc in [("bq", bq), ("bk", bk), ("bv", bv)]:
            rows = KQ
            braw = ctx.enter_context(nc.sbuf_tensor("p2_braw_" + nm, [rows, 128], F32))
            bcol[nm] = ctx.enter_context(nc.sbuf_tensor("p2_bcol_" + nm, [128, rows], F32))
            sldA.bump(16)
            E("g", "dma_start", out=braw[:, :],
              in_=bsrc.rearrange("(r p) -> r p", p=128), incs=[(sldA.h, 16)])
            brawls[nm] = braw
        bn = 0
        for nm in ("bq", "bk", "bv"):
            rows = KQ
            prev = sevA.n
            smmA.bump(1)
            E("t", "transpose", ps[0][0:128, 0:rows], brawls[nm][:, :],
              ident[0:rows, 0:rows],
              waits=[(sldA.h, 16 * 3), (ident_ini.h, 2), (sevA.h, prev)],
              incs=[(smmA.h, 1)])
            sevA.bump(1)
            E("v", "tensor_copy", bcol[nm][:, :], ps[0][:, 0:rows],
              waits=[(smmA.h, smmA.n)], incs=[(sevA.h, 1)])
        BIAS_DONE = sevA.n
        NHP = H // 2
        Asl = ctx.enter_context(nc.sbuf_tensor("p2_A", [128, NHP * 64], BF16))
        Msl = ctx.enter_context(nc.sbuf_tensor("p2_M", [128, NHP * 64], F32))
        Wcl = ctx.enter_context(nc.sbuf_tensor("p2_Wc", [128, NHP * 64], F32))
        gts = ctx.enter_context(nc.sbuf_tensor("p2_gt", [1, H], F32))
        gtc = ctx.enter_context(nc.sbuf_tensor("p2_gtc", [128, NHP], F32))
        Atmp = ctx.enter_context(nc.sbuf_tensor("p2_At", [128, 64], F32))
        aini = E.sem("aini")
        adma = E.sem("adma")
        agm = E.sem("agm")
        E("g", "memset", Msl[:, :], 0.0, incs=[(aini.h, 1)])
        aini.bump(1)
        for e in range(2):
            adma.bump(16)
            E("g", "dma_start",
              out=Wcl[e * 64:(e + 1) * 64, :].rearrange("f (j g) -> f j g", g=64),
              in_=wc.rearrange("(j e) f g -> e f j g", e=2)[e],
              incs=[(adma.h, 16)])
        adma.bump(16)
        E("g", "dma_start", out=gts[:, :], in_=gate[None, :], incs=[(adma.h, 16)])
        E("sc", "activation", gts[:, :], gts[:, :], AF.Sigmoid,
          waits=[(adma.h, adma.n)], incs=[(aini.h, 1)])
        aini.bump(1)
        msw = E.sem("msw")
        MS = sw.shape[1]
        for f in range(64):
            L = min(MS, 64 - f)
            for e in range(2):
                msw.bump(16)
                dst = Msl[e * 64 + f:e * 64 + f + 1, :].rearrange(
                    "p (j g) -> p j g", g=64)[:, :, f:f + L]
                E("sc", "dma_start", out=dst,
                  in_=sw.rearrange("(j e) t -> e j t", e=2)[e][None, :, 0:L],
                  allow_slow_non_contiguous=True,
                  waits=([(aini.h, 1)] if (f == 0 and e == 0) else []),
                  incs=[(msw.h, 16)])
        for j in range(NHP):
            for e in range(2):
                agm.bump(1)
                E("t", "matmul", ps[1][e * 64:e * 64 + 64, j:j + 1],
                  scr["ones_row"][0:1, 0:64], gts[0:1, 2 * j + e:2 * j + e + 1],
                  start=True, stop=True,
                  waits=([(aini.h, aini.n), (ini.h, INI)] if (j == 0 and e == 0) else []),
                  incs=[(agm.h, 1)])
        E("v", "tensor_copy", gtc[:, :], ps[1][:, 0:NHP],
          waits=[(agm.h, agm.n)], incs=[(agm.h, 1)])
        agm.bump(1)
        AGM_DONE = agm.n
        acb = E.sem("acb")
        for j in range(NHP):
            w = [(agm.h, agm.n)]
            if j == 0:
                w += [(msw.h, msw.n), (aini.h, aini.n)]
            else:
                w.append((acb.h, acb.n))
            E("v", "tensor_sub", Atmp[:, :], Wcl[:, j * 64:(j + 1) * 64],
              Msl[:, j * 64:(j + 1) * 64], waits=w, incs=[(acb.h, 1)])
            acb.bump(1)
            E("v", "tensor_scalar", Atmp[:, :], Atmp[:, :], gtc[:, j:j + 1], 0.125,
              ALU.mult, ALU.mult, waits=[(acb.h, acb.n)], incs=[(acb.h, 1)])
            acb.bump(1)
            E("v", "scalar_tensor_tensor", Asl[:, j * 64:(j + 1) * 64],
              Msl[:, j * 64:(j + 1) * 64], 0.125, Atmp[:, :],
              ALU.mult, ALU.add, waits=[(acb.h, acb.n)], incs=[(acb.h, 1)])
            acb.bump(1)
        A_DONE = acb.n



    wring = WStream(E, "p2_wring", 256, 16)

    OGG = 256
    NOG = DQ // OGG
    wqkvsb = sbt("wqkv_sb", [128, 3 * NOG * KD * OGG], BF16)
    wld = [E.sem(f"wld{i}") for i in range(3)]

    def wsb_ap(wi, ogg, k):
        return wqkvsb[:, ((wi * NOG + ogg) * KD + k) * OGG:
                      ((wi * NOG + ogg) * KD + k) * OGG + OGG]

    for wi, (wb, nm) in enumerate(((wqb, "wq"), (wkb, "wk"), (wvb, "wv"))):
        for ogg in range(NOG):
            for k in range(KD):
                wld[wi].bump(16)
                E("sy", "dma_start", out=wsb_ap(wi, ogg, k),
                  in_=wb[ogg * OGG:(ogg + 1) * OGG, k * 128:(k + 1) * 128],
                  transpose=True,
                  waits=([cast_mark[nm]] if (ogg == 0 and k == 0) else []),
                  incs=[(wld[wi].h, 16)])
    WLD = [s.n for s in wld]

    xld = [E.sem("xld0"), E.sem("xld1")]
    xrel = E.sem("xrel")
    xg_lv = {}

    def sched_x2_group(tg):
        s = xld[tg % 2]
        for k in range(KD):
            s.bump(16)
            E("g", "dma_start",
              out=x2g[:, ((tg % 2) * KD + k) * TGW:((tg % 2) * KD + k) * TGW + TGW],
              in_=x2b[k * 128:(k + 1) * 128, tg * TGW:(tg + 1) * TGW],
              waits=([cast_mark["x2"]] if tg == 0 and k == 0 else
                     ([(xrel.h, tg - 1)] if (tg >= 2 and k == 0) else [])),
              incs=[(s.h, 16)])
        xg_lv[tg] = (s.h, s.n)

    def xg_ap(tg, k):
        return x2g[:, ((tg % 2) * KD + k) * TGW:((tg % 2) * KD + k) * TGW + TGW]

    qev, kev, vev = E.sem("qev"), E.sem("kev"), E.sem("vev")
    qst = [E.sem("qst0"), E.sem("qst1")]
    pmm = E.sem("pmm")

    for tg in range(NTG):
        sched_x2_group(tg)
        for ogg in range(DQ // OGG):
            for k in range(KD):
                ap = wsb_ap(1, ogg, k)
                ldw = (wld[1].h, WLD[1])
                for m in range(2):
                    w = [ldw] if m == 0 else []
                    if k == 0 and m == 0:
                        if ogg < 2:
                            w.append((qev.h, qev.n))
                        else:
                            w.append((kev.h, kev.n - 2))
                        w.append(xg_lv[tg])
                        if l0 and tg == 0 and ogg == 0:
                            w += [(sevA.h, BIAS_DONE), (agm.h, AGM_DONE)]
                    incs = []
                    if k == KD - 1 and m == 1:
                        pmm.bump(1)
                        incs = [(pmm.h, 1)]
                    E("t", "matmul", ps[(ogg % 2) * 2 + m][:, 0:TGW],
                      ap[:, m * 128:(m + 1) * 128], xg_ap(tg, k),
                      start=(k == 0), stop=(k == KD - 1), waits=w, incs=incs)
            for m in range(2):
                r = ogg * 2 + m
                kw = [(pmm.h, pmm.n)] if m == 0 else []
                if l0:
                    if m == 0:
                        kw.append((sevA.h, BIAS_DONE))
                    E("v", "tensor_scalar_add",
                      Ksl[:, r * S + tg * TGW:r * S + tg * TGW + TGW],
                      ps[(ogg % 2) * 2 + m][:, 0:TGW], bcol["bk"][:, r:r + 1],
                      waits=kw, incs=[(kev.h, 1)])
                else:
                    E("v", "tensor_copy",
                      Ksl[:, r * S + tg * TGW:r * S + tg * TGW + TGW],
                      ps[(ogg % 2) * 2 + m][:, 0:TGW],
                      waits=kw, incs=[(kev.h, 1)])
                kev.bump(1)
        NV = cdiv(DQ, 256)
        VW = min(256, DQ)
        HPT = VW // 64
        for tt in range(TGW // 128):
            gt = tg * (TGW // 128) + tt
            for nv in range(NV):
                vbank = ps[4 + (vev.n % 2)]
                for k in range(KD):
                    ap = wsb_ap(2, nv, k)
                    w = [(wld[2].h, WLD[2])]
                    if k == 0:
                        w.append((vev.h, max(0, vev.n - 1)))
                        if l0 and tg == 0 and tt == 0 and nv == 0:
                            w.append((agm.h, AGM_DONE))
                    incs = []
                    if k == KD - 1:
                        pmm.bump(1)
                        incs = [(pmm.h, 1)]
                    E("t", "matmul", vbank[:, 0:VW],
                      xg_ap(tg, k)[:, tt * 128:(tt + 1) * 128], ap,
                      start=(k == 0), stop=(k == KD - 1), waits=w, incs=incs)
                hbase = nv * HPT
                dst = Vsl[:, gt * (H * 65) + hbase * 65:gt * (H * 65) + hbase * 65 + HPT * 65]
                E("v", "tensor_copy",
                  dst.rearrange("p (h c) -> p h c", c=65)[:, :, 0:64],
                  vbank[:, 0:VW].rearrange("p (h c) -> p h c", c=64),
                  waits=[(pmm.h, pmm.n), (ini.h, INI)], incs=[(vev.h, 1)])
                vev.bump(1)
        for tt in range(TGW // 128):
            gb = tg * (TGW // 128) + tt
            ob = gb
            for ogg in range(DQ // OGG):
                for k in range(KD):
                    ap = wsb_ap(0, ogg, k)
                    ldw = (wld[0].h, WLD[0])
                    for m in range(2):
                        w = [ldw] if m == 0 else []
                        if k == 0 and m == 0:
                            if ogg < 2:
                                w.append((kev.h, kev.n))
                            delta = 2 if (DQ // OGG) >= 2 else 0
                            w.append((qev.h, max(0, qev.n - delta)))
                        incs = []
                        if k == KD - 1 and m == 1:
                            pmm.bump(1)
                            incs = [(pmm.h, 1)]
                        E("t", "matmul", ps[(ogg % 2) * 2 + m][:, 0:128],
                          ap[:, m * 128:(m + 1) * 128],
                          xg_ap(tg, k)[:, tt * 128:(tt + 1) * 128],
                          start=(k == 0), stop=(k == KD - 1), waits=w, incs=incs)
                pb = ob % 2
                for m in range(2):
                    r = ogg * 2 + m
                    w = []
                    if m == 0:
                        w.append((pmm.h, pmm.n))
                        if ogg == 0:
                            w.append((qst[pb].h, qst[pb].n))
                    if l0:
                        if ogg == 0 and m == 0:
                            w.append((sevA.h, BIAS_DONE))
                        E("v", "tensor_scalar_add",
                          Qblk[:, (pb * KQ + r) * 128:(pb * KQ + r) * 128 + 128],
                          ps[(ogg % 2) * 2 + m][:, 0:128],
                          bcol["bq"][:, r:r + 1], waits=w, incs=[(qev.h, 1)])
                    else:
                        E("sc", "activation",
                          Qblk[:, (pb * KQ + r) * 128:(pb * KQ + r) * 128 + 128],
                          ps[(ogg % 2) * 2 + m][:, 0:128], AF.Copy, scale=0.125,
                          waits=w, incs=[(qev.h, 1)])
                    qev.bump(1)
            pb = ob % 2
            if l0:
                qpm = E.sem("qpm")
                for hd in range(H):
                    ft, fo = hd // 2, (hd % 2) * 64
                    j = hd // 2
                    if hd >= 2:
                        whd = qev.n - 1
                    else:
                        whd = max(0, qev.n - (2 * KQ + 1) + hd)
                    w = [(qev.h, qev.n if hd == 0 else whd)]
                    if ob < 2 and hd == 0:
                        w.append((acb.h, A_DONE))
                    qpm.bump(1)
                    E("t", "matmul", ps[6 + (hd % 2)][0:64, 0:128],
                      Asl[:, j * 64:(j + 1) * 64][fo:fo + 64, :],
                      Qblk[:, (pb * KQ + ft) * 128:(pb * KQ + ft) * 128 + 128][fo:fo + 64, :],
                      start=True, stop=True, waits=w, incs=[(qpm.h, 1)])
                    cw = [(qpm.h, qpm.n)]
                    if hd == 0:
                        cw.append((qst[pb].h, qst[pb].n))
                    E("v", "tensor_copy",
                      Qblk2[:, (pb * KQ + ft) * 128:(pb * KQ + ft) * 128 + 128][fo:fo + 64, :],
                      ps[6 + (hd % 2)][0:64, 0:128],
                      waits=cw, incs=[(qev.h, 1)])
                    qev.bump(1)
                qsrc = Qblk2
            else:
                qsrc = Qblk
            qst[pb].bump(16)
            E("g", "dma_start",
              out=Qdram.rearrange("(r p) t -> p r t", p=128)[:, :, ob * 128:(ob + 1) * 128],
              in_=qsrc[:, pb * KQ * 128:(pb + 1) * KQ * 128].rearrange("p (r c) -> p r c", c=128),
              waits=[(qev.h, qev.n)], incs=[(qst[pb].h, 16)])
        xrel.bump(1)
        E("v", "tensor_copy", dum[0:1, 0:1], scr["epsc"][0:1, 0:1],
          waits=[(pmm.h, pmm.n)], incs=[(xrel.h, 1)])
    KEV, VEV, QEV = kev.n, vev.n, qev.n

    p0r = E.sem("p0r")
    p0t = E.sem("p0t")
    p0c = E.sem("p0c")
    smm = E.sem("smm")
    sxp = E.sem("sxp")
    msk = E.sem("msk")
    avm = E.sem("avm")
    aev = E.sem("aev")
    rr = E.sem("rr")
    qql = E.sem("qql")
    womm = E.sem("womm")
    woev = E.sem("woev")
    x2ld = E.sem("x2ld")
    stq = E.sem("stq")
    kpw = E.sem("kpw")

    def qg_qap(hd, qg):
        ft, fo = hd // 2, (hd % 2) * 64
        return Qq[:, ft * 512:ft * 512 + 512][fo:fo + 64, :]

    for qg in range(NQG):
        w = [(qst[0].h, qst[0].n), (qst[1].h, qst[1].n)]
        if qg > 0:
            w += [(smm.h, smm.n), (p0t.h, p0t.n)]
        for k in range(KQ):
            qql.bump(16)
            E("g", "dma_start", out=Qq[:, k * 512:(k + 1) * 512],
              in_=Qdram[k * 128:(k + 1) * 128, qg * 512:(qg + 1) * 512],
              waits=(w if k == 0 else []), incs=[(qql.h, 16)])
        qqlv = qql.n
        for hd in range(H):
            ft, fo = hd // 2, (hd % 2) * 64
            for tt in range(4):
                gb = 4 * qg + tt
                kap = Ksl[:, ft * S + gb * 128:ft * S + gb * 128 + 128][fo:fo + 64, :]
                qap = Qq[:, ft * 512 + tt * 128:ft * 512 + tt * 128 + 128][fo:fo + 64, :]
                w = [(p0r.h, p0r.n)] if tt == 0 else []
                if hd == 0 and tt == 0:
                    w.append((qql.h, qqlv))
                    if qg == 0:
                        w += [(kev.h, KEV), (vev.h, VEV)]
                    else:
                        w.append((woev.h, woev.n))
                E("t", "matmul", ps[6][:, tt * 128:(tt + 1) * 128], qap, kap,
                  start=True, stop=True, waits=w,
                  incs=([(p0t.h, 1)] if tt == 3 else []))
            p0t.bump(1)
            E("v", "tensor_reduce", cmx[:, hd * 4:hd * 4 + 4],
              ps[6][:, 0:512].rearrange("p (t c) -> p t c", c=128),
              AX.X, ALU.max, waits=[(p0t.h, p0t.n)], incs=[(p0r.h, 1)])
            p0r.bump(1)
            E("v", "reduce_max", cmx[:, hd * 4:hd * 4 + 1],
              cmx[:, hd * 4:hd * 4 + 4], AX.X,
              waits=[(p0r.h, p0r.n)], incs=[(p0r.h, 1)])
            p0r.bump(1)
            E("t", "transpose", ps[7][0:1, 0:128], cmx[:, hd * 4:hd * 4 + 1],
              ident[:, :], waits=[(p0r.h, p0r.n), (ident_ini.h, 2)],
              incs=[(p0t.h, 1)])
            p0t.bump(1)
            E("v", "reduce_max", srow[0:1, 512 + hd:513 + hd],
              ps[7][0:1, 0:128], AX.X, negate=True,
              waits=[(p0t.h, p0t.n)], incs=[(p0r.h, 1)])
            p0r.bump(1)
            E("t", "matmul", ps[7][:, 128:129], scr["ones_row"][:, :],
              srow[0:1, 512 + hd:513 + hd], start=True, stop=True,
              waits=[(p0r.h, p0r.n), (ini.h, INI)], incs=[(p0t.h, 1)])
            p0t.bump(1)
            E("v", "tensor_copy", cb[:, hd:hd + 1], ps[7][:, 128:129],
              waits=[(p0t.h, p0t.n), (sxp.h, sxp.n)], incs=[(p0c.h, 1)])
            p0c.bump(1)
        womm_prev = womm.n
        for hd in range(H):
            ft, fo = hd // 2, (hd % 2) * 64
            nchunk = 4 * qg + 4
            avbank = ps[2 + (hd % 2)]
            for kb in range(nchunk):
                kap = Ksl[:, ft * S + kb * 128:ft * S + kb * 128 + 128][fo:fo + 64, :]
                sbank = ps[kb % 2]
                w = [(sxp.h, max(0, sxp.n - 1))]
                if kb == 0:
                    w.append((p0c.h, p0c.n))
                    w.append((aev.h, max(0, aev.n - 1)))
                E("t", "matmul", sbank[:, 0:512], kap, qg_qap(hd, qg),
                  start=True, stop=True, waits=w, incs=[(smm.h, 1)])
                smm.bump(1)
                r = kb - 4 * qg
                if 0 <= r <= 3:
                    t0 = r
                    E("v", "tensor_tensor",
                      sbank[:, t0 * 128:(t0 + 1) * 128],
                      sbank[:, t0 * 128:(t0 + 1) * 128], tri[:, :], ALU.add,
                      waits=[(smm.h, smm.n), (ini.h, INI)], incs=[(msk.h, 1)])
                    msk.bump(1)
                    ew = [(msk.h, msk.n)]
                else:
                    ew = [(smm.h, smm.n)]
                aslot = att[:, (kb % 2) * 512:(kb % 2) * 512 + 512]
                E("sc", "activation", aslot, sbank[:, 0:512], AF.Exp,
                  bias=cb[:, hd:hd + 1], waits=ew, incs=[(sxp.h, 1)])
                sxp.bump(1)
                nzero = min(4, r) if r >= 1 else 0
                for t0 in range(nzero):
                    E("v", "memset", aslot[:, t0 * 128:(t0 + 1) * 128], 0.0,
                      waits=[(sxp.h, sxp.n)], incs=[(msk.h, 1)])
                    msk.bump(1)
                avw = [(sxp.h, sxp.n)]
                if nzero:
                    avw.append((msk.h, msk.n))
                vap = Vsl[:, kb * (H * 65) + hd * 65:kb * (H * 65) + (hd + 1) * 65]
                E("t", "matmul", avbank[0:65, 0:512], vap, aslot,
                  start=(kb == 0), stop=(kb == nchunk - 1),
                  waits=avw, incs=[(avm.h, 1)])
                avm.bump(1)
            sslot = srow[0:1, 0:512]
            E("v", "tensor_copy", sslot, avbank[64:65, 0:512],
              waits=[(avm.h, avm.n), (p0t.h, p0t.n)], incs=[(rr.h, 1)])
            rr.bump(1)
            E("sc", "activation", sslot, sslot, AF.Ln,
              waits=[(rr.h, rr.n)], incs=[(rr.h, 1)])
            rr.bump(1)
            E("sc", "activation", sslot, sslot, AF.Exp, scale=-1.0,
              waits=[(rr.h, rr.n)], incs=[(rr.h, 1)])
            rr.bump(1)
            wb = [(rr.h, rr.n), (aev.h, max(0, aev.n - 1))]
            E("t", "matmul", ps[4 + (hd % 2)][0:64, 0:512],
              scr["ones_row"][0:1, 0:64], sslot, start=True, stop=True,
              waits=wb, incs=[(p0t.h, 1)])
            p0t.bump(1)
            E("v", "tensor_copy", rbs[0:64, (hd % 2) * 512:(hd % 2) * 512 + 512],
              ps[4 + (hd % 2)][0:64, 0:512],
              waits=[(p0t.h, p0t.n)], incs=[(rr.h, 1)])
            rr.bump(1)
            ow = [(rr.h, rr.n)]
            if qg > 0 and hd == 0:
                ow.append((womm.h, womm_prev))
            if l0:
                E("v", "tensor_tensor",
                  Osl[:, ft * 512:ft * 512 + 512][fo:fo + 64, :],
                  avbank[0:64, 0:512], rbs[0:64, (hd % 2) * 512:(hd % 2) * 512 + 512],
                  ALU.mult, waits=ow, incs=[(rr.h, 1)])
                rr.bump(1)
                E("v", "tensor_scalar_add",
                  Osl[:, ft * 512:ft * 512 + 512][fo:fo + 64, :],
                  Osl[:, ft * 512:ft * 512 + 512][fo:fo + 64, :],
                  bcol["bv"][fo:fo + 64, ft:ft + 1],
                  waits=[(rr.h, rr.n)], incs=[(aev.h, 1)])
            else:
                E("v", "tensor_tensor",
                  Osl[:, ft * 512:ft * 512 + 512][fo:fo + 64, :],
                  avbank[0:64, 0:512], rbs[0:64, (hd % 2) * 512:(hd % 2) * 512 + 512],
                  ALU.mult, waits=ow, incs=[(aev.h, 1)])
            aev.bump(1)
        def evacwo(ogg, m2, banks, mm_val, qg=qg):
            for m in range(m2):
                idx = ogg * m2 + m
                w = [(stq.h, stq.n)]
                if m == 0:
                    w.append((womm.h, mm_val))
                E("v", "tensor_copy",
                  z3[:, idx * 512:(idx + 1) * 512], banks[m][:, 0:512],
                  waits=w, incs=[(woev.h, 1)])
                woev.bump(1)

        mm_stage(E, ws=wring, nw=1,
                 wsrc_fn=lambda w, ogg, k: wob[ogg * OGG:(ogg + 1) * OGG,
                                               k * 128:(k + 1) * 128],
                 n_ogg=D // OGG, OGG=OGG, kc=KQ,
                 rhs_fn=lambda k: Osl[:, k * 512:(k + 1) * 512],
                 width=512, ps_pool=ps, bank_base=0, mm_sem=womm, ev_sem=woev,
                 evac_fn=evacwo, ev_per_ogg=2, kprog=kpw,
                 pre_waits=[(aev.h, aev.n)],
                 first_load_waits=[[cast_mark["wo"]]])
        WOE = woev.n
        for s in range(KD):
            stq.bump(16)
            E("sy", "dma_start",
              out=out[s * 128:(s + 1) * 128, qg * 512:(qg + 1) * 512],
              in_=z3[:, s * 512:(s + 1) * 512],
              waits=[(woev.h, WOE - KD + s + 1)], incs=[(stq.h, 16)])

    E.wait("sy", stq.h, stq.n)
    E.finalize()
    return E




_D, _S, _B, _H = 1024, 4096, 4, 16
_DIFF = 2730
_HID = 4096
_HB = 384


def _build_p2_prog(l0):
    DQ = _D // 2
    H2 = DQ // 64
    nc = bass.Bass("TRN2", target_bir_lowering=False)
    ap = lambda n, s: nc.dram_tensor(n, s, F32, kind="ExternalInput").ap()
    x2 = ap("x2", [_D, _S])
    wq, wk, wv = [ap(n, [DQ, _D]) for n in ("wq", "wk", "wv")]
    wo = ap("wo", [_D, DQ])
    kw = {}
    if l0:
        kw = dict(bq=ap("bq", [DQ]), bk=ap("bk", [DQ]), bv=ap("bv", [DQ]),
                  wc=ap("wc", [H2, 64, 64]), sw=ap("sw", [H2, 32]),
                  gate=ap("gate", [H2]))
    out = nc.dram_tensor("out", [_D, _S], F32, kind="ExternalOutput").ap()
    ctx = ExitStack()
    with ctx:
        emit_pd2(nc, ctx, D=_D, DQ=DQ, S=_S, x2=x2, out=out,
                 wq=wq, wk=wk, wv=wv, wo=wo, **kw)
    return nc


def _build_pb_prog():
    T = _S // 2
    nc = bass.Bass("TRN2", target_bir_lowering=False)
    ap = lambda n, s: nc.dram_tensor(n, s, F32, kind="ExternalInput").ap()
    TT = _HB + T
    x_in = ap("x_in", [_D, TT])
    wqkv = ap("wqkv", [_D, 3 * _D])
    wo = ap("wo", [_D, _D])
    w1 = ap("w1", [_D, 2 * _DIFF])
    b1 = ap("b1", [2 * _DIFF])
    w2 = ap("w2", [_DIFF, _D])
    b2 = ap("b2", [_D])
    ln_g, ln_b = ap("ln_g", [_D]), ap("ln_b", [_D])
    qs, ks = ap("qs", [64]), ap("ks", [64])
    posb = ap("posb", [1])
    lmask = ap("lmask", [TT])
    out = nc.dram_tensor("out", [_D, T], F32, kind="ExternalOutput").ap()
    ctx = ExitStack()
    with ctx:
        emit_pb(nc, ctx, D=_D, T=T, DI=_DIFF, x_in=x_in, out=out, wqkv=wqkv,
                wo=wo, w1=w1, b1=b1, w2=w2, b2=b2, ln_g=ln_g, ln_b=ln_b,
                qs=qs, ks=ks, posb=posb, lmask=lmask, SHALF=_S // 2)
    return nc


def _build_pc_prog(three, bias):
    T = _B * _S // 8
    nc = bass.Bass("TRN2", target_bir_lowering=False)
    a = nc.dram_tensor("a", [_D, T], F32, kind="ExternalInput").ap()
    b = nc.dram_tensor("b", [_D, T], F32, kind="ExternalInput").ap()
    kw = {}
    if three:
        kw["c"] = nc.dram_tensor("c", [_D, T], F32, kind="ExternalInput").ap()
    if bias:
        kw["bias"] = nc.dram_tensor("bias", [_D], F32, kind="ExternalInput").ap()
    out = nc.dram_tensor("out", [_D, T], F32, kind="ExternalOutput").ap()
    ctx = ExitStack()
    with ctx:
        emit_pc(nc, ctx, D=_D, T=T, a=a, b=b, out=out, **kw)
    return nc


def _build_pe_prog():
    T = _B * _S // 8
    nc = bass.Bass("TRN2", target_bir_lowering=False)
    ap = lambda n, s: nc.dram_tensor(n, s, F32, kind="ExternalInput").ap()
    x3 = ap("x3", [_D, T])
    w1, w1s = ap("w1", [_HID, _D]), ap("w1s", [_HID])
    w2, w2s = ap("w2", [_HID, _D]), ap("w2s", [_HID])
    h1, h1s = ap("h1", [_HID, _HID]), ap("h1s", [_HID])
    h2, h2s = ap("h2", [_HID, _HID]), ap("h2s", [_HID])
    w3, w3s = ap("w3", [_D, _HID]), ap("w3s", [_D])
    out = nc.dram_tensor("out", [_D, T], F32, kind="ExternalOutput").ap()
    ctx = ExitStack()
    with ctx:
        emit_pe(nc, ctx, D=_D, HID=_HID, T=T, TC=512, x3=x3, out=out,
                w1=w1, w1s=w1s, w2=w2, w2s=w2s, h1=h1, h1s=h1s,
                h2=h2, h2s=h2s, w3=w3, w3s=w3s)
    return nc


_PROGS = {}


def kernel(**inputs):
    from concourse.bass_utils import run_bass_kernel_spmd

    f32 = np.float32
    c = lambda a: np.ascontiguousarray(np.asarray(a, dtype=f32))
    if not _PROGS:
        _PROGS["pa"] = _build_p2_prog(True)
        _PROGS["pb"] = _build_pb_prog()
        _PROGS["pc2"] = _build_pc_prog(False, False)
        _PROGS["pc3b"] = _build_pc_prog(True, True)
        _PROGS["pc3"] = _build_pc_prog(True, False)
        _PROGS["pd"] = _build_p2_prog(False)
        _PROGS["pe"] = _build_pe_prog()

    x = c(inputs["L0_attn_x"])
    xfm = np.ascontiguousarray(x.transpose(0, 2, 1))

    DQ = _D // 2
    Tc = _B * _S // 8

    def attn_phase(key, xin_fm, wl_fn):
        feeds = []
        for b in range(_B):
            for hh in range(2):
                feeds.append(dict(x2=np.ascontiguousarray(xin_fm[b]),
                                  **wl_fn(hh)))
        res = run_bass_kernel_spmd(_PROGS[key], feeds, core_ids=list(range(8)))
        ya = np.concatenate([res.results[2 * b]["out"] for b in range(_B)], axis=1)
        yb = np.concatenate([res.results[2 * b + 1]["out"] for b in range(_B)],
                            axis=1)
        return ya, yb

    def ln_combine(key, acat, bcat, ccat=None, bias=None):
        feeds = []
        for j in range(8):
            fd = dict(a=np.ascontiguousarray(acat[:, j * Tc:(j + 1) * Tc]),
                      b=np.ascontiguousarray(bcat[:, j * Tc:(j + 1) * Tc]))
            if ccat is not None:
                fd["c"] = np.ascontiguousarray(ccat[:, j * Tc:(j + 1) * Tc])
            if bias is not None:
                fd["bias"] = bias
            feeds.append(fd)
        res = run_bass_kernel_spmd(_PROGS[key], feeds, core_ids=list(range(8)))
        cat = np.concatenate([res.results[j]["out"] for j in range(8)], axis=1)
        return np.stack([cat[:, b * _S:(b + 1) * _S] for b in range(_B)])

    def wl0(hh):
        r = slice(hh * DQ, (hh + 1) * DQ)
        h = slice(hh * _H // 2, (hh + 1) * _H // 2)
        return dict(wq=c(inputs["L0_attn_Wq"][r]), wk=c(inputs["L0_attn_Wk"][r]),
                    wv=c(inputs["L0_attn_Wv"][r]),
                    wo=c(inputs["L0_attn_Wo"][:, r]),
                    bq=c(inputs["L0_attn_bq"][r]), bk=c(inputs["L0_attn_bk"][r]),
                    bv=c(inputs["L0_attn_bv"][r]),
                    wc=c(inputs["L0_attn_W_cross"][h]),
                    sw=c(inputs["L0_attn_shift_weights"][h]),
                    gate=c(inputs["L0_attn_gate"][h]))
    ya, yb = attn_phase("pa", xfm, wl0)
    xcat = np.concatenate([xfm[b] for b in range(_B)], axis=1)
    x1 = ln_combine("pc3b", xcat, ya, yb, bias=c(inputs["L0_attn_bo"]))

    T = _S // 2
    TT = _HB + T
    xcur = x1
    for i in range(6):
        wl = dict(wqkv=c(inputs["L0_ffn_Wqkv"][i]), wo=c(inputs["L0_ffn_Wo"][i]),
                  w1=c(inputs["L0_ffn_W1"][i]), b1=c(inputs["L0_ffn_b1"][i]),
                  w2=c(inputs["L0_ffn_W2"][i]), b2=c(inputs["L0_ffn_b2"][i]),
                  ln_g=c(inputs["L0_ffn_ln_g"][i]),
                  ln_b=c(inputs["L0_ffn_ln_b"][i]),
                  qs=c(inputs["L0_ffn_q_scale"][i]),
                  ks=c(inputs["L0_ffn_k_scale"][i]))
        feeds = []
        for b in range(_B):
            for h2 in range(2):
                t0 = h2 * T
                lo = t0 - _HB
                xin = np.zeros((_D, TT), f32)
                lm = np.full(TT, -1e20, f32)
                s0 = max(0, lo)
                xin[:, s0 - lo:] = xcur[b][:, s0:t0 + T]
                lm[s0 - lo:] = 0.0
                feeds.append(dict(x_in=xin, posb=np.array([lo], f32),
                                  lmask=lm, **wl))
        res = run_bass_kernel_spmd(_PROGS["pb"], feeds, core_ids=list(range(8)))
        nxt = np.empty((_B, _D, _S), f32)
        ci = 0
        for b in range(_B):
            for h2 in range(2):
                nxt[b][:, h2 * T:(h2 + 1) * T] = res.results[ci]["out"]
                ci += 1
        xcur = nxt

    x1cat = np.concatenate([x1[b] for b in range(_B)], axis=1)
    fcat = np.concatenate([xcur[b] for b in range(_B)], axis=1)
    x2fm = ln_combine("pc2", x1cat, fcat)

    def wl1(hh):
        r = slice(hh * DQ, (hh + 1) * DQ)
        return dict(wq=c(inputs["L1_attn_wq"][r]), wk=c(inputs["L1_attn_wk"][r]),
                    wv=c(inputs["L1_attn_wv"][r]),
                    wo=c(inputs["L1_attn_wo"][:, r]))
    ya, yb = attn_phase("pd", x2fm, wl1)
    x2cat = np.concatenate([x2fm[b] for b in range(_B)], axis=1)
    x3 = ln_combine("pc3", x2cat, ya, yb)

    x3cat = np.concatenate([x3[b] for b in range(_B)], axis=1)
    wle = dict(w1=c(inputs["L1_ffn_w1_w"]), w1s=c(inputs["L1_ffn_w1_s"]),
               w2=c(inputs["L1_ffn_w2_w"]), w2s=c(inputs["L1_ffn_w2_s"]),
               h1=c(inputs["L1_ffn_h1_w"]), h1s=c(inputs["L1_ffn_h1_s"]),
               h2=c(inputs["L1_ffn_h2_w"]), h2s=c(inputs["L1_ffn_h2_s"]),
               w3=c(inputs["L1_ffn_w3_w"]), w3s=c(inputs["L1_ffn_w3_s"]))
    feeds = [dict(x3=np.ascontiguousarray(x3cat[:, j * Tc:(j + 1) * Tc]), **wle)
             for j in range(8)]
    res = run_bass_kernel_spmd(_PROGS["pe"], feeds, core_ids=list(range(8)))
    ocat = np.concatenate([res.results[j]["out"] for j in range(8)], axis=1)
    return np.ascontiguousarray(
        np.stack([ocat[:, b * _S:(b + 1) * _S].T for b in range(_B)])).astype(f32)
